# version 03
# baseline (speedup 1.0000x reference)
import contextlib

import numpy as np

import concourse.bass as bass
import concourse.mybir as mybir
from concourse.bass_utils import run_bass_kernel_spmd

F32 = mybir.dt.float32
BF16 = mybir.dt.bfloat16
I32 = mybir.dt.int32
AF = mybir.ActivationFunctionType
OP = mybir.AluOpType
AX = mybir.AxisListType.X

NCORES = 8
B, S, D = 4, 4096, 1024
T = 2048
NT = T // 128
ND = D // 128
H, HD = 16, 64

ENGS = ("sync", "tensor", "vector", "scalar", "gpsimd")

SUBLAYERS = ("l0a", "l0f", "l0b", "l0f", "l1a", "l1f", "mb", "l1f")
DEBUG_TAPS = ()


def ts(i, sz=128):
    return slice(i * sz, (i + 1) * sz)


class Buf:
    __slots__ = ("w", "rs")

    def __init__(self):
        self.w = None
        self.rs = []


class Prog:
    def __init__(self, nc):
        self.nc = nc
        self.cmds = {e: [] for e in ENGS}
        self.sems = {}
        self.cnt = {}
        self.qwait = {}
        self.qgate = {}
        self.stk = None

    def sem(self, name):
        h = self.stk.enter_context(self.nc.semaphore(name))
        self.sems[name] = h
        self.cnt[name] = 0

    def _edges(self, eng, rd, wr, extra):
        toks = list(extra)
        for b in rd:
            toks.append(b.w)
        for b in wr:
            toks.append(b.w)
            toks.extend(b.rs)
        self.waits(eng, toks)

    def _done(self, rd, wr, tok):
        for b in rd:
            b.rs.append(tok)
        for b in wr:
            b.w = tok
            b.rs = []

    def dma(self, q, out, in_, rd=(), wr=(), extra=(), **kw):
        self._edges(q, rd, wr, extra)
        sem = "q_" + q
        w = self.qwait.get(sem, 0)
        if w > self.qgate.get(sem, 0):
            s0 = self.sems[sem]

            def g(e, s0=s0, w=w):
                e.wait_ge(s0, w)

            self.cmds[q].append(g)
            self.qgate[sem] = w
        self.cnt[sem] += 16
        v = self.cnt[sem]
        s = self.sems[sem]

        def f(e, out=out, in_=in_, s=s, kw=kw):
            e.dma_start(out=out, in_=in_, **kw).then_inc(s, 16)

        self.cmds[q].append(f)
        tok = (sem, v)
        self._done(rd, wr, tok)
        return tok

    def op(self, eng, fn, rd=(), wr=(), extra=()):
        self._edges(eng, rd, wr, extra)
        sem = "t_" + eng
        self.cnt[sem] += 1
        v = self.cnt[sem]
        s = self.sems[sem]

        def f(e, fn=fn, s=s):
            fn(e).then_inc(s, 1)

        self.cmds[eng].append(f)
        tok = (sem, v)
        self._done(rd, wr, tok)
        return tok

    def waits(self, eng, toks):
        best = {}
        for t in toks:
            if t is None:
                continue
            sem, v = t
            if sem.startswith("q_"):
                v = self.cnt[sem]
                if self.qwait.get(sem, 0) < v:
                    self.qwait[sem] = v
            if best.get(sem, -1) < v:
                best[sem] = v
        for sem in sorted(best):
            v = best[sem]
            s = self.sems[sem]

            def f(e, s=s, v=v):
                e.wait_ge(s, v)

            self.cmds[eng].append(f)

    def replay(self, block):
        for eng in ENGS:
            cmds = self.cmds[eng]

            def runner(e, cmds=cmds):
                for fn in cmds:
                    fn(e)

            getattr(block, eng)(runner)



def build_program(n_ranks=NCORES, sublayers=SUBLAYERS):
    nc = bass.Bass("TRN2", target_bir_lowering=False)
    di = {}

    def inp(name, shape):
        di[name] = nc.dram_tensor(name, shape, F32, kind="ExternalInput").ap()

    inp("xin", [T, D])
    inp("tpos", [T, 1])
    inp("modd", [1, 1])
    for w in ("l0a_wkT", "l0a_wvT", "l0a_wqT", "l0a_wkpT", "l0a_wqpT", "l0a_woT"):
        inp(w, [D, D])
    for v in ("l0a_bk", "l0a_bv", "l0a_bq", "l0a_bkp", "l0a_bqp", "l0a_bo",
              "l0a_ps", "l0a_lng", "l0a_lnb"):
        inp(v, [1, D])
    inp("l0f_wT", [D, D])
    for w in ("l0b_wqT", "l0b_wkT", "l0b_wvT", "l0b_woT"):
        inp(w, [D, D])
    for v in ("l0b_bq", "l0b_bk", "l0b_bv", "l0b_bo"):
        inp(v, [1, D])
    for w in ("l1a_wq", "l1a_wk", "l1a_wv"):
        inp(w, [D, D])
    inp("l1a_lns", [1, D])
    inp("l1a_lnb", [1, D])
    inp("l1f_wT", [D, D])
    inp("l1f_gwT", [D, 8])
    inp("l1f_uT", [D, 128])
    inp("l1f_v", [128, D])
    inp("mb_ipT", [D, 2048])
    inp("mb_xpT", [D, 96])
    inp("mb_dtT", [64, D])
    inp("mb_opT", [2048, D])
    inp("mb_cx", [D, 4])
    inp("mb_cz", [D, 4])
    inp("mb_alog", [D, 16])
    inp("mb_dvec", [D, 1])
    inp("mb_dtb", [D, 1])

    xout = nc.dram_tensor("xout", [T, D], F32, kind="ExternalOutput").ap()

    xspill = nc.dram_tensor("xspill", [T, D], F32).ap()
    tspill = nc.dram_tensor("tspill", [T, D], F32).ap()
    uspill = nc.dram_tensor("uspill", [T, D], F32).ap()
    ospill = nc.dram_tensor("ospill", [T, D], F32).ap()
    xnspill = nc.dram_tensor("xnspill", [T, D], F32).ap()
    ccx_in = [nc.dram_tensor(f"ccx{j}_in", [D, T], F32).ap() for j in range(2)]
    cch_in = nc.dram_tensor("cch_in", [4, 2048], F32).ap()
    cch_out = nc.dram_tensor("cch_out", [8, 2048], F32).ap()
    cc0_in = nc.dram_tensor("cc0_in", [D, 16], F32).ap()
    bcsp = nc.dram_tensor("bcsp", [32, 2048], F32).ap()
    cc0_out = nc.dram_tensor("cc0_out", [2 * D, 16], F32).ap()
    ccx_out = [nc.dram_tensor(f"ccx{j}_out", [2 * D, T], F32).ap()
               for j in range(2)]
    groups = [[2 * i, 2 * i + 1] for i in range(max(1, n_ranks // 2))]

    P = Prog(nc)
    dbg_n = [0]

    def dbg(name, ap_fn, rd):
        if name not in DEBUG_TAPS:
            return
        shp = [ap_fn.shape[0], ap_fn.shape[1]]
        o = nc.dram_tensor("dbg_" + name + f"_{dbg_n[0]}", shp, F32,
                           kind="ExternalOutput").ap()
        dbg_n[0] += 1
        P.dma("scalar", o, ap_fn, rd=rd)

    with contextlib.ExitStack() as stk:
        P.stk = stk
        for s in ("q_sync", "q_scalar", "q_gpsimd", "q_vector",
                  "t_sync", "t_tensor", "t_vector", "t_scalar", "t_gpsimd"):
            P.sem(s)

        def sbuf(name, shape, dt=F32):
            return stk.enter_context(nc.sbuf_tensor(name, shape, dt))

        A = sbuf("A", [128, 16384])
        Ba = sbuf("Ba", [128, 16384])
        W = sbuf("W", [128, 8192])
        Sc = sbuf("Sc", [128, 2048])
        Cn = sbuf("Cn", [128, 3072])
        cols = sbuf("cols", [128, 512])
        ident = sbuf("ident", [128, 128])
        ident16 = sbuf("ident16", [128, 128], BF16)
        tri = sbuf("tri", [128, 128])
        rowb = sbuf("rowb", [128, 128])
        wq16 = sbuf("wq16", [128, 8192], BF16)
        aq16 = sbuf("aq16", [128, 1024], BF16)
        aqf16 = sbuf("aqf16", [128, 1024], BF16)
        rint = sbuf("rint", [128, 1024], I32)

        psum = [stk.enter_context(nc.psum_tensor(f"p{i}", [128, 512], F32))
                for i in range(8)]
        pB = [Buf() for _ in range(8)]
        prr = [0]

        def pacq():
            i = prr[0]
            prr[0] = (prr[0] + 1) % 8
            return psum[i], pB[i]

        bA = [Buf() for _ in range(NT)]
        bAf = [Buf() for _ in range(ND)]
        bBa = [Buf() for _ in range(ND)]
        bW = Buf()
        bSc = [Buf() for _ in range(2)]
        bCn = [Buf() for _ in range(3)]
        bcols = Buf()
        bident = Buf()
        bident16 = Buf()
        btri = Buf()
        bwq = Buf()
        baq = Buf()
        baqf = Buf()
        brint = Buf()

        def A_t(i):
            return A[:, ts(i, 1024)]

        def A_f(j):
            return A[:, ts(j, 2048)]

        def Ba_t(j):
            return Ba[:, ts(j, 2048)]

        bBs = [Buf() for _ in range(16)]

        def Bs_t(i):
            return Ba[:, ts(i, 1024)]

        ba_state = ["slots"]

        def ba_ensure_slots():
            if ba_state[0] != "slots":
                ba_alias_fm2slots()

        def ba_ensure_fm():
            if ba_state[0] != "fm":
                ba_alias_slots2fm()

        def ba_alias_fm2slots():
            ba_state[0] = "slots"
            for i in range(16):
                j = i // 2
                rs = []
                if bBa[j].w is not None:
                    rs.append(bBa[j].w)
                rs.extend(bBa[j].rs)
                bBs[i].w = None
                bBs[i].rs = rs

        def ba_alias_slots2fm():
            ba_state[0] = "fm"
            for j in range(ND):
                rs = []
                for i in (2 * j, 2 * j + 1):
                    if bBs[i].w is not None:
                        rs.append(bBs[i].w)
                    rs.extend(bBs[i].rs)
                bBa[j].w = None
                bBa[j].rs = rs

        def Sc_t(i):
            return Sc[:, ts(i, 1024)]

        def Cn_t(i):
            return Cn[:, ts(i, 1024)]

        def a_alias_tm2fm():
            for j in range(ND):
                rs = []
                w = None
                for i in (2 * j, 2 * j + 1):
                    if bA[i].w is not None:
                        rs.append(bA[i].w)
                    rs.extend(bA[i].rs)
                bAf[j].w = None
                bAf[j].rs = rs

        def a_alias_fm2tm():
            for i in range(NT):
                j = i // 2
                rs = []
                if bAf[j].w is not None:
                    rs.append(bAf[j].w)
                rs.extend(bAf[j].rs)
                bA[i].w = None
                bA[i].rs = rs

        ones_c = nc.const_aps.tensor(1.0, [128, 128], F32)
        kconst = sbuf("kconst", [128, 16])
        bkconst = Buf()
        _kc_map = {}
        _kc_next = [0]

        def KC(v):
            v = float(v)
            if v not in _kc_map:
                j = _kc_next[0]
                _kc_next[0] += 1
                P.op("vector", lambda e, j=j, v=v: e.memset(
                    kconst[:, j:j + 1], v), wr=[bkconst])
                _kc_map[v] = kconst[:, j:j + 1]
            return _kc_map[v]
        P.op("gpsimd", lambda e: e.affine_select(
            ident[:], ones_c, pattern=[[1, 128]], compare_op=OP.is_equal,
            fill=0.0, base=0, channel_multiplier=-1), wr=[bident])
        P.op("vector", lambda e: e.tensor_copy(ident16[:], ident[:]),
             rd=[bident], wr=[bident16])
        P.op("vector", lambda e: e.memset(tri[:], 0.0), wr=[btri])
        for _blk in range(2):
            P.op("gpsimd", lambda e, _blk=_blk: e.affine_select(
                tri[ts(_blk, 64), ts(_blk, 64)], ones_c[0:64, 0:64],
                pattern=[[1, 64]], compare_op=OP.is_ge, fill=0.0, base=0,
                channel_multiplier=-1), rd=[btri], wr=[btri])

        def transpose_f32(src_ap, dst_ap, rd, wr):
            p, pb = pacq()
            P.op("tensor", lambda e, p=p: e.transpose(
                p[:, 0:128], src_ap, ident[:]), rd=[*rd, bident], wr=[pb])
            return P.op("vector", lambda e, p=p: e.tensor_copy(
                dst_ap, p[:, 0:128]), rd=[pb], wr=list(wr))

        def transpose_bf16(src_ap, dst_ap, rd, wr):
            p, pb = pacq()
            P.op("tensor", lambda e, p=p: e.transpose(
                p[:].bitcast(BF16)[:, 0:128], src_ap, ident16[:]),
                rd=[*rd, bident16], wr=[pb])
            return P.op("vector", lambda e, p=p: e.tensor_copy(
                dst_ap, p[:].bitcast(BF16)[:, 0:128]), rd=[pb], wr=list(wr))

        for i in range(NT):
            P.dma("sync", A_t(i), di["xin"][ts(i), :], wr=[bA[i]])

        browb = Buf()

        def load_cols(dst_cols_ap, name):
            P.dma("sync", rowb[0:8, :],
                  di[name].rearrange("o (a f) -> (o a) f", a=8), wr=[browb])
            p, pb = pacq()
            P.op("tensor", lambda e, p=p: e.transpose(
                p[:, 0:8], rowb[0:8, :], ident[0:8, 0:8]),
                rd=[browb, bident], wr=[pb])
            return P.op("vector", lambda e, p=p: e.tensor_copy(
                dst_cols_ap, p[:, 0:8]), rd=[pb], wr=[bcols])

        def bcast_row(slot, name):
            return P.dma("sync", Cn_t(slot),
                         di[name].partition_broadcast(128), wr=[bCn[slot]])

        def load_weight(name, csl=None, woff=0, rsl=None):
            src = di[name]
            if rsl is None:
                rsl = slice(0, src.shape[0])
            nk = (rsl.stop - rsl.start) // 128
            if csl is None:
                csl = slice(0, src.shape[1])
            ncols = csl.stop - csl.start
            for kc in range(nk):
                r0 = rsl.start + kc * 128
                P.dma("sync", W[:, woff + kc * ncols:woff + (kc + 1) * ncols],
                      src[r0:r0 + 128, csl], wr=[bW])

        def Wk_(kc, ncols=D):
            return W[:, kc * ncols:(kc + 1) * ncols]

        def emit_ln(eps=1e-5, fold2=None):
            for i in range(NT):
                P.op("vector", lambda e, i=i: e.reduce_sum(
                    cols[:, i:i + 1], A_t(i), axis=AX), rd=[bA[i]], wr=[bcols])
                P.op("scalar", lambda e, i=i: e.activation(
                    Sc_t(0), A_t(i), AF.Square, accum_out=cols[:, 16 + i:17 + i]),
                    rd=[bA[i]], wr=[bSc[0], bcols])
            P.op("vector", lambda e: e.tensor_scalar(
                cols[:, 32:48], cols[:, 0:16], 1.0 / D, None, OP.mult),
                rd=[bcols], wr=[bcols])
            P.op("vector", lambda e: e.tensor_scalar(
                cols[:, 112:128], cols[:, 16:32], 1.0 / D, None, OP.mult),
                rd=[bcols], wr=[bcols])
            P.op("vector", lambda e: e.tensor_tensor(
                cols[:, 96:112], cols[:, 32:48], cols[:, 32:48], OP.mult),
                rd=[bcols], wr=[bcols])
            P.op("vector", lambda e: e.tensor_tensor(
                cols[:, 48:64], cols[:, 112:128], cols[:, 96:112], OP.subtract),
                rd=[bcols], wr=[bcols])
            _b = KC(eps)
            P.op("scalar", lambda e, _b=_b: e.activation(
                cols[:, 96:112], cols[:, 48:64], AF.Ln, bias=_b),
                rd=[bcols, bkconst], wr=[bcols])
            if fold2 is None:
                P.op("scalar", lambda e: e.activation(
                    cols[:, 64:80], cols[:, 96:112], AF.Exp, scale=-0.5),
                    rd=[bcols], wr=[bcols])
            else:
                P.op("scalar", lambda e: e.activation(
                    cols[:, 128:144], cols[:, 96:112], AF.Exp, scale=-0.5),
                    rd=[bcols], wr=[bcols])
                P.op("vector", lambda e: e.tensor_tensor(
                    cols[:, 144:160], cols[:, 128:144], cols[:, 128:144],
                    OP.mult), rd=[bcols], wr=[bcols])
                P.op("vector", lambda e: e.tensor_tensor(
                    cols[:, 160:176], cols[:, 48:64], cols[:, 144:160],
                    OP.mult), rd=[bcols], wr=[bcols])
                _b2 = KC(fold2)
                P.op("scalar", lambda e, _b2=_b2: e.activation(
                    cols[:, 176:192], cols[:, 160:176], AF.Ln,
                    bias=_b2), rd=[bcols, bkconst], wr=[bcols])
                P.op("scalar", lambda e: e.activation(
                    cols[:, 144:160], cols[:, 176:192], AF.Exp, scale=-0.5),
                    rd=[bcols], wr=[bcols])
                P.op("vector", lambda e: e.tensor_tensor(
                    cols[:, 64:80], cols[:, 128:144], cols[:, 144:160],
                    OP.mult), rd=[bcols], wr=[bcols])
            P.op("vector", lambda e: e.tensor_tensor(
                cols[:, 96:112], cols[:, 32:48], cols[:, 64:80], OP.mult),
                rd=[bcols], wr=[bcols])
            P.op("vector", lambda e: e.tensor_scalar(
                cols[:, 80:96], cols[:, 96:112], -1.0, None, OP.mult),
                rd=[bcols], wr=[bcols])

        def ln_s(i):
            return cols[:, 64 + i:65 + i]

        def ln_b(i):
            return cols[:, 80 + i:81 + i]

        def emit_l0a():
            ba_ensure_fm()
            emit_ln()
            for i in range(NT):
                P.op("scalar", lambda e, i=i: e.activation(
                    Sc_t(0), A_t(i), AF.Identity, bias=ln_b(i), scale=ln_s(i)),
                    rd=[bA[i], bcols], wr=[bSc[0]])
                for kc in range(ND):
                    transpose_f32(Sc_t(0)[:, ts(kc)], Ba_t(kc)[:, ts(i)],
                                  rd=[bSc[0]], wr=[bBa[kc]])
                P.dma("scalar", xspill[ts(i), :], A_t(i), rd=[bA[i]])
            a_alias_tm2fm()
            dbg("xnfm0", Ba_t(0)[:, 0:1024], [bBa[0]])

            bias_cols = cols[:, 448:456]

            def proj_to_fm(dst_f, dst_b, bias_name):
                load_cols(bias_cols, bias_name)
                for ec in range(ND):
                    for tc in range(4):
                        p, pb = pacq()
                        for dc in range(ND):
                            P.op("tensor", lambda e, dc=dc, ec=ec, tc=tc, p=p:
                                 e.matmul(p[:], Wk_(dc)[:, ts(ec)],
                                          Ba_t(dc)[:, ts(tc, 512)],
                                          start=(dc == 0), stop=(dc == ND - 1)),
                                 rd=[bW, bBa[dc]], wr=[pb])
                        P.op("scalar", lambda e, ec=ec, tc=tc, p=p:
                             e.activation(dst_f(ec)[:, ts(tc, 512)], p[:],
                                          AF.Identity,
                                          bias=bias_cols[:, ec:ec + 1]),
                             rd=[pb, bcols], wr=[dst_b[ec]])

            def phase_to_hbm(dst_dram, bias_name):
                bcast_row(0, bias_name)
                for i in range(NT):
                    for oc in range(2):
                        p, pb = pacq()
                        for ec in range(ND):
                            P.op("tensor", lambda e, ec=ec, i=i, oc=oc, p=p:
                                 e.matmul(p[:], A_f(ec)[:, ts(i)],
                                          Wk_(ec)[:, ts(oc, 512)],
                                          start=(ec == 0), stop=(ec == ND - 1)),
                                 rd=[bW, bAf[ec]], wr=[pb])
                        P.op("vector", lambda e, oc=oc, p=p: e.tensor_tensor(
                            Sc_t(0)[:, ts(oc, 512)], p[:],
                            Cn_t(0)[:, ts(oc, 512)], OP.add),
                            rd=[pb, bCn[0]], wr=[bSc[0]])
                    P.op("scalar", lambda e: e.activation(
                        Sc_t(1), Sc_t(0), AF.Tanh), rd=[bSc[0]], wr=[bSc[1]])
                    P.op("vector", lambda e: e.tensor_tensor(
                        Sc_t(0), Sc_t(1), Cn_t(1), OP.mult),
                        rd=[bSc[1], bCn[1]], wr=[bSc[0]])
                    P.dma("scalar", dst_dram[ts(i), :], Sc_t(0), rd=[bSc[0]])

            bcast_row(1, "l0a_ps")
            load_weight("l0a_wkT")
            proj_to_fm(A_f, bAf, "l0a_bk")
            dbg("keyfm0", A_f(0)[:, 0:1024], [bAf[0]])
            load_weight("l0a_wkpT")
            phase_to_hbm(tspill, "l0a_bkp")
            load_weight("l0a_wqT")
            proj_to_fm(A_f, bAf, "l0a_bq")
            load_weight("l0a_wqpT")
            phase_to_hbm(uspill, "l0a_bqp")
            a_alias_fm2tm()
            load_weight("l0a_wvT")
            bcast_row(0, "l0a_bv")
            for i in range(NT):
                for oc in range(2):
                    p, pb = pacq()
                    for dc in range(ND):
                        P.op("tensor", lambda e, dc=dc, i=i, oc=oc, p=p:
                             e.matmul(p[:], Ba_t(dc)[:, ts(i)],
                                      Wk_(dc)[:, ts(oc, 512)],
                                      start=(dc == 0), stop=(dc == ND - 1)),
                             rd=[bW, bBa[dc]], wr=[pb])
                    P.op("vector", lambda e, i=i, oc=oc, p=p: e.tensor_tensor(
                        A_t(i)[:, ts(oc, 512)], p[:], Cn_t(0)[:, ts(oc, 512)],
                        OP.add), rd=[pb, bCn[0]], wr=[bA[i]])

            ba_alias_fm2slots()
            bcast_row(1, "l0a_lng")
            bcast_row(2, "l0a_lnb")
            load_weight("l0a_woT")
            bcast_row(0, "l0a_bo")
            for i in range(NT):
                P.dma("sync", Bs_t(1), tspill[ts(i), :], wr=[bBs[1]])
                P.dma("sync", Bs_t(2), uspill[ts(i), :], wr=[bBs[2]])
                P.dma("sync", Bs_t(9), xspill[ts(i), :], wr=[bBs[9]])
                if i == 0:
                    dbg("kp0", Bs_t(1), [bBs[1]])
                    dbg("qp0", Bs_t(2), [bBs[2]])
                    dbg("val0", A_t(0), [bA[0]])
                P.op("scalar", lambda e: e.activation(
                    Bs_t(3), Bs_t(1), AF.Sin, scale=0.5), rd=[bBs[1]],
                    wr=[bBs[3]])
                P.op("scalar", lambda e: e.activation(
                    Bs_t(3), Bs_t(3), AF.Square), rd=[bBs[3]], wr=[bBs[3]])
                P.op("vector", lambda e: e.tensor_scalar(
                    Bs_t(3), Bs_t(3), -2.0, 1.0, OP.mult, OP.add),
                    rd=[bBs[3]], wr=[bBs[3]])
                P.op("scalar", lambda e: e.activation(
                    Bs_t(4), Bs_t(1), AF.Sin), rd=[bBs[1]], wr=[bBs[4]])
                P.op("vector", lambda e, i=i: e.tensor_tensor(
                    Bs_t(3), A_t(i), Bs_t(3), OP.mult), rd=[bA[i], bBs[3]],
                    wr=[bBs[3]])
                P.op("vector", lambda e, i=i: e.tensor_tensor(
                    Bs_t(4), A_t(i), Bs_t(4), OP.mult), rd=[bA[i], bBs[4]],
                    wr=[bBs[4]])
                for (sslot, dslot) in ((3, 5), (4, 6)):
                    for oc in range(2):
                        p, pb = pacq()
                        P.op("tensor", lambda e, sslot=sslot, oc=oc, p=p:
                             e.matmul(p[:], tri[:],
                                      Bs_t(sslot)[:, ts(oc, 512)],
                                      start=True, stop=True),
                             rd=[btri, bBs[sslot]], wr=[pb])
                        P.op("vector", lambda e, dslot=dslot, oc=oc, p=p:
                             e.tensor_copy(Bs_t(dslot)[:, ts(oc, 512)], p[:]),
                             rd=[pb], wr=[bBs[dslot]])
                if i == 0:
                    dbg("memr0", Bs_t(5), [bBs[5]])
                P.op("scalar", lambda e: e.activation(
                    Bs_t(3), Bs_t(2), AF.Sin, scale=0.5), rd=[bBs[2], bBs[3]],
                    wr=[bBs[3]])
                P.op("scalar", lambda e: e.activation(
                    Bs_t(3), Bs_t(3), AF.Square), rd=[bBs[3]], wr=[bBs[3]])
                P.op("vector", lambda e: e.tensor_scalar(
                    Bs_t(3), Bs_t(3), -2.0, 1.0, OP.mult, OP.add),
                    rd=[bBs[3]], wr=[bBs[3]])
                P.op("scalar", lambda e: e.activation(
                    Bs_t(4), Bs_t(2), AF.Sin), rd=[bBs[2], bBs[4]], wr=[bBs[4]])
                P.op("vector", lambda e: e.tensor_tensor(
                    Bs_t(3), Bs_t(5), Bs_t(3), OP.mult), rd=[bBs[5], bBs[3]],
                    wr=[bBs[3]])
                P.op("vector", lambda e: e.tensor_tensor(
                    Bs_t(4), Bs_t(6), Bs_t(4), OP.mult), rd=[bBs[6], bBs[4]],
                    wr=[bBs[4]])
                P.op("vector", lambda e: e.tensor_tensor(
                    Bs_t(7), Bs_t(3), Bs_t(4), OP.add), rd=[bBs[3], bBs[4]],
                    wr=[bBs[7]])
                if i == 0:
                    dbg("retr0", Bs_t(7), [bBs[7]])
                P.op("vector", lambda e: e.reduce_sum(
                    cols[:, 200:201], Bs_t(7), axis=AX), rd=[bBs[7]],
                    wr=[bcols])
                P.op("scalar", lambda e: e.activation(
                    Sc_t(0), Bs_t(7), AF.Square, accum_out=cols[:, 201:202]),
                    rd=[bBs[7]], wr=[bSc[0], bcols])
                P.op("vector", lambda e: e.tensor_scalar(
                    cols[:, 202:203], cols[:, 200:201], 1.0 / D, None,
                    OP.mult), rd=[bcols], wr=[bcols])
                P.op("vector", lambda e: e.tensor_scalar(
                    cols[:, 203:204], cols[:, 201:202], 1.0 / D, None,
                    OP.mult), rd=[bcols], wr=[bcols])
                P.op("vector", lambda e: e.tensor_tensor(
                    cols[:, 204:205], cols[:, 202:203], cols[:, 202:203],
                    OP.mult), rd=[bcols], wr=[bcols])
                P.op("vector", lambda e: e.tensor_tensor(
                    cols[:, 204:205], cols[:, 203:204], cols[:, 204:205],
                    OP.subtract), rd=[bcols], wr=[bcols])
                _b3 = KC(1e-5 * D)
                P.op("scalar", lambda e, _b3=_b3: e.activation(
                    cols[:, 205:206], cols[:, 204:205], AF.Ln, bias=_b3),
                    rd=[bcols, bkconst], wr=[bcols])
                P.op("scalar", lambda e: e.activation(
                    cols[:, 205:206], cols[:, 205:206], AF.Exp, scale=-0.5),
                    rd=[bcols], wr=[bcols])
                P.op("vector", lambda e: e.tensor_tensor(
                    cols[:, 206:207], cols[:, 202:203], cols[:, 205:206],
                    OP.mult), rd=[bcols], wr=[bcols])
                P.op("vector", lambda e: e.tensor_scalar(
                    cols[:, 206:207], cols[:, 206:207], -1.0, None, OP.mult),
                    rd=[bcols], wr=[bcols])
                P.op("scalar", lambda e: e.activation(
                    Bs_t(7), Bs_t(7), AF.Identity, bias=cols[:, 206:207],
                    scale=cols[:, 205:206]), rd=[bBs[7], bcols], wr=[bBs[7]])
                P.op("vector", lambda e: e.tensor_tensor(
                    Bs_t(7), Bs_t(7), Cn_t(1), OP.mult), rd=[bBs[7], bCn[1]],
                    wr=[bBs[7]])
                P.op("vector", lambda e: e.tensor_tensor(
                    Bs_t(7), Bs_t(7), Cn_t(2), OP.add), rd=[bBs[7], bCn[2]],
                    wr=[bBs[7]])
                if i == 0:
                    dbg("lnr0", Bs_t(7), [bBs[7]])
                for kc in range(ND):
                    transpose_f32(Bs_t(7)[:, ts(kc)], Bs_t(0)[:, ts(kc)],
                                  rd=[bBs[7]], wr=[bBs[0]])
                for oc in range(2):
                    p, pb = pacq()
                    for dc in range(ND):
                        P.op("tensor", lambda e, dc=dc, oc=oc, p=p: e.matmul(
                            p[:], Bs_t(0)[:, ts(dc)], Wk_(dc)[:, ts(oc, 512)],
                            start=(dc == 0), stop=(dc == ND - 1)),
                            rd=[bW, bBs[0]], wr=[pb])
                    P.op("scalar", lambda e, oc=oc, p=p: e.activation(
                        Bs_t(8)[:, ts(oc, 512)], p[:], AF.Identity),
                        rd=[pb], wr=[bBs[8]])
                P.op("vector", lambda e: e.tensor_tensor(
                    Bs_t(8), Bs_t(8), Cn_t(0), OP.add), rd=[bBs[8], bCn[0]],
                    wr=[bBs[8]])
                P.op("scalar", lambda e, i=i: e.activation(
                    Bs_t(1), Bs_t(9), AF.Identity, bias=ln_b(i),
                    scale=ln_s(i)), rd=[bBs[9], bcols], wr=[bBs[1]])
                P.op("vector", lambda e: e.tensor_tensor(
                    Bs_t(8), Bs_t(8), Bs_t(1), OP.add), rd=[bBs[8], bBs[1]],
                    wr=[bBs[8]])
                P.op("vector", lambda e, i=i: e.tensor_tensor(
                    A_t(i), Bs_t(9), Bs_t(8), OP.add), rd=[bBs[9], bBs[8]],
                    wr=[bA[i]])

        ffn_ready = [False]
        ffn_s127 = cols[:, 496:497]

        def emit_l0ffn_setup():
            load_weight("l0f_wT")
            for kc in range(ND):
                P.op("scalar", lambda e, kc=kc: e.activation(
                    Sc_t(0), Wk_(kc), AF.Abs,
                    accum_out=cols[:, 480 + kc:481 + kc]),
                    rd=[bW], wr=[bSc[0], bcols])
            P.op("vector", lambda e: e.reduce_sum(
                cols[:, 488:489], cols[:, 480:488], axis=AX), rd=[bcols],
                wr=[bcols])
            p, pb = pacq()
            P.op("tensor", lambda e, p=p: e.matmul(
                p[0:1, 0:1], cols[:, 488:489], ones_c[:, 0:1],
                start=True, stop=True), rd=[bcols], wr=[pb])
            P.op("vector", lambda e, p=p: e.tensor_copy(
                cols[0:1, 489:490], p[0:1, 0:1]), rd=[pb], wr=[bcols])
            _b4 = KC(1e-8)
            P.op("scalar", lambda e, _b4=_b4: e.activation(
                cols[0:1, 490:491], cols[0:1, 489:490], AF.Identity,
                scale=1.0 / float(D * D), bias=_b4[0:1, :]),
                rd=[bcols, bkconst], wr=[bcols])
            P.op("vector", lambda e: e.reciprocal(
                cols[0:1, 491:492], cols[0:1, 490:491]), rd=[bcols], wr=[bcols])
            P.op("scalar", lambda e: e.activation(
                cols[0:1, 492:493], cols[0:1, 490:491], AF.Identity,
                scale=1.0 / 127.0), rd=[bcols], wr=[bcols])
            p2, pb2 = pacq()
            P.op("tensor", lambda e, p=p2: e.matmul(
                p[:, 0:2], ones_c[0:1, :], cols[0:1, 491:493],
                start=True, stop=True), rd=[bcols], wr=[pb2])
            P.op("vector", lambda e, p=p2: e.tensor_copy(
                cols[:, 493:494], p[:, 0:1]), rd=[pb2], wr=[bcols])
            P.op("vector", lambda e, p=p2: e.tensor_copy(
                ffn_s127, p[:, 1:2]), rd=[pb2], wr=[bcols])
            for kc in range(ND):
                P.op("vector", lambda e, kc=kc: e.tensor_scalar(
                    Bs_t(10), Wk_(kc), cols[:, 493:494], None, OP.mult),
                    rd=[bW, bcols], wr=[bBs[10]])
                P.op("vector", lambda e: e.tensor_scalar(
                    Bs_t(11), Bs_t(10), 0.5, None, OP.is_ge), rd=[bBs[10]],
                    wr=[bBs[11]])
                P.op("vector", lambda e: e.tensor_scalar(
                    Bs_t(12), Bs_t(10), -0.5, None, OP.is_le), rd=[bBs[10]],
                    wr=[bBs[12]])
                P.op("vector", lambda e, kc=kc: e.tensor_tensor(
                    wq16[:, ts(kc, 1024)], Bs_t(11), Bs_t(12), OP.subtract),
                    rd=[bBs[11], bBs[12]], wr=[bwq])
            ffn_ready[0] = True

        def emit_l0ffn():
            ba_ensure_slots()
            if not ffn_ready[0]:
                emit_l0ffn_setup()
            emit_ln()
            for i in range(NT):
                c0 = 208 + 8 * (i % 2)
                P.op("scalar", lambda e, i=i: e.activation(
                    Bs_t(0), A_t(i), AF.Identity, bias=ln_b(i), scale=ln_s(i)),
                    rd=[bA[i], bcols], wr=[bBs[0]])
                P.op("vector", lambda e, c0=c0: e.reduce_max(
                    cols[:, c0:c0 + 1], Bs_t(0), axis=AX,
                    apply_absolute_value=True), rd=[bBs[0]], wr=[bcols])
                P.op("vector", lambda e, c0=c0: e.tensor_scalar(
                    cols[:, c0 + 1:c0 + 2], cols[:, c0:c0 + 1], 1e-8, None,
                    OP.add), rd=[bcols], wr=[bcols])
                P.op("vector", lambda e, c0=c0: e.reciprocal(
                    cols[:, c0 + 2:c0 + 3], cols[:, c0 + 1:c0 + 2]),
                    rd=[bcols], wr=[bcols])
                P.op("vector", lambda e, c0=c0: e.tensor_scalar(
                    cols[:, c0 + 3:c0 + 4], cols[:, c0 + 2:c0 + 3], 127.0,
                    None, OP.mult), rd=[bcols], wr=[bcols])
                P.op("scalar", lambda e, c0=c0: e.activation(
                    Bs_t(1), Bs_t(0), AF.Identity,
                    scale=cols[:, c0 + 3:c0 + 4]), rd=[bBs[0], bcols],
                    wr=[bBs[1]])
                P.op("vector", lambda e: e.tensor_copy(rint[:], Bs_t(1)),
                     rd=[bBs[1]], wr=[brint])
                P.op("vector", lambda e: e.tensor_copy(Bs_t(2), rint[:]),
                     rd=[brint], wr=[bBs[2]])
                P.op("vector", lambda e: e.tensor_tensor(
                    Bs_t(3), Bs_t(1), Bs_t(2), OP.subtract),
                    rd=[bBs[1], bBs[2]], wr=[bBs[3]])
                P.op("vector", lambda e: e.tensor_scalar(
                    Bs_t(4), Bs_t(3), 0.5, None, OP.is_ge), rd=[bBs[3]],
                    wr=[bBs[4]])
                P.op("vector", lambda e: e.tensor_scalar(
                    Bs_t(3), Bs_t(3), -0.5, None, OP.is_le), rd=[bBs[3]],
                    wr=[bBs[3]])
                P.op("vector", lambda e: e.tensor_tensor(
                    Bs_t(4), Bs_t(4), Bs_t(3), OP.subtract),
                    rd=[bBs[4], bBs[3]], wr=[bBs[4]])
                P.op("vector", lambda e: e.tensor_tensor(
                    aq16[:], Bs_t(2), Bs_t(4), OP.add), rd=[bBs[2], bBs[4]],
                    wr=[baq])
                for kc in range(ND):
                    transpose_bf16(aq16[:, ts(kc)], aqf16[:, ts(kc)],
                                   rd=[baq], wr=[baqf])
                P.op("vector", lambda e, c0=c0: e.tensor_tensor(
                    cols[:, c0 + 4:c0 + 5], cols[:, c0 + 1:c0 + 2], ffn_s127,
                    OP.mult), rd=[bcols], wr=[bcols])
                for oc in range(2):
                    p, pb = pacq()
                    for kc in range(ND):
                        P.op("tensor", lambda e, kc=kc, oc=oc, p=p: e.matmul(
                            p[:], aqf16[:, ts(kc)],
                            wq16[:, ts(kc, 1024)][:, ts(oc, 512)],
                            start=(kc == 0), stop=(kc == ND - 1)),
                            rd=[baqf, bwq], wr=[pb])
                    P.op("scalar", lambda e, oc=oc, p=p, c0=c0: e.activation(
                        Bs_t(1)[:, ts(oc, 512)], p[:], AF.Identity,
                        scale=cols[:, c0 + 4:c0 + 5]), rd=[pb, bcols],
                        wr=[bBs[1]])
                P.op("vector", lambda e, i=i: e.tensor_tensor(
                    A_t(i), A_t(i), Bs_t(1), OP.add), rd=[bBs[1], bA[i]],
                    wr=[bA[i]])

        def emit_l1f():
            ba_ensure_slots()
            load_weight("l1f_wT")
            P.dma("sync", rowb[:, 0:64].rearrange("p (a g) -> p a g", a=8),
                  di["l1f_gwT"].rearrange("(a p) g -> p a g", p=128),
                  wr=[browb])
            P.dma("sync", Cn_t(0), di["l1f_v"], wr=[bCn[0]])
            P.dma("sync", Cn_t(1).rearrange("p (a r) -> p a r", a=8),
                  di["l1f_uT"].rearrange("(a p) r -> p a r", p=128),
                  wr=[bCn[1]])
            emit_ln()
            for i in range(NT):
                P.op("scalar", lambda e, i=i: e.activation(
                    Bs_t(0), A_t(i), AF.Identity, bias=ln_b(i), scale=ln_s(i)),
                    rd=[bA[i], bcols], wr=[bBs[0]])
                for kc in range(ND):
                    transpose_f32(Bs_t(0)[:, ts(kc)], Bs_t(1)[:, ts(kc)],
                                  rd=[bBs[0]], wr=[bBs[1]])
                pg, pgb = pacq()
                for dc in range(ND):
                    P.op("tensor", lambda e, dc=dc, p=pg: e.matmul(
                        p[:, 0:8], Bs_t(1)[:, ts(dc)],
                        rowb[:, dc * 8:(dc + 1) * 8],
                        start=(dc == 0), stop=(dc == ND - 1)),
                        rd=[bBs[1], browb], wr=[pgb])
                pt, ptb = pacq()
                for dc in range(ND):
                    P.op("tensor", lambda e, dc=dc, p=pt: e.matmul(
                        p[:, 0:128], Bs_t(1)[:, ts(dc)],
                        Cn_t(1)[:, ts(dc)],
                        start=(dc == 0), stop=(dc == ND - 1)),
                        rd=[bBs[1], bCn[1]], wr=[ptb])
                P.op("scalar", lambda e, p=pg: e.activation(
                    Bs_t(2)[:, 0:8], p[:, 0:8], AF.Exp), rd=[pgb], wr=[bBs[2]])
                P.op("vector", lambda e: e.reduce_sum(
                    cols[:, 220:221], Bs_t(2)[:, 0:8], axis=AX),
                    rd=[bBs[2]], wr=[bcols])
                P.op("vector", lambda e: e.reciprocal(
                    cols[:, 221:222], cols[:, 220:221]), rd=[bcols], wr=[bcols])
                P.op("scalar", lambda e: e.activation(
                    Bs_t(2)[:, 16:24], Bs_t(2)[:, 0:8], AF.Identity,
                    scale=cols[:, 221:222]), rd=[bBs[2], bcols], wr=[bBs[2]])
                P.op("vector", lambda e, p=pt: e.tensor_tensor(
                    Bs_t(3)[:, 0:128].rearrange("p (e r) -> p e r", e=8),
                    p[:, 0:128].rearrange("p (e r) -> p e r", e=8),
                    Bs_t(2)[:, 16:24].unsqueeze(2).broadcast_to([128, 8, 16]),
                    OP.mult), rd=[ptb, bBs[2]], wr=[bBs[3]])
                transpose_f32(Bs_t(3)[:, 0:128], Bs_t(3)[:, 128:256],
                              rd=[bBs[3]], wr=[bBs[3]])
                for oc in range(2):
                    pl, plb = pacq()
                    P.op("tensor", lambda e, oc=oc, p=pl: e.matmul(
                        p[:], Bs_t(3)[:, 128:256], Cn_t(0)[:, ts(oc, 512)],
                        start=True, stop=True), rd=[bBs[3], bCn[0]], wr=[plb])
                    pb_, pbb = pacq()
                    for dc in range(ND):
                        P.op("tensor", lambda e, dc=dc, oc=oc, p=pb_: e.matmul(
                            p[:], Bs_t(1)[:, ts(dc)], Wk_(dc)[:, ts(oc, 512)],
                            start=(dc == 0), stop=(dc == ND - 1)),
                            rd=[bBs[1], bW], wr=[pbb])
                    P.op("scalar", lambda e, oc=oc, p=pb_: e.activation(
                        Bs_t(4)[:, ts(oc, 512)], p[:], AF.Identity),
                        rd=[pbb], wr=[bBs[4]])
                    P.op("vector", lambda e, oc=oc, p=pl: e.scalar_tensor_tensor(
                        Bs_t(4)[:, ts(oc, 512)], p[:], 1.0 / 16.0,
                        Bs_t(4)[:, ts(oc, 512)], OP.mult, OP.add),
                        rd=[plb, bBs[4]], wr=[bBs[4]])
                P.op("vector", lambda e, i=i: e.tensor_tensor(
                    A_t(i), A_t(i), Bs_t(4), OP.add), rd=[bBs[4], bA[i]],
                    wr=[bA[i]])

        bccin = [Buf(), Buf()]
        bccout = [Buf(), Buf()]
        bxsp = Buf()
        bosp = Buf()
        bxnsp = Buf()
        btsp = Buf()

        def fresh(olds):
            b = Buf()
            rs = []
            for o in olds:
                if o.w is not None:
                    rs.append(o.w)
                rs.extend(o.rs)
            b.rs = [t for t in rs if t is not None]
            return b

        def A_sl(slot, c0, c1):
            return A[:, slot * 1024 + c0:slot * 1024 + c1]

        def emit_mha(cfg):
            ba_ensure_fm()
            cci, cco = ccx_in[cfg["cc"]], ccx_out[cfg["cc"]]
            bci, bco = bccin[cfg["cc"]], bccout[cfg["cc"]]
            rope = cfg["rope"]
            if cfg["fold2"]:
                emit_ln(eps=1e-5, fold2=1e-6)
                bcast_row(1, cfg["lns"])
                bcast_row(2, cfg["lnb"])
            else:
                emit_ln(eps=1e-5)
            for i in range(NT):
                P.op("scalar", lambda e, i=i: e.activation(
                    Sc_t(0), A_t(i), AF.Identity, bias=ln_b(i), scale=ln_s(i)),
                    rd=[bA[i], bcols], wr=[bSc[0]])
                if cfg["fold2"]:
                    P.op("vector", lambda e: e.tensor_tensor(
                        Sc_t(0), Sc_t(0), Cn_t(1), OP.mult),
                        rd=[bSc[0], bCn[1]], wr=[bSc[0]])
                    P.op("vector", lambda e: e.tensor_tensor(
                        Sc_t(0), Sc_t(0), Cn_t(2), OP.add),
                        rd=[bSc[0], bCn[2]], wr=[bSc[0]])
                    P.dma("scalar", xnspill[ts(i), :], Sc_t(0), rd=[bSc[0]],
                          wr=[bxnsp])
                for kc in range(ND):
                    transpose_f32(Sc_t(0)[:, ts(kc)], Ba_t(kc)[:, ts(i)],
                                  rd=[bSc[0]], wr=[bBa[kc]])
                P.dma("scalar", xspill[ts(i), :], A_t(i), rd=[bA[i]],
                      wr=[bxsp])
            for kc in range(ND):
                P.dma("scalar", cci[ts(kc), :], Ba_t(kc), rd=[bBa[kc]],
                      wr=[bci])
            P.op("gpsimd", lambda e: e.collective_compute(
                "AllGather", OP.bypass, replica_groups=groups,
                ins=[cci.opt()], outs=[cco.opt()]), rd=[bci], wr=[bco])

            allA = bA + bAf
            bA0 = fresh(allA)
            bA1 = fresh(allA)
            bA2 = fresh(allA)
            bA3 = fresh(allA)
            bqf = [fresh(allA) for _ in range(4)]
            bkf = fresh(allA)
            bs14 = fresh(allA)
            bs15 = fresh(allA)

            c2pi = (6.28125, 1.9352436065674e-3, 6.336622499e-7)
            inv2pi = 0.15915494309189535

            def build_tab(tcol_ap, dst_cos, dst_sin, dbuf):
                th = A_sl(3, 128, 160)
                fr = A_sl(3, 512, 544)
                v = A_sl(3, 544, 576)
                kk = A_sl(3, 576, 608)
                dd = A_sl(3, 608, 640)
                g1 = A_sl(3, 640, 672)
                g2 = A_sl(3, 672, 704)
                red = A_sl(3, 704, 736)
                sA = A_sl(3, 736, 768)
                sB = A_sl(3, 768, 800)
                cp = A_sl(3, 800, 832)
                P.op("vector", lambda e: e.tensor_scalar(
                    fr, th, tcol_ap, None, OP.mult), rd=[bA3], wr=[bA3])
                P.op("vector", lambda e: e.tensor_scalar(
                    v, fr, inv2pi, None, OP.mult), rd=[bA3], wr=[bA3])
                P.op("vector", lambda e: e.tensor_copy(rint[:, 0:32], v),
                     rd=[bA3], wr=[brint])
                P.op("vector", lambda e: e.tensor_copy(kk, rint[:, 0:32]),
                     rd=[brint, bA3], wr=[bA3])
                P.op("vector", lambda e: e.tensor_tensor(
                    dd, v, kk, OP.subtract), rd=[bA3], wr=[bA3])
                P.op("vector", lambda e: e.tensor_scalar(
                    g1, dd, 0.5, None, OP.is_ge), rd=[bA3], wr=[bA3])
                P.op("vector", lambda e: e.tensor_scalar(
                    g2, dd, -0.5, None, OP.is_le), rd=[bA3], wr=[bA3])
                P.op("vector", lambda e: e.tensor_tensor(
                    g1, g1, g2, OP.subtract), rd=[bA3], wr=[bA3])
                P.op("vector", lambda e: e.tensor_tensor(
                    kk, kk, g1, OP.add), rd=[bA3], wr=[bA3])
                P.op("vector", lambda e: e.scalar_tensor_tensor(
                    red, kk, -c2pi[0], fr, OP.mult, OP.add),
                    rd=[bA3], wr=[bA3])
                P.op("vector", lambda e: e.scalar_tensor_tensor(
                    red, kk, -c2pi[1], red, OP.mult, OP.add),
                    rd=[bA3], wr=[bA3])
                P.op("vector", lambda e: e.scalar_tensor_tensor(
                    red, kk, -c2pi[2], red, OP.mult, OP.add),
                    rd=[bA3], wr=[bA3])
                P.op("scalar", lambda e: e.activation(
                    sA, red, AF.Sin, scale=0.5), rd=[bA3], wr=[bA3])
                P.op("scalar", lambda e: e.activation(
                    sB, red, AF.Sin, scale=0.25), rd=[bA3], wr=[bA3])
                P.op("scalar", lambda e: e.activation(
                    cp, sB, AF.Square), rd=[bA3], wr=[bA3])
                P.op("vector", lambda e: e.tensor_scalar(
                    cp, cp, -2.0, 1.0, OP.mult, OP.add), rd=[bA3], wr=[bA3])
                P.op("vector", lambda e, dst_sin=dst_sin: e.tensor_tensor(
                    dst_sin, sA, cp, OP.mult), rd=[bA3], wr=[dbuf])
                P.op("vector", lambda e, dst_sin=dst_sin: e.tensor_scalar(
                    dst_sin, dst_sin, 2.0, None, OP.mult), rd=[dbuf],
                    wr=[dbuf])
                P.op("scalar", lambda e: e.activation(
                    sA, sA, AF.Square), rd=[bA3], wr=[bA3])
                P.op("vector", lambda e, dst_cos=dst_cos: e.tensor_scalar(
                    dst_cos, sA, -2.0, 1.0, OP.mult, OP.add),
                    rd=[bA3], wr=[dbuf])

            if rope:
                P.op("gpsimd", lambda e: e.iota(
                    A_sl(3, 128, 160), pattern=[[1, 32]], base=0,
                    channel_multiplier=0,
                    allow_small_or_imprecise_dtypes=True), wr=[bA3])
                _lnth = -0.2878231366242557
                P.op("scalar", lambda e: e.activation(
                    A_sl(3, 128, 160), A_sl(3, 128, 160), AF.Exp,
                    scale=_lnth), rd=[bA3], wr=[bA3])
                for g in range(32):
                    P.op("gpsimd", lambda e, g=g: e.iota(
                        A_sl(3, 160, 168), pattern=[[0, 8]], base=g * 128,
                        channel_multiplier=1,
                        allow_small_or_imprecise_dtypes=True),
                        rd=[bA3], wr=[bA3])
                    build_tab(A_sl(3, 160, 161),
                              A_sl(0, g * 32, g * 32 + 32),
                              A_sl(1, g * 32, g * 32 + 32), bA0)
                for i in range(NT):
                    P.dma("sync", A_sl(3, 160, 161), di["tpos"][ts(i), :],
                          rd=[bA3], wr=[bA3])
                    build_tab(A_sl(3, 160, 161),
                              A_sl(2, i * 32, i * 32 + 32),
                              A_sl(2, 512 + i * 32, 512 + i * 32 + 32), bA2)

            dbg("cosk0", A_sl(0, 0, 1024), [bA0])
            dbg("sink0", A_sl(1, 0, 1024), [bA0])
            dbg("cosq0", A_sl(2, 0, 1024), [bA2])

            def rope_apply(tm_ap, tm_buf, cos_ap, sin_ap, tab_buf,
                           t1_ap, t2_ap, sbuf_):
                v3 = tm_ap.rearrange("p (g j) -> p g j", j=32)
                cb = cos_ap.unsqueeze(1).broadcast_to([128, 16, 32])
                sb2 = sin_ap.unsqueeze(1).broadcast_to([128, 16, 32])
                P.op("vector", lambda e: e.tensor_tensor(
                    t1_ap.rearrange("p (g j) -> p g j", j=32), v3, cb,
                    OP.mult), rd=[tm_buf, tab_buf], wr=[sbuf_])
                P.op("vector", lambda e: e.tensor_tensor(
                    t2_ap.rearrange("p (g j) -> p g j", j=32), v3, sb2,
                    OP.mult), rd=[tm_buf, tab_buf], wr=[sbuf_])
                h4 = tm_ap.rearrange("p (h q j) -> p h q j", q=2, j=32)
                t1h = t1_ap.rearrange("p (h q j) -> p h q j", q=2, j=32)
                t2h = t2_ap.rearrange("p (h q j) -> p h q j", q=2, j=32)
                P.op("vector", lambda e: e.tensor_tensor(
                    h4[:, :, 0, :], t1h[:, :, 0, :], t2h[:, :, 1, :],
                    OP.subtract), rd=[sbuf_], wr=[tm_buf])
                P.op("vector", lambda e: e.tensor_tensor(
                    h4[:, :, 1, :], t1h[:, :, 1, :], t2h[:, :, 0, :],
                    OP.add), rd=[sbuf_], wr=[tm_buf])

            bxn_kc = fresh(bBs + bBa)
            bvau = fresh(bBs + bBa)
            bprB = fresh(bBs + bBa)
            bacc = fresh(bBs + bBa)
            bkvs = bprB
            if cfg["bias"]:
                bcast_row(0, cfg["bq"])
                bcast_row(1, cfg["bk"])
                bcast_row(2, cfg["bv"])

            def vau(t):
                return Ba[:, 4096 + t * 520:4096 + t * 520 + 520]

            def prA(kt):
                return Sc[:, kt * 512:kt * 512 + 512]

            def prB(kt):
                if kt < 3:
                    return Ba[:, 6176 + kt * 512:6176 + kt * 512 + 512]
                return A_sl(15, 0, 512)

            def prBbuf(kt):
                return bprB if kt < 3 else bs15

            for g in range(2):
                for dc in range(ND):
                    P.dma("sync", Ba[:, dc * 1024:(dc + 1) * 1024],
                          cci[ts(dc), 0:1024], rd=[bci], wr=[bxn_kc])
                    P.dma("sync",
                          Ba[:, 8192 + dc * 1024:8192 + (dc + 1) * 1024],
                          cci[ts(dc), 1024:2048], rd=[bci], wr=[bacc])
                load_weight(cfg["wq"], csl=ts(g, 512))

                def xn_own(dc, tt):
                    base = dc * 1024 if tt < 8 else 8192 + dc * 1024
                    off = (tt % 8) * 128
                    return Ba[:, base + off:base + off + 128]

                for tt in range(NT):
                    if cfg["qfm_direct"]:
                        for oc in range(4):
                            p, pb = pacq()
                            for dc in range(ND):
                                P.op("tensor", lambda e, dc=dc, oc=oc, tt=tt,
                                     p=p: e.matmul(
                                         p[:, 0:128],
                                         Wk_(dc, 512)[:, ts(oc)],
                                         xn_own(dc, tt),
                                         start=(dc == 0),
                                         stop=(dc == ND - 1)),
                                     rd=[bW, bxn_kc, bacc], wr=[pb])
                            P.op("vector", lambda e, oc=oc, tt=tt, p=p:
                                 e.tensor_copy(
                                     A[:, (4 + 2 * oc) * 1024 + tt * 128:
                                       (4 + 2 * oc) * 1024 + tt * 128 + 128],
                                     p[:, 0:128]), rd=[pb], wr=[bqf[oc]])
                    else:
                        p, pb = pacq()
                        for dc in range(ND):
                            P.op("tensor", lambda e, dc=dc, tt=tt, p=p:
                                 e.matmul(p[:], xn_own(dc, tt),
                                          Wk_(dc, 512),
                                          start=(dc == 0),
                                          stop=(dc == ND - 1)),
                                 rd=[bW, bxn_kc, bacc], wr=[pb])
                        P.op("vector", lambda e, p=p, g=g: e.tensor_tensor(
                            A_sl(14, 0, 512), p[:],
                            Cn_t(0)[:, g * 512:g * 512 + 512], OP.add),
                            rd=[pb, bCn[0]], wr=[bs14])
                        rope_apply(A_sl(14, 0, 512), bs14,
                                   A_sl(2, tt * 32, tt * 32 + 32),
                                   A_sl(2, 512 + tt * 32, 512 + tt * 32 + 32),
                                   bA2, A_sl(14, 512, 1024),
                                   A_sl(15, 512, 1024), bs15)
                        if tt == 0 and g == 0:
                            dbg("qrope0", A_sl(14, 0, 512), [bs14])
                        for c in range(4):
                            transpose_f32(
                                A_sl(14, c * 128, c * 128 + 128),
                                A[:, (4 + 2 * c) * 1024 + tt * 128:
                                  (4 + 2 * c) * 1024 + tt * 128 + 128],
                                rd=[bs14], wr=[bqf[c]])

                load_weight(cfg["wk"], csl=ts(g, 512))
                load_weight(cfg["wv"], csl=ts(g, 512), woff=4096)
                for t in range(4):
                    P.op("vector", lambda e, t=t: e.memset(
                        vau(t).rearrange("p (h e) -> p h e", e=65)[:, :, 64:65],
                        1.0), rd=[bxn_kc], wr=[bvau])

                for kc in range(8):
                    blk = kc // 4
                    c0 = (kc * 512) % 2048
                    for dc in range(ND):
                        P.dma("sync", Ba[:, dc * 512:(dc + 1) * 512],
                              cco[blk * 1024 + dc * 128:
                                  blk * 1024 + (dc + 1) * 128, c0:c0 + 512],
                              rd=[bco], wr=[bxn_kc])
                    for kt in range(4):
                        gt = kc * 4 + kt
                        p, pb = pacq()
                        for dc in range(ND):
                            P.op("tensor", lambda e, dc=dc, kt=kt, p=p:
                                 e.matmul(
                                     p[:],
                                     Ba[:, dc * 512 + kt * 128:
                                        dc * 512 + kt * 128 + 128],
                                     W[:, dc * 512:(dc + 1) * 512],
                                     start=(dc == 0), stop=(dc == ND - 1)),
                                 rd=[bW, bxn_kc], wr=[pb])
                        if cfg["bias"]:
                            P.op("vector", lambda e, p=p, g=g: e.tensor_tensor(
                                Ba[:, 6176:6688], p[:],
                                Cn_t(1)[:, g * 512:g * 512 + 512], OP.add),
                                rd=[pb, bCn[1]], wr=[bkvs])
                        else:
                            P.op("vector", lambda e, p=p: e.tensor_copy(
                                Ba[:, 6176:6688], p[:]), rd=[pb], wr=[bkvs])
                        if rope:
                            rope_apply(Ba[:, 6176:6688], bkvs,
                                       A_sl(0, gt * 32, gt * 32 + 32),
                                       A_sl(1, gt * 32, gt * 32 + 32),
                                       bA0, Ba[:, 6688:7200],
                                       Ba[:, 7200:7712], bkvs)
                        if kc == 0 and kt == 0 and g == 0:
                            dbg("krope0", Ba[:, 6176:6688], [bkvs])
                        for c in range(4):
                            transpose_f32(
                                Ba[:, 6176 + c * 128:6176 + c * 128 + 128],
                                A[:, 12 * 1024 + c * 512 + kt * 128:
                                  12 * 1024 + c * 512 + kt * 128 + 128],
                                rd=[bkvs], wr=[bkf])
                        p, pb = pacq()
                        for dc in range(ND):
                            P.op("tensor", lambda e, dc=dc, kt=kt, p=p:
                                 e.matmul(
                                     p[:],
                                     Ba[:, dc * 512 + kt * 128:
                                        dc * 512 + kt * 128 + 128],
                                     W[:, 4096 + dc * 512:
                                       4096 + (dc + 1) * 512],
                                     start=(dc == 0), stop=(dc == ND - 1)),
                                 rd=[bW, bxn_kc], wr=[pb])
                        dst = vau(kt).rearrange(
                            "p (h e) -> p h e", e=65)[:, :, 0:64]
                        if cfg["bias"]:
                            P.op("vector", lambda e, p=p, g=g, dst=dst:
                                 e.tensor_tensor(
                                     dst,
                                     p[:].rearrange("p (h e) -> p h e", e=64),
                                     Cn_t(2)[:, g * 512:g * 512 + 512]
                                     .rearrange("p (h e) -> p h e", e=64),
                                     OP.add), rd=[pb, bCn[2]], wr=[bvau])
                        else:
                            P.op("vector", lambda e, p=p, dst=dst:
                                 e.tensor_copy(
                                     dst,
                                     p[:].rearrange("p (h e) -> p h e",
                                                    e=64)),
                                 rd=[pb], wr=[bvau])
                    if kc == 0 and g == 0:
                        dbg("vau0", vau(0)[:, 0:520], [bvau])
                    for hp in range(4):
                        for qb in range(4):
                            for kt in range(4):
                                pA_, pAb = pacq()
                                pB_, pBb = pacq()
                                kbase = 12 * 1024 + hp * 512 + kt * 128
                                qbase = (4 + 2 * hp) * 1024 + qb * 512
                                P.op("tensor", lambda e, kbase=kbase,
                                     qbase=qbase, p=pA_: e.matmul(
                                         p[:], A[0:64, kbase:kbase + 128],
                                         A[0:64, qbase:qbase + 512],
                                         start=True, stop=True,
                                         tile_position=(0, 0)),
                                     rd=[bkf, bqf[hp]], wr=[pAb])
                                P.op("tensor", lambda e, kbase=kbase,
                                     qbase=qbase, p=pB_: e.matmul(
                                         p[:], A[64:128, kbase:kbase + 128],
                                         A[64:128, qbase:qbase + 512],
                                         start=True, stop=True,
                                         tile_position=(64, 0)),
                                     rd=[bkf, bqf[hp]], wr=[pBb])
                                P.op("scalar", lambda e, kt=kt, p=pA_:
                                     e.activation(prA(kt), p[:], AF.Exp,
                                                  scale=cfg["sc"]),
                                     rd=[pAb],
                                     wr=[bSc[0] if kt < 2 else bSc[1]])
                                P.op("scalar", lambda e, kt=kt, p=pB_:
                                     e.activation(prB(kt), p[:], AF.Exp,
                                                  scale=cfg["sc"]),
                                     rd=[pBb], wr=[prBbuf(kt)])
                            if (kc == 0 and g == 0 and hp == 0
                                    and qb == 0):
                                dbg("prA0", prA(0), [bSc[0]])
                            for hh in range(2):
                                h = hp * 2 + hh
                                for qt in range(4):
                                    po, pob = pacq()
                                    for kt in range(4):
                                        pr = (prA(kt) if hh == 0 else prB(kt))
                                        prb = ((bSc[0] if kt < 2 else bSc[1])
                                               if hh == 0 else prBbuf(kt))
                                        P.op("tensor", lambda e, kt=kt, h=h,
                                             qt=qt, po=po, pr=pr: e.matmul(
                                                 po[:, 0:65],
                                                 pr[:, qt * 128:qt * 128 + 128],
                                                 vau(kt)[:, h * 65:
                                                         h * 65 + 65],
                                                 start=(kt == 0),
                                                 stop=(kt == 3)),
                                             rd=[prb, bvau], wr=[pob])
                                    qtile = qb * 4 + qt
                                    ob0 = 8192 + qtile * 512 + h * 64
                                    db0 = 3 * 1024 + qtile * 8 + h
                                    if kc == 0:
                                        P.op("vector", lambda e, po=po,
                                             ob0=ob0: e.tensor_copy(
                                                 Ba[:, ob0:ob0 + 64],
                                                 po[:, 0:64]),
                                             rd=[pob], wr=[bacc])
                                        P.op("vector", lambda e, po=po,
                                             db0=db0: e.tensor_copy(
                                                 A[:, db0:db0 + 1],
                                                 po[:, 64:65]),
                                             rd=[pob], wr=[bA3])
                                    else:
                                        P.op("vector", lambda e, po=po,
                                             ob0=ob0: e.tensor_tensor(
                                                 Ba[:, ob0:ob0 + 64],
                                                 Ba[:, ob0:ob0 + 64],
                                                 po[:, 0:64], OP.add),
                                             rd=[pob], wr=[bacc])
                                        P.op("vector", lambda e, po=po,
                                             db0=db0: e.tensor_tensor(
                                                 A[:, db0:db0 + 1],
                                                 A[:, db0:db0 + 1],
                                                 po[:, 64:65], OP.add),
                                             rd=[pob], wr=[bA3])
                if g == 0:
                    dbg("acc0", Ba[:, 8192:8192 + 512], [bacc])
                    dbg("den0", A_sl(3, 0, 128), [bA3])
                P.op("vector", lambda e: e.reciprocal(
                    A_sl(3, 256, 384), A_sl(3, 0, 128)), rd=[bA3], wr=[bA3])
                for qtile in range(NT):
                    for h in range(8):
                        P.op("scalar", lambda e, qtile=qtile, h=h:
                             e.activation(
                                 Sc[:, h * 64:h * 64 + 64],
                                 Ba[:, 8192 + qtile * 512 + h * 64:
                                    8192 + qtile * 512 + h * 64 + 64],
                                 AF.Identity,
                                 scale=A[:, 3 * 1024 + 256 + qtile * 8 + h:
                                         3 * 1024 + 256 + qtile * 8 + h + 1]),
                             rd=[bacc, bA3], wr=[bSc[0]])
                    P.dma("scalar", ospill[ts(qtile), g * 512:g * 512 + 512],
                          Sc[:, 0:512], rd=[bSc[0]], wr=[bosp])

            for i in range(NT):
                bA[i] = fresh([bA0, bA1, bA2, bA3, *bqf, bkf, bs14, bs15,
                               bA[i], bAf[i // 2]])
            if cfg["wo"] is None:
                for i in range(NT):
                    P.dma("sync", A_t(i), xspill[ts(i), :],
                          rd=[bxsp], wr=[bA[i]])
                    P.dma("sync", Sc_t(0), xnspill[ts(i), :], rd=[bxnsp],
                          wr=[bSc[0]])
                    P.op("vector", lambda e, i=i: e.tensor_tensor(
                        A_t(i), A_t(i), Sc_t(0), OP.add),
                        rd=[bSc[0]], wr=[bA[i]])
                    P.dma("sync", Sc_t(0), ospill[ts(i), :], rd=[bosp],
                          wr=[bSc[0]])
                    P.op("vector", lambda e, i=i: e.tensor_tensor(
                        A_t(i), A_t(i), Sc_t(0), OP.add),
                        rd=[bSc[0]], wr=[bA[i]])
            else:
                bwo = fresh([bA0, bA1, bA2, bA3, *bqf, bkf])
                for dc in range(ND):
                    P.dma("sync", A[:, dc * 1024:(dc + 1) * 1024],
                          di[cfg["wo"]][ts(dc), :], rd=[bwo], wr=[bwo])
                bcast_row(0, cfg["bo"])
                bo14 = fresh([bs14, bs15])
                for i in range(NT):
                    P.dma("sync", Sc[:, 0:1024], ospill[ts(i), :],
                          rd=[bosp], wr=[bSc[0], bSc[1]])
                    for c in range(ND):
                        transpose_f32(Sc[:, c * 128:c * 128 + 128],
                                      Sc[:, 1024 + c * 128:
                                         1024 + c * 128 + 128],
                                      rd=[bSc[0], bSc[1]],
                                      wr=[bSc[0], bSc[1]])
                    for oc in range(2):
                        p, pb = pacq()
                        for dc in range(ND):
                            P.op("tensor", lambda e, dc=dc, oc=oc, p=p:
                                 e.matmul(
                                     p[:],
                                     Sc[:, 1024 + dc * 128:
                                        1024 + dc * 128 + 128],
                                     A[:, dc * 1024 + oc * 512:
                                       dc * 1024 + oc * 512 + 512],
                                     start=(dc == 0), stop=(dc == ND - 1)),
                                 rd=[bSc[0], bSc[1], bwo], wr=[pb])
                        P.op("vector", lambda e, p=p, oc=oc: e.tensor_copy(
                            A_sl(14, oc * 512, oc * 512 + 512), p[:]),
                            rd=[pb], wr=[bo14])
                    P.op("vector", lambda e: e.tensor_tensor(
                        A_sl(14, 0, 1024), A_sl(14, 0, 1024), Cn_t(0),
                        OP.add), rd=[bCn[0]], wr=[bo14])
                    P.dma("sync", A_sl(15, 0, 1024), xspill[ts(i), :],
                          rd=[bxsp], wr=[bs15])
                    P.op("vector", lambda e: e.tensor_tensor(
                        A_sl(14, 0, 1024), A_sl(14, 0, 1024),
                        A_sl(15, 0, 1024), OP.add), rd=[bs15], wr=[bo14])
                    P.dma("scalar", tspill[ts(i), :], A_sl(14, 0, 1024),
                          rd=[bo14], wr=[btsp])
                for i in range(NT):
                    bA[i] = fresh([bwo, bo14, bs15, bA[i]])
                    P.dma("sync", A_t(i), tspill[ts(i), :], rd=[btsp],
                          wr=[bA[i]])
            for j in range(ND):
                bBa[j] = fresh([bacc, bxn_kc, bvau, bprB, bs15, bBa[j]])
            for i in range(16):
                bBs[i] = fresh([bBa[i // 2]])
            ba_state[0] = "slots"

        def emit_l0b():
            emit_mha(dict(cc=0, rope=True, fold2=False, bias=True,
                          wq="l0b_wqT", wk="l0b_wkT", wv="l0b_wvT",
                          bq="l0b_bq", bk="l0b_bk", bv="l0b_bv",
                          wo="l0b_woT", bo="l0b_bo", sc=0.125,
                          qfm_direct=False, lns=None, lnb=None))

        def emit_l1a():
            emit_mha(dict(cc=1, rope=False, fold2=True, bias=False,
                          wq="l1a_wq", wk="l1a_wk", wv="l1a_wv",
                          bq=None, bk=None, bv=None,
                          wo=None, bo=None, sc=0.03125,
                          qfm_direct=True, lns="l1a_lns", lnb="l1a_lnb"))

        def emit_mb():
            ba_ensure_fm()
            bch_i = Buf()
            bch_o = Buf()
            bh0_i = Buf()
            bh0_o = Buf()
            bWs = bW
            buspb = Buf()
            bCns = Buf()
            bCnk = Buf()
            emit_ln(eps=1e-5)
            for i in range(NT):
                P.op("scalar", lambda e, i=i: e.activation(
                    Sc_t(0), A_t(i), AF.Identity, bias=ln_b(i), scale=ln_s(i)),
                    rd=[bA[i], bcols], wr=[bSc[0]])
                for kc in range(ND):
                    transpose_f32(Sc_t(0)[:, ts(kc)], Ba_t(kc)[:, ts(i)],
                                  rd=[bSc[0]], wr=[bBa[kc]])
                P.dma("scalar", xspill[ts(i), :], A_t(i), rd=[bA[i]],
                      wr=[bxsp])
            P.dma("sync", cols[0:1, 230:231], di["modd"], wr=[bcols])
            pmk, pmkb = pacq()
            P.op("tensor", lambda e, p=pmk: e.matmul(
                p[:, 0:1], ones_c[0:1, :], cols[0:1, 230:231],
                start=True, stop=True), rd=[bcols], wr=[pmkb])
            P.op("vector", lambda e, p=pmk: e.tensor_copy(
                cols[:, 231:232], p[:, 0:1]), rd=[pmkb], wr=[bcols])
            P.op("vector", lambda e: e.tensor_scalar(
                cols[:, 232:233], cols[:, 231:232], -1.0, 1.0, OP.mult,
                OP.add), rd=[bcols], wr=[bcols])
            modd_c = cols[:, 231:232]
            mev_c = cols[:, 232:233]

            allA = bA + bAf
            bxm = [fresh(allA) for _ in range(ND)]

            def load_ip(half):
                load_weight("mb_ipT", csl=slice(half * 1024,
                                                half * 1024 + 1024))

            for half in range(2):
                load_ip(half)
                for bt, rows in ((0, (0, 1)), (15, (127,))):
                    for oc in range(2):
                        p, pb = pacq()
                        for dc in range(ND):
                            P.op("tensor", lambda e, dc=dc, bt=bt, oc=oc, p=p:
                                 e.matmul(p[:], Ba_t(dc)[:, ts(bt)],
                                          Wk_(dc)[:, ts(oc, 512)],
                                          start=(dc == 0),
                                          stop=(dc == ND - 1)),
                                 rd=[bW, bBa[dc]], wr=[pb])
                        P.op("vector", lambda e, oc=oc, p=p: e.tensor_copy(
                            Sc_t(1)[:, ts(oc, 512)], p[:]), rd=[pb],
                            wr=[bSc[1]])
                    for r, ridx in (((0, 1), None) if False else
                                    [(r, {(0, 0): 0, (0, 1): 1,
                                          (15, 127): 2}[(bt, r)])
                                     for r in rows]):
                        P.dma("scalar",
                              cch_in[ridx:ridx + 1,
                                     half * 1024:half * 1024 + 1024],
                              Sc_t(1)[r:r + 1, :], rd=[bSc[1]], wr=[bch_i])
                        if ridx == 2:
                            P.dma("scalar",
                                  cch_in[3:4, half * 1024:half * 1024 + 1024],
                                  Sc_t(1)[r:r + 1, :], rd=[bSc[1]],
                                  wr=[bch_i])
            P.op("gpsimd", lambda e: e.collective_compute(
                "AllGather", OP.bypass, replica_groups=groups,
                ins=[cch_in.opt()], outs=[cch_out.opt()]),
                rd=[bch_i], wr=[bch_o])
            P.dma("sync", Sc[0:8, 0:2048], cch_out, rd=[bch_o],
                  wr=[bSc[0], bSc[1]])
            for zi in range(2):
                for ct in range(ND):
                    p, pb = pacq()
                    P.op("tensor", lambda e, zi=zi, ct=ct, p=p: e.transpose(
                        p[:, 0:8], Sc[0:8, zi * 1024 + ct * 128:
                                      zi * 1024 + ct * 128 + 128],
                        ident[0:8, 0:8]),
                        rd=[bSc[0], bSc[1], bident], wr=[pb])
                    base = zi * 64 + ct * 8
                    P.op("vector", lambda e, p=p, base=base: e.tensor_tensor(
                        rowb[:, base:base + 1], p[:, 2:3], modd_c, OP.mult),
                        rd=[pb, bcols], wr=[browb])
                    P.op("vector", lambda e, p=p, base=base: e.tensor_tensor(
                        rowb[:, base + 1:base + 2], p[:, 4:5], mev_c,
                        OP.mult), rd=[pb, bcols], wr=[browb])
                    P.op("vector", lambda e, p=p, base=base: e.tensor_tensor(
                        rowb[:, base + 2:base + 3], p[:, 5:6], mev_c,
                        OP.mult), rd=[pb, bcols], wr=[browb])

            load_ip(0)
            for ct in range(ND):
                for tc in range(4):
                    p, pb = pacq()
                    for dc in range(ND):
                        P.op("tensor", lambda e, dc=dc, ct=ct, tc=tc, p=p:
                             e.matmul(p[:], Wk_(dc)[:, ts(ct)],
                                      Ba_t(dc)[:, ts(tc, 512)],
                                      start=(dc == 0), stop=(dc == ND - 1)),
                             rd=[bW, bBa[dc]], wr=[pb])
                    P.op("vector", lambda e, ct=ct, tc=tc, p=p: e.tensor_copy(
                        A[:, ct * 2048 + tc * 512:ct * 2048 + tc * 512 + 512],
                        p[:]), rd=[pb], wr=[bxm[ct]])

            P.dma("sync", Cn[:, 2048:2080].rearrange("p (a k) -> p a k", a=8),
                  di["mb_cx"].rearrange("(a p) k -> p a k", p=128), wr=[bCnk])
            P.dma("sync", Cn[:, 2080:2112].rearrange("p (a k) -> p a k", a=8),
                  di["mb_cz"].rearrange("(a p) k -> p a k", p=128), wr=[bCnk])

            def conv_silu(src_ap, src_buf, wbase, wc_buf, halo_base,
                          dst_ap, dst_buf):
                scr = Cn[:, 0:2048]
                w_cols = Cn[:, wbase:wbase + 4]
                P.op("vector", lambda e: e.tensor_scalar(
                    scr, src_ap, w_cols[:, 1:2], None, OP.mult),
                    rd=[src_buf, wc_buf], wr=[bCns])
                P.op("vector", lambda e: e.scalar_tensor_tensor(
                    scr[:, 1:2048], src_ap[:, 0:2047], w_cols[:, 0:1],
                    scr[:, 1:2048], OP.mult, OP.add),
                    rd=[src_buf, wc_buf, bCns], wr=[bCns])
                P.op("vector", lambda e: e.scalar_tensor_tensor(
                    scr[:, 0:2047], src_ap[:, 1:2048], w_cols[:, 2:3],
                    scr[:, 0:2047], OP.mult, OP.add),
                    rd=[src_buf, wc_buf, bCns], wr=[bCns])
                P.op("vector", lambda e: e.scalar_tensor_tensor(
                    scr[:, 0:2046], src_ap[:, 2:2048], w_cols[:, 3:4],
                    scr[:, 0:2046], OP.mult, OP.add),
                    rd=[src_buf, wc_buf, bCns], wr=[bCns])
                hL = rowb[:, halo_base:halo_base + 1]
                hR0 = rowb[:, halo_base + 1:halo_base + 2]
                hR1 = rowb[:, halo_base + 2:halo_base + 3]
                P.op("vector", lambda e: e.scalar_tensor_tensor(
                    scr[:, 0:1], hL, w_cols[:, 0:1], scr[:, 0:1],
                    OP.mult, OP.add), rd=[browb, wc_buf, bCns], wr=[bCns])
                P.op("vector", lambda e: e.scalar_tensor_tensor(
                    scr[:, 2047:2048], hR0, w_cols[:, 2:3],
                    scr[:, 2047:2048], OP.mult, OP.add),
                    rd=[browb, wc_buf, bCns], wr=[bCns])
                P.op("vector", lambda e: e.scalar_tensor_tensor(
                    scr[:, 2046:2047], hR0, w_cols[:, 3:4],
                    scr[:, 2046:2047], OP.mult, OP.add),
                    rd=[browb, wc_buf, bCns], wr=[bCns])
                P.op("vector", lambda e: e.scalar_tensor_tensor(
                    scr[:, 2047:2048], hR1, w_cols[:, 3:4],
                    scr[:, 2047:2048], OP.mult, OP.add),
                    rd=[browb, wc_buf, bCns], wr=[bCns])
                P.op("scalar", lambda e: e.activation(
                    dst_ap, scr, AF.Silu), rd=[bCns], wr=[dst_buf])

            load_ip(1)
            for ct in range(ND):
                for tc in range(4):
                    p, pb = pacq()
                    for dc in range(ND):
                        P.op("tensor", lambda e, dc=dc, ct=ct, tc=tc, p=p:
                             e.matmul(p[:], Wk_(dc)[:, ts(ct)],
                                      Ba_t(dc)[:, ts(tc, 512)],
                                      start=(dc == 0), stop=(dc == ND - 1)),
                             rd=[bW, bBa[dc]], wr=[pb])
                    P.op("vector", lambda e, tc=tc, p=p: e.tensor_copy(
                        Sc[:, tc * 512:tc * 512 + 512], p[:]), rd=[pb],
                        wr=[bSc[0], bSc[1]])
                conv_silu(Sc[:, 0:2048], bSc[0], 2080 + ct * 4, bCnk,
                          64 + ct * 8, Sc[:, 0:2048], bSc[0])
                P.dma("scalar",
                      tspill[ts(ct, 256), :].rearrange(
                          "(p x) d -> p (x d)", p=128),
                      Sc[:, 0:2048], rd=[bSc[0]], wr=[btsp])
            for ct in range(ND):
                conv_silu(A[:, ct * 2048:(ct + 1) * 2048], bxm[ct],
                          2048 + ct * 4, bCnk, ct * 8,
                          A[:, ct * 2048:(ct + 1) * 2048], bxm[ct])

            dbg("mxc0", A[:, 0:2048], [bxm[0]])
            load_weight("mb_xpT")
            for tc in range(4):
                p, pb = pacq()
                for dc in range(ND):
                    P.op("tensor", lambda e, dc=dc, tc=tc, p=p: e.matmul(
                        p[0:96, :], W[:, dc * 96:(dc + 1) * 96],
                        A[:, dc * 2048 + tc * 512:dc * 2048 + tc * 512 + 512],
                        start=(dc == 0), stop=(dc == ND - 1)),
                        rd=[bW, bxm[dc]], wr=[pb])
                P.op("vector", lambda e, tc=tc, p=p: e.tensor_copy(
                    Sc[0:96, tc * 512:tc * 512 + 512], p[0:96, :]),
                    rd=[pb], wr=[bSc[0], bSc[1]])

            dbg("mxdbl", Sc[0:96, 0:2048], [bSc[0], bSc[1]])
            bbcs = Buf()
            P.dma("scalar", bcsp, Sc[64:96, 0:2048], rd=[bSc[0], bSc[1]],
                  wr=[bbcs])

            bdel = [fresh([bBs[2 * j], bBs[2 * j + 1], bBa[j]])
                    for j in range(ND)]
            P.dma("sync", W[0:64, 1024:2048], di["mb_dtT"], wr=[bW])
            P.dma("sync", rowb[0:8, :],
                  di["mb_dtb"].rearrange("(a f) o -> (o a) f", a=8),
                  wr=[browb])
            pdt, pdtb = pacq()
            P.op("tensor", lambda e, p=pdt: e.transpose(
                p[:, 0:8], rowb[0:8, :], ident[0:8, 0:8]),
                rd=[browb, bident], wr=[pdtb])
            P.op("vector", lambda e, p=pdt: e.tensor_copy(
                cols[:, 240:248], p[:, 0:8]), rd=[pdtb], wr=[bcols])
            P.dma("sync", rowb[0:8, :],
                  di["mb_dvec"].rearrange("(a f) o -> (o a) f", a=8),
                  wr=[browb])
            pdv, pdvb = pacq()
            P.op("tensor", lambda e, p=pdv: e.transpose(
                p[:, 0:8], rowb[0:8, :], ident[0:8, 0:8]),
                rd=[browb, bident], wr=[pdvb])
            P.op("vector", lambda e, p=pdv: e.tensor_copy(
                cols[:, 248:256], p[:, 0:8]), rd=[pdvb], wr=[bcols])
            _one = KC(1.0)
            for ct in range(ND):
                for tc in range(4):
                    p, pb = pacq()
                    P.op("tensor", lambda e, ct=ct, tc=tc, p=p: e.matmul(
                        p[:], W[0:64, 1024 + ct * 128:1024 + ct * 128 + 128],
                        Sc[0:64, tc * 512:tc * 512 + 512],
                        start=True, stop=True),
                        rd=[bW, bSc[0], bSc[1]], wr=[pb])
                    P.op("vector", lambda e, ct=ct, tc=tc, p=p: e.tensor_copy(
                        Ba[:, ct * 2048 + tc * 512:ct * 2048 + tc * 512 + 512],
                        p[:]), rd=[pb], wr=[bdel[ct]])
                dr = Ba[:, ct * 2048:(ct + 1) * 2048]
                P.op("scalar", lambda e, dr=dr, ct=ct: e.activation(
                    Cn[:, 0:2048], dr, AF.Abs, bias=cols[:, 240 + ct:241 + ct]),
                    rd=[bdel[ct], bcols], wr=[bCns])
                P.op("scalar", lambda e: e.activation(
                    Cn[:, 0:2048], Cn[:, 0:2048], AF.Exp, scale=-1.0),
                    rd=[bCns], wr=[bCns])
                P.op("scalar", lambda e, _one=_one: e.activation(
                    Cn[:, 0:2048], Cn[:, 0:2048], AF.Ln, bias=_one),
                    rd=[bCns, bkconst], wr=[bCns])
                P.op("scalar", lambda e, dr=dr, ct=ct: e.activation(
                    W[:, 2048:4096], dr, AF.Relu,
                    bias=cols[:, 240 + ct:241 + ct]),
                    rd=[bdel[ct], bcols], wr=[bWs])
                P.op("vector", lambda e, dr=dr: e.tensor_tensor(
                    dr, Cn[:, 0:2048], W[:, 2048:4096], OP.add),
                    rd=[bCns, bWs], wr=[bdel[ct]])
                P.op("vector", lambda e, ct=ct, dr=dr: e.tensor_tensor(
                    Cn[:, 0:2048], dr,
                    A[:, ct * 2048:(ct + 1) * 2048], OP.mult),
                    rd=[bdel[ct], bxm[ct]], wr=[bCns])
                P.dma("scalar",
                      ospill[ts(ct, 256), :].rearrange(
                          "(p x) d -> p (x d)", p=128),
                      Cn[:, 0:2048], rd=[bCns], wr=[bosp])
                P.op("vector", lambda e, ct=ct: e.tensor_scalar(
                    A[:, ct * 2048:(ct + 1) * 2048],
                    A[:, ct * 2048:(ct + 1) * 2048],
                    cols[:, 248 + ct:249 + ct], None, OP.mult),
                    rd=[bcols], wr=[bxm[ct]])
            dbg("mdel0", Ba[:, 0:2048], [bdel[0]])
            dbg("my0", A[:, 0:2048], [bxm[0]])
            P.op("vector", lambda e: e.memset(W[:, 0:2048], 0.0), wr=[bWs])
            for ct in range(ND):
                P.op("vector", lambda e, ct=ct: e.tensor_tensor_scan(
                    W[:, 2048:4096], Ba[:, ct * 2048:(ct + 1) * 2048],
                    W[:, 0:2048], 0.0, OP.add, OP.add),
                    rd=[bdel[ct], bWs], wr=[bWs])
                P.dma("scalar",
                      uspill[ts(ct, 256), :].rearrange(
                          "(p x) d -> p (x d)", p=128),
                      W[:, 2048:4096], rd=[bWs], wr=[buspb])
            for ct in range(ND):
                P.dma("sync", Cn[:, 2112 + ct * 16:2112 + ct * 16 + 16],
                      di["mb_alog"][ts(ct), :], wr=[bCnk])
            P.op("scalar", lambda e: e.activation(
                Cn[:, 2112:2240], Cn[:, 2112:2240], AF.Exp),
                rd=[bCnk], wr=[bCnk])
            P.op("vector", lambda e: e.tensor_scalar(
                Cn[:, 2112:2240], Cn[:, 2112:2240], -1.0, None, OP.mult),
                rd=[bCnk], wr=[bCnk])

            for n in range(16):
                P.dma("sync", W[:, 4096:6144],
                      bcsp[n:n + 1, :].partition_broadcast(128),
                      rd=[bbcs], wr=[bWs])
                P.dma("sync", W[:, 6144:8192],
                      bcsp[16 + n:17 + n, :].partition_broadcast(128),
                      rd=[bbcs], wr=[bWs])
                for ct in range(ND):
                    acol = Cn[:, 2112 + ct * 16 + n:2112 + ct * 16 + n + 1]
                    P.op("scalar", lambda e, ct=ct, acol=acol: e.activation(
                        W[:, 0:2048], Ba[:, ct * 2048:(ct + 1) * 2048],
                        AF.Exp, scale=acol),
                        rd=[bdel[ct], bCnk], wr=[bWs])
                    P.dma("sync", W[:, 2048:4096],
                          ospill[ts(ct, 256), :].rearrange(
                              "(p x) d -> p (x d)", p=128),
                          rd=[bosp], wr=[bWs])
                    P.op("vector", lambda e, ct=ct: e.tensor_tensor(
                        W[:, 2048:4096], W[:, 2048:4096],
                        W[:, 4096:6144], OP.mult), rd=[bWs], wr=[bWs])
                    P.op("vector", lambda e: e.tensor_tensor_scan(
                        Cn[:, 0:2048], W[:, 0:2048], W[:, 2048:4096], 0.0,
                        OP.mult, OP.add), rd=[bWs], wr=[bCns])
                    P.op("vector", lambda e, ct=ct, n=n: e.tensor_copy(
                        Cn[:, 2368 + ct * 16 + n:2368 + ct * 16 + n + 1],
                        Cn[:, 2047:2048]), rd=[bCns], wr=[bCnk])
                    P.op("vector", lambda e: e.tensor_tensor(
                        Cn[:, 0:2048], Cn[:, 0:2048], W[:, 6144:8192],
                        OP.mult), rd=[bCns, bWs], wr=[bCns])
                    P.op("vector", lambda e, ct=ct: e.tensor_tensor(
                        A[:, ct * 2048:(ct + 1) * 2048],
                        A[:, ct * 2048:(ct + 1) * 2048],
                        Cn[:, 0:2048], OP.add), rd=[bCns], wr=[bxm[ct]])
            dbg("mys", A[:, 0:2048], [bxm[0]])
            dbg("mhend", Cn[:, 2368:2496], [bCnk])
            for ct in range(ND):
                P.dma("scalar", cc0_in[ts(ct), :],
                      Cn[:, 2368 + ct * 16:2368 + ct * 16 + 16],
                      rd=[bCnk], wr=[bh0_i])
            P.op("gpsimd", lambda e: e.collective_compute(
                "AllGather", OP.bypass, replica_groups=groups,
                ins=[cc0_in.opt()], outs=[cc0_out.opt()]),
                rd=[bh0_i], wr=[bh0_o])
            for ct in range(ND):
                P.dma("sync", Cn[:, 2240 + ct * 16:2240 + ct * 16 + 16],
                      cc0_out[ts(ct), :], rd=[bh0_o], wr=[bCnk])
            P.op("vector", lambda e: e.tensor_scalar(
                Cn[:, 2240:2368], Cn[:, 2240:2368], modd_c, None, OP.mult),
                rd=[bcols, bCnk], wr=[bCnk])
            for n in range(16):
                P.dma("sync", W[:, 6144:8192],
                      bcsp[16 + n:17 + n, :].partition_broadcast(128),
                      rd=[bbcs], wr=[bWs])
                for ct in range(ND):
                    P.dma("sync", W[:, 0:2048],
                          uspill[ts(ct, 256), :].rearrange(
                              "(p x) d -> p (x d)", p=128),
                          rd=[buspb], wr=[bWs])
                    acol = Cn[:, 2112 + ct * 16 + n:2112 + ct * 16 + n + 1]
                    P.op("scalar", lambda e, acol=acol: e.activation(
                        W[:, 2048:4096], W[:, 0:2048], AF.Exp, scale=acol),
                        rd=[bWs, bCnk], wr=[bWs])
                    h0c = Cn[:, 2240 + ct * 16 + n:2240 + ct * 16 + n + 1]
                    P.op("vector", lambda e, h0c=h0c: e.scalar_tensor_tensor(
                        W[:, 0:2048], W[:, 2048:4096], h0c, W[:, 6144:8192],
                        OP.mult, OP.mult), rd=[bWs, bCnk], wr=[bWs])
                    P.op("vector", lambda e, ct=ct: e.tensor_tensor(
                        A[:, ct * 2048:(ct + 1) * 2048],
                        A[:, ct * 2048:(ct + 1) * 2048],
                        W[:, 0:2048], OP.add), rd=[bWs], wr=[bxm[ct]])

            load_weight("mb_opT", csl=slice(0, 1024), rsl=slice(0, 1024))
            for tt in range(NT):
                for oc in range(2):
                    p, pb = pacq()
                    for dc in range(ND):
                        P.op("tensor", lambda e, dc=dc, tt=tt, oc=oc, p=p:
                             e.matmul(
                                 p[:],
                                 A[:, dc * 2048 + tt * 128:
                                   dc * 2048 + tt * 128 + 128],
                                 W[:, dc * 1024 + oc * 512:
                                   dc * 1024 + oc * 512 + 512],
                                 start=(dc == 0), stop=(dc == ND - 1)),
                             rd=[bW, bxm[dc]], wr=[pb])
                    P.op("vector", lambda e, oc=oc, p=p: e.tensor_copy(
                        Sc[:, oc * 512:oc * 512 + 512], p[:]), rd=[pb],
                        wr=[bSc[0]])
                P.dma("scalar", ospill[ts(tt), :], Sc[:, 0:1024],
                      rd=[bSc[0]], wr=[bosp])
            bzc = []
            for ct in range(ND):
                bzc.append(fresh([bdel[ct]]))
                P.dma("sync", Ba[:, ct * 2048:(ct + 1) * 2048],
                      tspill[ts(ct, 256), :].rearrange(
                          "(p x) d -> p (x d)", p=128),
                      rd=[btsp], wr=[bzc[ct]])
            load_weight("mb_opT", csl=slice(0, 1024), rsl=slice(1024, 2048))
            for tt in range(NT):
                bA[tt] = fresh([bxm[tt // 2], bA[tt]])
                P.dma("sync", A_t(tt), xspill[ts(tt), :], rd=[bxsp],
                      wr=[bA[tt]])
                P.dma("sync", Sc[:, 1024:2048], ospill[ts(tt), :],
                      rd=[bosp], wr=[bSc[1]])
                for oc in range(2):
                    p, pb = pacq()
                    for dc in range(ND):
                        P.op("tensor", lambda e, dc=dc, tt=tt, oc=oc, p=p:
                             e.matmul(
                                 p[:],
                                 Ba[:, dc * 2048 + tt * 128:
                                   dc * 2048 + tt * 128 + 128],
                                 W[:, dc * 1024 + oc * 512:
                                   dc * 1024 + oc * 512 + 512],
                                 start=(dc == 0), stop=(dc == ND - 1)),
                             rd=[bW, bzc[dc]], wr=[pb])
                    P.op("vector", lambda e, oc=oc, p=p: e.tensor_copy(
                        Sc[:, oc * 512:oc * 512 + 512], p[:]), rd=[pb],
                        wr=[bSc[0]])
                P.op("vector", lambda e, tt=tt: e.tensor_tensor(
                    A_t(tt), A_t(tt), Sc[:, 1024:2048], OP.add),
                    rd=[bSc[1]], wr=[bA[tt]])
                P.op("vector", lambda e, tt=tt: e.tensor_tensor(
                    A_t(tt), A_t(tt), Sc[:, 0:1024], OP.add),
                    rd=[bSc[0]], wr=[bA[tt]])
            for j in range(ND):
                bBa[j] = fresh([bzc[j], bBa[j]])
            for i in range(16):
                bBs[i] = fresh([bBa[i // 2]])
            ba_state[0] = "slots"

        emitters = {"l0a": emit_l0a, "l0f": emit_l0ffn, "l1f": emit_l1f,
                    "l0b": emit_l0b, "l1a": emit_l1a, "mb": emit_mb}
        for sl in sublayers:
            if sl in emitters:
                emitters[sl]()

        st = []
        for i in range(NT):
            st.append(P.dma("scalar", xout[ts(i), :], A_t(i), rd=[bA[i]]))
        P.waits("sync", st)

        with nc.Block() as block:
            P.replay(block)

    return nc



def _prep_inputs(I):
    c = np.ascontiguousarray
    base = {
        "l0a_wkT": c(I["L0_attn_a_Wk"].T), "l0a_wvT": c(I["L0_attn_a_Wv"].T),
        "l0a_wqT": c(I["L0_attn_a_Wq"].T), "l0a_wkpT": c(I["L0_attn_a_Wkp"].T),
        "l0a_wqpT": c(I["L0_attn_a_Wqp"].T), "l0a_woT": c(I["L0_attn_a_Wo"].T),
        "l0a_bk": I["L0_attn_a_bk"].reshape(1, D), "l0a_bv": I["L0_attn_a_bv"].reshape(1, D),
        "l0a_bq": I["L0_attn_a_bq"].reshape(1, D), "l0a_bkp": I["L0_attn_a_bkp"].reshape(1, D),
        "l0a_bqp": I["L0_attn_a_bqp"].reshape(1, D), "l0a_bo": I["L0_attn_a_bo"].reshape(1, D),
        "l0a_ps": I["L0_attn_a_phase_scale"].reshape(1, D),
        "l0a_lng": I["L0_attn_a_ln_g"].reshape(1, D),
        "l0a_lnb": I["L0_attn_a_ln_b"].reshape(1, D),
        "l0f_wT": c(I["L0_ffn_weight"].T),
        "l0b_wqT": c(I["L0_attn_b_Wq"].T), "l0b_wkT": c(I["L0_attn_b_Wk"].T),
        "l0b_wvT": c(I["L0_attn_b_Wv"].T), "l0b_woT": c(I["L0_attn_b_Wo"].T),
        "l0b_bq": I["L0_attn_b_bq"].reshape(1, D), "l0b_bk": I["L0_attn_b_bk"].reshape(1, D),
        "l0b_bv": I["L0_attn_b_bv"].reshape(1, D), "l0b_bo": I["L0_attn_b_bo"].reshape(1, D),
        "l1a_wq": c(I["L1_attn_a_wq"]), "l1a_wk": c(I["L1_attn_a_wk"]),
        "l1a_wv": c(I["L1_attn_a_wv"]),
        "l1a_lns": I["L1_attn_a_ln_scale"].reshape(1, D),
        "l1a_lnb": I["L1_attn_a_ln_bias"].reshape(1, D),
        "l1f_wT": c(I["L1_ffn_weight"].T),
        "l1f_gwT": c(I["L1_ffn_gate_w"].T),
        "l1f_uT": c(I["L1_ffn_lora_U"].transpose(2, 0, 1).reshape(D, 128)),
        "l1f_v": c(I["L1_ffn_lora_V"].transpose(0, 2, 1).reshape(128, D)),
        "mb_ipT": c(I["L1_attn_b_in_proj_w"].T),
        "mb_xpT": c(I["L1_attn_b_x_proj_w"].T),
        "mb_dtT": c(I["L1_attn_b_dt_proj_w"].T),
        "mb_opT": c(I["L1_attn_b_out_proj_w"].T),
        "mb_cx": c(I["L1_attn_b_conv_x_w"].reshape(D, 4)),
        "mb_cz": c(I["L1_attn_b_conv_z_w"].reshape(D, 4)),
        "mb_alog": c(I["L1_attn_b_A_log"]),
        "mb_dvec": I["L1_attn_b_D"].reshape(D, 1),
        "mb_dtb": I["L1_attn_b_dt_proj_b"].reshape(D, 1),
    }
    base = {k: np.ascontiguousarray(v, dtype=np.float32) for k, v in base.items()}
    x = np.ascontiguousarray(I["L0_attn_a_x"], dtype=np.float32)
    in_maps = []
    for core in range(NCORES):
        b, h = core // 2, core % 2
        m = dict(base)
        m["xin"] = np.ascontiguousarray(x[b, h * T:(h + 1) * T, :])
        m["tpos"] = np.arange(h * T, (h + 1) * T, dtype=np.float32).reshape(T, 1)
        m["modd"] = np.full((1, 1), float(h), dtype=np.float32)
        in_maps.append(m)
    return in_maps


_NC_CACHE = {}


def kernel(**inputs):
    in_maps = _prep_inputs(inputs)
    if "nc" not in _NC_CACHE:
        _NC_CACHE["nc"] = build_program(NCORES)
    nc = _NC_CACHE["nc"]
    res = run_bass_kernel_spmd(nc, in_maps, core_ids=list(range(NCORES)))
    out = np.empty((B, S, D), dtype=np.float32)
    for core in range(NCORES):
        b, h = core // 2, core % 2
        out[b, h * T:(h + 1) * T, :] = res.results[core]["xout"]
    return out


# version 04
# speedup vs baseline: 1.3108x; 1.3108x over previous
import contextlib

import numpy as np

import concourse.bass as bass
import concourse.mybir as mybir
from concourse.bass_utils import run_bass_kernel_spmd

F32 = mybir.dt.float32
BF16 = mybir.dt.bfloat16
I32 = mybir.dt.int32
AF = mybir.ActivationFunctionType
OP = mybir.AluOpType
AX = mybir.AxisListType.X

NCORES = 8
B, S, D = 4, 4096, 1024
T = 2048
NT = T // 128
ND = D // 128
H, HD = 16, 64

ENGS = ("sync", "tensor", "vector", "scalar", "gpsimd")

SUBLAYERS = ("l0a", "l0f", "l0b", "l0f", "l1a", "l1f", "mb", "l1f")
DEBUG_TAPS = ()


def ts(i, sz=128):
    return slice(i * sz, (i + 1) * sz)


class Buf:
    __slots__ = ("w", "rs")

    def __init__(self):
        self.w = None
        self.rs = []


class Prog:
    def __init__(self, nc):
        self.nc = nc
        self.cmds = {e: [] for e in ENGS}
        self.sems = {}
        self.cnt = {}
        self.qwait = {}
        self.qgate = {}
        self.stk = None

    def sem(self, name):
        h = self.stk.enter_context(self.nc.semaphore(name))
        self.sems[name] = h
        self.cnt[name] = 0

    def _edges(self, eng, rd, wr, extra):
        toks = list(extra)
        for b in rd:
            toks.append(b.w)
        for b in wr:
            toks.append(b.w)
            toks.extend(b.rs)
        self.waits(eng, toks)

    def _done(self, rd, wr, tok):
        for b in rd:
            b.rs.append(tok)
        for b in wr:
            b.w = tok
            b.rs = []

    def dma(self, q, out, in_, rd=(), wr=(), extra=(), **kw):
        self._edges(q, rd, wr, extra)
        sem = "q_" + q
        w = self.qwait.get(sem, 0)
        if w > self.qgate.get(sem, 0):
            s0 = self.sems[sem]

            def g(e, s0=s0, w=w):
                e.wait_ge(s0, w)

            self.cmds[q].append(g)
            self.qgate[sem] = w
        self.cnt[sem] += 16
        v = self.cnt[sem]
        s = self.sems[sem]

        def f(e, out=out, in_=in_, s=s, kw=kw):
            e.dma_start(out=out, in_=in_, **kw).then_inc(s, 16)

        self.cmds[q].append(f)
        tok = (sem, v)
        self._done(rd, wr, tok)
        return tok

    def op(self, eng, fn, rd=(), wr=(), extra=()):
        self._edges(eng, rd, wr, extra)
        sem = "t_" + eng
        self.cnt[sem] += 1
        v = self.cnt[sem]
        s = self.sems[sem]

        def f(e, fn=fn, s=s):
            fn(e).then_inc(s, 1)

        self.cmds[eng].append(f)
        tok = (sem, v)
        self._done(rd, wr, tok)
        return tok

    def waits(self, eng, toks):
        best = {}
        for t in toks:
            if t is None:
                continue
            sem, v = t
            if sem.startswith("q_"):
                v = self.cnt[sem]
                if self.qwait.get(sem, 0) < v:
                    self.qwait[sem] = v
            if best.get(sem, -1) < v:
                best[sem] = v
        for sem in sorted(best):
            v = best[sem]
            s = self.sems[sem]

            def f(e, s=s, v=v):
                e.wait_ge(s, v)

            self.cmds[eng].append(f)

    def replay(self, block):
        for eng in ENGS:
            cmds = self.cmds[eng]

            def runner(e, cmds=cmds):
                for fn in cmds:
                    fn(e)

            getattr(block, eng)(runner)



def build_program(n_ranks=NCORES, sublayers=SUBLAYERS):
    nc = bass.Bass("TRN2", target_bir_lowering=False)
    di = {}

    def inp(name, shape):
        di[name] = nc.dram_tensor(name, shape, F32, kind="ExternalInput").ap()

    inp("xin", [T, D])
    inp("tpos", [T, 1])
    inp("modd", [1, 1])
    for w in ("l0a_wkT", "l0a_wvT", "l0a_wqT", "l0a_wkpT", "l0a_wqpT", "l0a_woT"):
        inp(w, [D, D])
    for v in ("l0a_bk", "l0a_bv", "l0a_bq", "l0a_bkp", "l0a_bqp", "l0a_bo",
              "l0a_ps", "l0a_lng", "l0a_lnb"):
        inp(v, [1, D])
    inp("l0f_wT", [D, D])
    for w in ("l0b_wqT", "l0b_wkT", "l0b_wvT", "l0b_woT"):
        inp(w, [D, D])
    for v in ("l0b_bq", "l0b_bk", "l0b_bv", "l0b_bo"):
        inp(v, [1, D])
    for w in ("l1a_wq", "l1a_wk", "l1a_wv"):
        inp(w, [D, D])
    inp("l1a_lns", [1, D])
    inp("l1a_lnb", [1, D])
    inp("l1f_wT", [D, D])
    inp("l1f_gwT", [D, 8])
    inp("l1f_uT", [D, 128])
    inp("l1f_v", [128, D])
    inp("mb_ipT", [D, 2048])
    inp("mb_xpT", [D, 96])
    inp("mb_dtT", [64, D])
    inp("mb_opT", [2048, D])
    inp("mb_cx", [D, 4])
    inp("mb_cz", [D, 4])
    inp("mb_alog", [D, 16])
    inp("mb_dvec", [D, 1])
    inp("mb_dtb", [D, 1])

    xout = nc.dram_tensor("xout", [T, D], F32, kind="ExternalOutput").ap()

    xspill = nc.dram_tensor("xspill", [T, D], F32).ap()
    tspill = nc.dram_tensor("tspill", [T, D], F32).ap()
    uspill = nc.dram_tensor("uspill", [T, D], F32).ap()
    ospill = nc.dram_tensor("ospill", [T, D], F32).ap()
    xnspill = nc.dram_tensor("xnspill", [T, D], F32).ap()
    denspill = nc.dram_tensor("denspill", [8, 2048], F32).ap()
    ccx_in = [nc.dram_tensor(f"ccx{j}_in", [D, T], F32).ap() for j in range(2)]
    cch_in = nc.dram_tensor("cch_in", [4, 2048], F32).ap()
    cch_out = nc.dram_tensor("cch_out", [8, 2048], F32).ap()
    cc0_in = nc.dram_tensor("cc0_in", [D, 16], F32).ap()
    bcsp = nc.dram_tensor("bcsp", [32, 2048], F32).ap()
    cc0_out = nc.dram_tensor("cc0_out", [2 * D, 16], F32).ap()
    ccx_out = [nc.dram_tensor(f"ccx{j}_out", [2 * D, T], F32).ap()
               for j in range(2)]
    groups = [[2 * i, 2 * i + 1] for i in range(max(1, n_ranks // 2))]

    P = Prog(nc)
    dbg_n = [0]

    def dbg(name, ap_fn, rd):
        if name not in DEBUG_TAPS:
            return
        shp = [ap_fn.shape[0], ap_fn.shape[1]]
        o = nc.dram_tensor("dbg_" + name + f"_{dbg_n[0]}", shp, F32,
                           kind="ExternalOutput").ap()
        dbg_n[0] += 1
        P.dma("scalar", o, ap_fn, rd=rd)

    with contextlib.ExitStack() as stk:
        P.stk = stk
        for s in ("q_sync", "q_scalar", "q_gpsimd", "q_vector",
                  "t_sync", "t_tensor", "t_vector", "t_scalar", "t_gpsimd"):
            P.sem(s)

        def sbuf(name, shape, dt=F32):
            return stk.enter_context(nc.sbuf_tensor(name, shape, dt))

        A = sbuf("A", [128, 16384])
        Ba = sbuf("Ba", [128, 16384])
        W = sbuf("W", [128, 8192])
        Sc = sbuf("Sc", [128, 2048])
        Cn = sbuf("Cn", [128, 3584])
        cols = sbuf("cols", [128, 512])
        ident = sbuf("ident", [128, 128])
        ident16 = sbuf("ident16", [128, 128], BF16)
        tri = sbuf("tri", [128, 128])
        rowb = sbuf("rowb", [128, 128])
        wq16 = sbuf("wq16", [128, 8192], BF16)
        aq16 = sbuf("aq16", [128, 1024], BF16)
        aqf16 = sbuf("aqf16", [128, 1024], BF16)
        rint = sbuf("rint", [128, 512], I32)

        psum = [stk.enter_context(nc.psum_tensor(f"p{i}", [128, 512], F32))
                for i in range(8)]
        pB = [Buf() for _ in range(8)]
        prr = [0]

        def pacq():
            i = prr[0]
            prr[0] = (prr[0] + 1) % 8
            return psum[i], pB[i]

        bA = [Buf() for _ in range(NT)]
        bAf = [Buf() for _ in range(ND)]
        bBa = [Buf() for _ in range(ND)]
        bW = Buf()
        bSc = [Buf() for _ in range(2)]
        bCn = [Buf() for _ in range(3)]
        bCn3 = Buf()
        bcols = Buf()
        bident = Buf()
        bident16 = Buf()
        btri = Buf()
        bwq = Buf()
        baq = Buf()
        baqf = Buf()
        brint = Buf()

        def A_t(i):
            return A[:, ts(i, 1024)]

        def A_f(j):
            return A[:, ts(j, 2048)]

        def Ba_t(j):
            return Ba[:, ts(j, 2048)]

        bBs = [Buf() for _ in range(16)]

        def Bs_t(i):
            return Ba[:, ts(i, 1024)]

        ba_state = ["slots"]

        def ba_ensure_slots():
            if ba_state[0] != "slots":
                ba_alias_fm2slots()

        def ba_ensure_fm():
            if ba_state[0] != "fm":
                ba_alias_slots2fm()

        def ba_alias_fm2slots():
            ba_state[0] = "slots"
            for i in range(16):
                j = i // 2
                rs = []
                if bBa[j].w is not None:
                    rs.append(bBa[j].w)
                rs.extend(bBa[j].rs)
                bBs[i].w = None
                bBs[i].rs = rs

        def ba_alias_slots2fm():
            ba_state[0] = "fm"
            for j in range(ND):
                rs = []
                for i in (2 * j, 2 * j + 1):
                    if bBs[i].w is not None:
                        rs.append(bBs[i].w)
                    rs.extend(bBs[i].rs)
                bBa[j].w = None
                bBa[j].rs = rs

        def Sc_t(i):
            return Sc[:, ts(i, 1024)]

        def Cn_t(i):
            return Cn[:, ts(i, 1024)]

        def a_alias_tm2fm():
            for j in range(ND):
                rs = []
                w = None
                for i in (2 * j, 2 * j + 1):
                    if bA[i].w is not None:
                        rs.append(bA[i].w)
                    rs.extend(bA[i].rs)
                bAf[j].w = None
                bAf[j].rs = rs

        def a_alias_fm2tm():
            for i in range(NT):
                j = i // 2
                rs = []
                if bAf[j].w is not None:
                    rs.append(bAf[j].w)
                rs.extend(bAf[j].rs)
                bA[i].w = None
                bA[i].rs = rs

        ones_c = nc.const_aps.tensor(1.0, [128, 128], F32)
        kconst = sbuf("kconst", [128, 16])
        bkconst = Buf()
        _kc_map = {}
        _kc_next = [0]

        def KC(v):
            v = float(v)
            if v not in _kc_map:
                j = _kc_next[0]
                _kc_next[0] += 1
                P.op("vector", lambda e, j=j, v=v: e.memset(
                    kconst[:, j:j + 1], v), wr=[bkconst])
                _kc_map[v] = kconst[:, j:j + 1]
            return _kc_map[v]
        P.op("gpsimd", lambda e: e.affine_select(
            ident[:], ones_c, pattern=[[1, 128]], compare_op=OP.is_equal,
            fill=0.0, base=0, channel_multiplier=-1), wr=[bident])
        P.op("vector", lambda e: e.tensor_copy(ident16[:], ident[:]),
             rd=[bident], wr=[bident16])
        P.op("vector", lambda e: e.memset(tri[:], 0.0), wr=[btri])
        for _blk in range(2):
            P.op("gpsimd", lambda e, _blk=_blk: e.affine_select(
                tri[ts(_blk, 64), ts(_blk, 64)], ones_c[0:64, 0:64],
                pattern=[[1, 64]], compare_op=OP.is_ge, fill=0.0, base=0,
                channel_multiplier=-1), rd=[btri], wr=[btri])

        def transpose_f32(src_ap, dst_ap, rd, wr):
            p, pb = pacq()
            P.op("tensor", lambda e, p=p: e.transpose(
                p[:, 0:128], src_ap, ident[:]), rd=[*rd, bident], wr=[pb])
            return P.op("vector", lambda e, p=p: e.tensor_copy(
                dst_ap, p[:, 0:128]), rd=[pb], wr=list(wr))

        def transpose_bf16(src_ap, dst_ap, rd, wr):
            p, pb = pacq()
            P.op("tensor", lambda e, p=p: e.transpose(
                p[:].bitcast(BF16)[:, 0:128], src_ap, ident16[:]),
                rd=[*rd, bident16], wr=[pb])
            return P.op("vector", lambda e, p=p: e.tensor_copy(
                dst_ap, p[:].bitcast(BF16)[:, 0:128]), rd=[pb], wr=list(wr))

        for i in range(NT):
            P.dma("sync", A_t(i), di["xin"][ts(i), :], wr=[bA[i]])

        browb = Buf()

        def load_cols(dst_cols_ap, name):
            P.dma("sync", rowb[0:8, :],
                  di[name].rearrange("o (a f) -> (o a) f", a=8), wr=[browb])
            p, pb = pacq()
            P.op("tensor", lambda e, p=p: e.transpose(
                p[:, 0:8], rowb[0:8, :], ident[0:8, 0:8]),
                rd=[browb, bident], wr=[pb])
            return P.op("vector", lambda e, p=p: e.tensor_copy(
                dst_cols_ap, p[:, 0:8]), rd=[pb], wr=[bcols])

        def bcast_row(slot, name):
            return P.dma("sync", Cn_t(slot),
                         di[name].partition_broadcast(128), wr=[bCn[slot]])

        def load_weight(name, csl=None, woff=0, rsl=None):
            src = di[name]
            if rsl is None:
                rsl = slice(0, src.shape[0])
            nk = (rsl.stop - rsl.start) // 128
            if csl is None:
                csl = slice(0, src.shape[1])
            ncols = csl.stop - csl.start
            for kc in range(nk):
                r0 = rsl.start + kc * 128
                P.dma("sync", W[:, woff + kc * ncols:woff + (kc + 1) * ncols],
                      src[r0:r0 + 128, csl], wr=[bW])

        def Wk_(kc, ncols=D):
            return W[:, kc * ncols:(kc + 1) * ncols]

        def emit_ln(eps=1e-5, fold2=None):
            for i in range(NT):
                P.op("vector", lambda e, i=i: e.reduce_sum(
                    cols[:, i:i + 1], A_t(i), axis=AX), rd=[bA[i]], wr=[bcols])
                P.op("scalar", lambda e, i=i: e.activation(
                    Sc_t(0), A_t(i), AF.Square, accum_out=cols[:, 16 + i:17 + i]),
                    rd=[bA[i]], wr=[bSc[0], bcols])
            P.op("vector", lambda e: e.tensor_scalar(
                cols[:, 32:48], cols[:, 0:16], 1.0 / D, None, OP.mult),
                rd=[bcols], wr=[bcols])
            P.op("vector", lambda e: e.tensor_scalar(
                cols[:, 112:128], cols[:, 16:32], 1.0 / D, None, OP.mult),
                rd=[bcols], wr=[bcols])
            P.op("vector", lambda e: e.tensor_tensor(
                cols[:, 96:112], cols[:, 32:48], cols[:, 32:48], OP.mult),
                rd=[bcols], wr=[bcols])
            P.op("vector", lambda e: e.tensor_tensor(
                cols[:, 48:64], cols[:, 112:128], cols[:, 96:112], OP.subtract),
                rd=[bcols], wr=[bcols])
            _b = KC(eps)
            P.op("scalar", lambda e, _b=_b: e.activation(
                cols[:, 96:112], cols[:, 48:64], AF.Ln, bias=_b),
                rd=[bcols, bkconst], wr=[bcols])
            if fold2 is None:
                P.op("scalar", lambda e: e.activation(
                    cols[:, 64:80], cols[:, 96:112], AF.Exp, scale=-0.5),
                    rd=[bcols], wr=[bcols])
            else:
                P.op("scalar", lambda e: e.activation(
                    cols[:, 128:144], cols[:, 96:112], AF.Exp, scale=-0.5),
                    rd=[bcols], wr=[bcols])
                P.op("vector", lambda e: e.tensor_tensor(
                    cols[:, 144:160], cols[:, 128:144], cols[:, 128:144],
                    OP.mult), rd=[bcols], wr=[bcols])
                P.op("vector", lambda e: e.tensor_tensor(
                    cols[:, 160:176], cols[:, 48:64], cols[:, 144:160],
                    OP.mult), rd=[bcols], wr=[bcols])
                _b2 = KC(fold2)
                P.op("scalar", lambda e, _b2=_b2: e.activation(
                    cols[:, 176:192], cols[:, 160:176], AF.Ln,
                    bias=_b2), rd=[bcols, bkconst], wr=[bcols])
                P.op("scalar", lambda e: e.activation(
                    cols[:, 144:160], cols[:, 176:192], AF.Exp, scale=-0.5),
                    rd=[bcols], wr=[bcols])
                P.op("vector", lambda e: e.tensor_tensor(
                    cols[:, 64:80], cols[:, 128:144], cols[:, 144:160],
                    OP.mult), rd=[bcols], wr=[bcols])
            P.op("vector", lambda e: e.tensor_tensor(
                cols[:, 96:112], cols[:, 32:48], cols[:, 64:80], OP.mult),
                rd=[bcols], wr=[bcols])
            P.op("vector", lambda e: e.tensor_scalar(
                cols[:, 80:96], cols[:, 96:112], -1.0, None, OP.mult),
                rd=[bcols], wr=[bcols])

        def ln_s(i):
            return cols[:, 64 + i:65 + i]

        def ln_b(i):
            return cols[:, 80 + i:81 + i]

        def emit_l0a():
            ba_ensure_fm()
            emit_ln()
            for i in range(NT):
                P.op("scalar", lambda e, i=i: e.activation(
                    Sc_t(0), A_t(i), AF.Identity, bias=ln_b(i), scale=ln_s(i)),
                    rd=[bA[i], bcols], wr=[bSc[0]])
                for kc in range(ND):
                    transpose_f32(Sc_t(0)[:, ts(kc)], Ba_t(kc)[:, ts(i)],
                                  rd=[bSc[0]], wr=[bBa[kc]])
                P.dma("scalar", xspill[ts(i), :], A_t(i), rd=[bA[i]])
            a_alias_tm2fm()
            dbg("xnfm0", Ba_t(0)[:, 0:1024], [bBa[0]])

            bias_cols = cols[:, 448:456]

            def proj_to_fm(dst_f, dst_b, bias_name):
                load_cols(bias_cols, bias_name)
                for ec in range(ND):
                    for tc in range(4):
                        p, pb = pacq()
                        for dc in range(ND):
                            P.op("tensor", lambda e, dc=dc, ec=ec, tc=tc, p=p:
                                 e.matmul(p[:], Wk_(dc)[:, ts(ec)],
                                          Ba_t(dc)[:, ts(tc, 512)],
                                          start=(dc == 0), stop=(dc == ND - 1)),
                                 rd=[bW, bBa[dc]], wr=[pb])
                        P.op("scalar", lambda e, ec=ec, tc=tc, p=p:
                             e.activation(dst_f(ec)[:, ts(tc, 512)], p[:],
                                          AF.Identity,
                                          bias=bias_cols[:, ec:ec + 1]),
                             rd=[pb, bcols], wr=[dst_b[ec]])

            def phase_to_hbm(dst_dram, bias_name):
                bcast_row(0, bias_name)
                for i in range(NT):
                    for oc in range(2):
                        p, pb = pacq()
                        for ec in range(ND):
                            P.op("tensor", lambda e, ec=ec, i=i, oc=oc, p=p:
                                 e.matmul(p[:], A_f(ec)[:, ts(i)],
                                          Wk_(ec)[:, ts(oc, 512)],
                                          start=(ec == 0), stop=(ec == ND - 1)),
                                 rd=[bW, bAf[ec]], wr=[pb])
                        P.op("vector", lambda e, oc=oc, p=p: e.tensor_tensor(
                            Sc_t(0)[:, ts(oc, 512)], p[:],
                            Cn_t(0)[:, ts(oc, 512)], OP.add),
                            rd=[pb, bCn[0]], wr=[bSc[0]])
                    P.op("scalar", lambda e: e.activation(
                        Sc_t(1), Sc_t(0), AF.Tanh), rd=[bSc[0]], wr=[bSc[1]])
                    P.op("vector", lambda e: e.tensor_tensor(
                        Sc_t(0), Sc_t(1), Cn_t(1), OP.mult),
                        rd=[bSc[1], bCn[1]], wr=[bSc[0]])
                    P.dma("scalar", dst_dram[ts(i), :], Sc_t(0), rd=[bSc[0]])

            bcast_row(1, "l0a_ps")
            load_weight("l0a_wkT")
            proj_to_fm(A_f, bAf, "l0a_bk")
            dbg("keyfm0", A_f(0)[:, 0:1024], [bAf[0]])
            load_weight("l0a_wkpT")
            phase_to_hbm(tspill, "l0a_bkp")
            load_weight("l0a_wqT")
            proj_to_fm(A_f, bAf, "l0a_bq")
            load_weight("l0a_wqpT")
            phase_to_hbm(uspill, "l0a_bqp")
            a_alias_fm2tm()
            load_weight("l0a_wvT")
            bcast_row(0, "l0a_bv")
            for i in range(NT):
                for oc in range(2):
                    p, pb = pacq()
                    for dc in range(ND):
                        P.op("tensor", lambda e, dc=dc, i=i, oc=oc, p=p:
                             e.matmul(p[:], Ba_t(dc)[:, ts(i)],
                                      Wk_(dc)[:, ts(oc, 512)],
                                      start=(dc == 0), stop=(dc == ND - 1)),
                             rd=[bW, bBa[dc]], wr=[pb])
                    P.op("vector", lambda e, i=i, oc=oc, p=p: e.tensor_tensor(
                        A_t(i)[:, ts(oc, 512)], p[:], Cn_t(0)[:, ts(oc, 512)],
                        OP.add), rd=[pb, bCn[0]], wr=[bA[i]])

            ba_alias_fm2slots()
            bcast_row(1, "l0a_lng")
            bcast_row(2, "l0a_lnb")
            load_weight("l0a_woT")
            bcast_row(0, "l0a_bo")
            for i in range(NT):
                P.dma("sync", Bs_t(1), tspill[ts(i), :], wr=[bBs[1]])
                P.dma("sync", Bs_t(2), uspill[ts(i), :], wr=[bBs[2]])
                P.dma("sync", Bs_t(9), xspill[ts(i), :], wr=[bBs[9]])
                if i == 0:
                    dbg("kp0", Bs_t(1), [bBs[1]])
                    dbg("qp0", Bs_t(2), [bBs[2]])
                    dbg("val0", A_t(0), [bA[0]])
                P.op("scalar", lambda e: e.activation(
                    Bs_t(3), Bs_t(1), AF.Sin, scale=0.5), rd=[bBs[1]],
                    wr=[bBs[3]])
                P.op("scalar", lambda e: e.activation(
                    Bs_t(3), Bs_t(3), AF.Square), rd=[bBs[3]], wr=[bBs[3]])
                P.op("vector", lambda e: e.tensor_scalar(
                    Bs_t(3), Bs_t(3), -2.0, 1.0, OP.mult, OP.add),
                    rd=[bBs[3]], wr=[bBs[3]])
                P.op("scalar", lambda e: e.activation(
                    Bs_t(4), Bs_t(1), AF.Sin), rd=[bBs[1]], wr=[bBs[4]])
                P.op("vector", lambda e, i=i: e.tensor_tensor(
                    Bs_t(3), A_t(i), Bs_t(3), OP.mult), rd=[bA[i], bBs[3]],
                    wr=[bBs[3]])
                P.op("vector", lambda e, i=i: e.tensor_tensor(
                    Bs_t(4), A_t(i), Bs_t(4), OP.mult), rd=[bA[i], bBs[4]],
                    wr=[bBs[4]])
                for (sslot, dslot) in ((3, 5), (4, 6)):
                    for oc in range(2):
                        p, pb = pacq()
                        P.op("tensor", lambda e, sslot=sslot, oc=oc, p=p:
                             e.matmul(p[:], tri[:],
                                      Bs_t(sslot)[:, ts(oc, 512)],
                                      start=True, stop=True),
                             rd=[btri, bBs[sslot]], wr=[pb])
                        P.op("vector", lambda e, dslot=dslot, oc=oc, p=p:
                             e.tensor_copy(Bs_t(dslot)[:, ts(oc, 512)], p[:]),
                             rd=[pb], wr=[bBs[dslot]])
                if i == 0:
                    dbg("memr0", Bs_t(5), [bBs[5]])
                P.op("scalar", lambda e: e.activation(
                    Bs_t(3), Bs_t(2), AF.Sin, scale=0.5), rd=[bBs[2], bBs[3]],
                    wr=[bBs[3]])
                P.op("scalar", lambda e: e.activation(
                    Bs_t(3), Bs_t(3), AF.Square), rd=[bBs[3]], wr=[bBs[3]])
                P.op("vector", lambda e: e.tensor_scalar(
                    Bs_t(3), Bs_t(3), -2.0, 1.0, OP.mult, OP.add),
                    rd=[bBs[3]], wr=[bBs[3]])
                P.op("scalar", lambda e: e.activation(
                    Bs_t(4), Bs_t(2), AF.Sin), rd=[bBs[2], bBs[4]], wr=[bBs[4]])
                P.op("vector", lambda e: e.tensor_tensor(
                    Bs_t(3), Bs_t(5), Bs_t(3), OP.mult), rd=[bBs[5], bBs[3]],
                    wr=[bBs[3]])
                P.op("vector", lambda e: e.tensor_tensor(
                    Bs_t(4), Bs_t(6), Bs_t(4), OP.mult), rd=[bBs[6], bBs[4]],
                    wr=[bBs[4]])
                P.op("vector", lambda e: e.tensor_tensor(
                    Bs_t(7), Bs_t(3), Bs_t(4), OP.add), rd=[bBs[3], bBs[4]],
                    wr=[bBs[7]])
                if i == 0:
                    dbg("retr0", Bs_t(7), [bBs[7]])
                P.op("vector", lambda e: e.reduce_sum(
                    cols[:, 200:201], Bs_t(7), axis=AX), rd=[bBs[7]],
                    wr=[bcols])
                P.op("scalar", lambda e: e.activation(
                    Sc_t(0), Bs_t(7), AF.Square, accum_out=cols[:, 201:202]),
                    rd=[bBs[7]], wr=[bSc[0], bcols])
                P.op("vector", lambda e: e.tensor_scalar(
                    cols[:, 202:203], cols[:, 200:201], 1.0 / D, None,
                    OP.mult), rd=[bcols], wr=[bcols])
                P.op("vector", lambda e: e.tensor_scalar(
                    cols[:, 203:204], cols[:, 201:202], 1.0 / D, None,
                    OP.mult), rd=[bcols], wr=[bcols])
                P.op("vector", lambda e: e.tensor_tensor(
                    cols[:, 204:205], cols[:, 202:203], cols[:, 202:203],
                    OP.mult), rd=[bcols], wr=[bcols])
                P.op("vector", lambda e: e.tensor_tensor(
                    cols[:, 204:205], cols[:, 203:204], cols[:, 204:205],
                    OP.subtract), rd=[bcols], wr=[bcols])
                _b3 = KC(1e-5 * D)
                P.op("scalar", lambda e, _b3=_b3: e.activation(
                    cols[:, 205:206], cols[:, 204:205], AF.Ln, bias=_b3),
                    rd=[bcols, bkconst], wr=[bcols])
                P.op("scalar", lambda e: e.activation(
                    cols[:, 205:206], cols[:, 205:206], AF.Exp, scale=-0.5),
                    rd=[bcols], wr=[bcols])
                P.op("vector", lambda e: e.tensor_tensor(
                    cols[:, 206:207], cols[:, 202:203], cols[:, 205:206],
                    OP.mult), rd=[bcols], wr=[bcols])
                P.op("vector", lambda e: e.tensor_scalar(
                    cols[:, 206:207], cols[:, 206:207], -1.0, None, OP.mult),
                    rd=[bcols], wr=[bcols])
                P.op("scalar", lambda e: e.activation(
                    Bs_t(7), Bs_t(7), AF.Identity, bias=cols[:, 206:207],
                    scale=cols[:, 205:206]), rd=[bBs[7], bcols], wr=[bBs[7]])
                P.op("vector", lambda e: e.tensor_tensor(
                    Bs_t(7), Bs_t(7), Cn_t(1), OP.mult), rd=[bBs[7], bCn[1]],
                    wr=[bBs[7]])
                P.op("vector", lambda e: e.tensor_tensor(
                    Bs_t(7), Bs_t(7), Cn_t(2), OP.add), rd=[bBs[7], bCn[2]],
                    wr=[bBs[7]])
                if i == 0:
                    dbg("lnr0", Bs_t(7), [bBs[7]])
                for kc in range(ND):
                    transpose_f32(Bs_t(7)[:, ts(kc)], Bs_t(0)[:, ts(kc)],
                                  rd=[bBs[7]], wr=[bBs[0]])
                for oc in range(2):
                    p, pb = pacq()
                    for dc in range(ND):
                        P.op("tensor", lambda e, dc=dc, oc=oc, p=p: e.matmul(
                            p[:], Bs_t(0)[:, ts(dc)], Wk_(dc)[:, ts(oc, 512)],
                            start=(dc == 0), stop=(dc == ND - 1)),
                            rd=[bW, bBs[0]], wr=[pb])
                    P.op("scalar", lambda e, oc=oc, p=p: e.activation(
                        Bs_t(8)[:, ts(oc, 512)], p[:], AF.Identity),
                        rd=[pb], wr=[bBs[8]])
                P.op("vector", lambda e: e.tensor_tensor(
                    Bs_t(8), Bs_t(8), Cn_t(0), OP.add), rd=[bBs[8], bCn[0]],
                    wr=[bBs[8]])
                P.op("scalar", lambda e, i=i: e.activation(
                    Bs_t(1), Bs_t(9), AF.Identity, bias=ln_b(i),
                    scale=ln_s(i)), rd=[bBs[9], bcols], wr=[bBs[1]])
                P.op("vector", lambda e: e.tensor_tensor(
                    Bs_t(8), Bs_t(8), Bs_t(1), OP.add), rd=[bBs[8], bBs[1]],
                    wr=[bBs[8]])
                P.op("vector", lambda e, i=i: e.tensor_tensor(
                    A_t(i), Bs_t(9), Bs_t(8), OP.add), rd=[bBs[9], bBs[8]],
                    wr=[bA[i]])

        ffn_ready = [False]
        ffn_s127 = cols[:, 496:497]

        def emit_l0ffn_setup():
            load_weight("l0f_wT")
            for kc in range(ND):
                P.op("scalar", lambda e, kc=kc: e.activation(
                    Sc_t(0), Wk_(kc), AF.Abs,
                    accum_out=cols[:, 480 + kc:481 + kc]),
                    rd=[bW], wr=[bSc[0], bcols])
            P.op("vector", lambda e: e.reduce_sum(
                cols[:, 488:489], cols[:, 480:488], axis=AX), rd=[bcols],
                wr=[bcols])
            p, pb = pacq()
            P.op("tensor", lambda e, p=p: e.matmul(
                p[0:1, 0:1], cols[:, 488:489], ones_c[:, 0:1],
                start=True, stop=True), rd=[bcols], wr=[pb])
            P.op("vector", lambda e, p=p: e.tensor_copy(
                cols[0:1, 489:490], p[0:1, 0:1]), rd=[pb], wr=[bcols])
            _b4 = KC(1e-8)
            P.op("scalar", lambda e, _b4=_b4: e.activation(
                cols[0:1, 490:491], cols[0:1, 489:490], AF.Identity,
                scale=1.0 / float(D * D), bias=_b4[0:1, :]),
                rd=[bcols, bkconst], wr=[bcols])
            P.op("vector", lambda e: e.reciprocal(
                cols[0:1, 491:492], cols[0:1, 490:491]), rd=[bcols], wr=[bcols])
            P.op("scalar", lambda e: e.activation(
                cols[0:1, 492:493], cols[0:1, 490:491], AF.Identity,
                scale=1.0 / 127.0), rd=[bcols], wr=[bcols])
            p2, pb2 = pacq()
            P.op("tensor", lambda e, p=p2: e.matmul(
                p[:, 0:2], ones_c[0:1, :], cols[0:1, 491:493],
                start=True, stop=True), rd=[bcols], wr=[pb2])
            P.op("vector", lambda e, p=p2: e.tensor_copy(
                cols[:, 493:494], p[:, 0:1]), rd=[pb2], wr=[bcols])
            P.op("vector", lambda e, p=p2: e.tensor_copy(
                ffn_s127, p[:, 1:2]), rd=[pb2], wr=[bcols])
            for kc in range(ND):
                P.op("vector", lambda e, kc=kc: e.tensor_scalar(
                    Bs_t(10), Wk_(kc), cols[:, 493:494], None, OP.mult),
                    rd=[bW, bcols], wr=[bBs[10]])
                P.op("vector", lambda e: e.tensor_scalar(
                    Bs_t(11), Bs_t(10), 0.5, None, OP.is_ge), rd=[bBs[10]],
                    wr=[bBs[11]])
                P.op("vector", lambda e: e.tensor_scalar(
                    Bs_t(12), Bs_t(10), -0.5, None, OP.is_le), rd=[bBs[10]],
                    wr=[bBs[12]])
                P.op("vector", lambda e, kc=kc: e.tensor_tensor(
                    wq16[:, ts(kc, 1024)], Bs_t(11), Bs_t(12), OP.subtract),
                    rd=[bBs[11], bBs[12]], wr=[bwq])
            ffn_ready[0] = True

        def emit_l0ffn():
            ba_ensure_slots()
            if not ffn_ready[0]:
                emit_l0ffn_setup()
            emit_ln()
            for i in range(NT):
                c0 = 208 + 8 * (i % 2)
                P.op("scalar", lambda e, i=i: e.activation(
                    Bs_t(0), A_t(i), AF.Identity, bias=ln_b(i), scale=ln_s(i)),
                    rd=[bA[i], bcols], wr=[bBs[0]])
                P.op("vector", lambda e, c0=c0: e.reduce_max(
                    cols[:, c0:c0 + 1], Bs_t(0), axis=AX,
                    apply_absolute_value=True), rd=[bBs[0]], wr=[bcols])
                P.op("vector", lambda e, c0=c0: e.tensor_scalar(
                    cols[:, c0 + 1:c0 + 2], cols[:, c0:c0 + 1], 1e-8, None,
                    OP.add), rd=[bcols], wr=[bcols])
                P.op("vector", lambda e, c0=c0: e.reciprocal(
                    cols[:, c0 + 2:c0 + 3], cols[:, c0 + 1:c0 + 2]),
                    rd=[bcols], wr=[bcols])
                P.op("vector", lambda e, c0=c0: e.tensor_scalar(
                    cols[:, c0 + 3:c0 + 4], cols[:, c0 + 2:c0 + 3], 127.0,
                    None, OP.mult), rd=[bcols], wr=[bcols])
                P.op("scalar", lambda e, c0=c0: e.activation(
                    Bs_t(1), Bs_t(0), AF.Identity,
                    scale=cols[:, c0 + 3:c0 + 4]), rd=[bBs[0], bcols],
                    wr=[bBs[1]])
                for _h in range(2):
                    P.op("vector", lambda e, _h=_h: e.tensor_copy(
                        rint[:], Bs_t(1)[:, ts(_h, 512)]),
                        rd=[bBs[1]], wr=[brint])
                    P.op("vector", lambda e, _h=_h: e.tensor_copy(
                        Bs_t(2)[:, ts(_h, 512)], rint[:]),
                        rd=[brint], wr=[bBs[2]])
                P.op("vector", lambda e: e.tensor_tensor(
                    Bs_t(3), Bs_t(1), Bs_t(2), OP.subtract),
                    rd=[bBs[1], bBs[2]], wr=[bBs[3]])
                P.op("vector", lambda e: e.tensor_scalar(
                    Bs_t(4), Bs_t(3), 0.5, None, OP.is_ge), rd=[bBs[3]],
                    wr=[bBs[4]])
                P.op("vector", lambda e: e.tensor_scalar(
                    Bs_t(3), Bs_t(3), -0.5, None, OP.is_le), rd=[bBs[3]],
                    wr=[bBs[3]])
                P.op("vector", lambda e: e.tensor_tensor(
                    Bs_t(4), Bs_t(4), Bs_t(3), OP.subtract),
                    rd=[bBs[4], bBs[3]], wr=[bBs[4]])
                P.op("vector", lambda e: e.tensor_tensor(
                    aq16[:], Bs_t(2), Bs_t(4), OP.add), rd=[bBs[2], bBs[4]],
                    wr=[baq])
                for kc in range(ND):
                    transpose_bf16(aq16[:, ts(kc)], aqf16[:, ts(kc)],
                                   rd=[baq], wr=[baqf])
                P.op("vector", lambda e, c0=c0: e.tensor_tensor(
                    cols[:, c0 + 4:c0 + 5], cols[:, c0 + 1:c0 + 2], ffn_s127,
                    OP.mult), rd=[bcols], wr=[bcols])
                for oc in range(2):
                    p, pb = pacq()
                    for kc in range(ND):
                        P.op("tensor", lambda e, kc=kc, oc=oc, p=p: e.matmul(
                            p[:], aqf16[:, ts(kc)],
                            wq16[:, ts(kc, 1024)][:, ts(oc, 512)],
                            start=(kc == 0), stop=(kc == ND - 1)),
                            rd=[baqf, bwq], wr=[pb])
                    P.op("scalar", lambda e, oc=oc, p=p, c0=c0: e.activation(
                        Bs_t(1)[:, ts(oc, 512)], p[:], AF.Identity,
                        scale=cols[:, c0 + 4:c0 + 5]), rd=[pb, bcols],
                        wr=[bBs[1]])
                P.op("vector", lambda e, i=i: e.tensor_tensor(
                    A_t(i), A_t(i), Bs_t(1), OP.add), rd=[bBs[1], bA[i]],
                    wr=[bA[i]])

        def emit_l1f():
            ba_ensure_slots()
            load_weight("l1f_wT")
            P.dma("sync", rowb[:, 0:64].rearrange("p (a g) -> p a g", a=8),
                  di["l1f_gwT"].rearrange("(a p) g -> p a g", p=128),
                  wr=[browb])
            P.dma("sync", Cn_t(0), di["l1f_v"], wr=[bCn[0]])
            P.dma("sync", Cn_t(1).rearrange("p (a r) -> p a r", a=8),
                  di["l1f_uT"].rearrange("(a p) r -> p a r", p=128),
                  wr=[bCn[1]])
            emit_ln()
            for i in range(NT):
                P.op("scalar", lambda e, i=i: e.activation(
                    Bs_t(0), A_t(i), AF.Identity, bias=ln_b(i), scale=ln_s(i)),
                    rd=[bA[i], bcols], wr=[bBs[0]])
                for kc in range(ND):
                    transpose_f32(Bs_t(0)[:, ts(kc)], Bs_t(1)[:, ts(kc)],
                                  rd=[bBs[0]], wr=[bBs[1]])
                pg, pgb = pacq()
                for dc in range(ND):
                    P.op("tensor", lambda e, dc=dc, p=pg: e.matmul(
                        p[:, 0:8], Bs_t(1)[:, ts(dc)],
                        rowb[:, dc * 8:(dc + 1) * 8],
                        start=(dc == 0), stop=(dc == ND - 1)),
                        rd=[bBs[1], browb], wr=[pgb])
                pt, ptb = pacq()
                for dc in range(ND):
                    P.op("tensor", lambda e, dc=dc, p=pt: e.matmul(
                        p[:, 0:128], Bs_t(1)[:, ts(dc)],
                        Cn_t(1)[:, ts(dc)],
                        start=(dc == 0), stop=(dc == ND - 1)),
                        rd=[bBs[1], bCn[1]], wr=[ptb])
                P.op("scalar", lambda e, p=pg: e.activation(
                    Bs_t(2)[:, 0:8], p[:, 0:8], AF.Exp), rd=[pgb], wr=[bBs[2]])
                P.op("vector", lambda e: e.reduce_sum(
                    cols[:, 220:221], Bs_t(2)[:, 0:8], axis=AX),
                    rd=[bBs[2]], wr=[bcols])
                P.op("vector", lambda e: e.reciprocal(
                    cols[:, 221:222], cols[:, 220:221]), rd=[bcols], wr=[bcols])
                P.op("scalar", lambda e: e.activation(
                    Bs_t(2)[:, 16:24], Bs_t(2)[:, 0:8], AF.Identity,
                    scale=cols[:, 221:222]), rd=[bBs[2], bcols], wr=[bBs[2]])
                P.op("vector", lambda e, p=pt: e.tensor_tensor(
                    Bs_t(3)[:, 0:128].rearrange("p (e r) -> p e r", e=8),
                    p[:, 0:128].rearrange("p (e r) -> p e r", e=8),
                    Bs_t(2)[:, 16:24].unsqueeze(2).broadcast_to([128, 8, 16]),
                    OP.mult), rd=[ptb, bBs[2]], wr=[bBs[3]])
                transpose_f32(Bs_t(3)[:, 0:128], Bs_t(3)[:, 128:256],
                              rd=[bBs[3]], wr=[bBs[3]])
                for oc in range(2):
                    pl, plb = pacq()
                    P.op("tensor", lambda e, oc=oc, p=pl: e.matmul(
                        p[:], Bs_t(3)[:, 128:256], Cn_t(0)[:, ts(oc, 512)],
                        start=True, stop=True), rd=[bBs[3], bCn[0]], wr=[plb])
                    pb_, pbb = pacq()
                    for dc in range(ND):
                        P.op("tensor", lambda e, dc=dc, oc=oc, p=pb_: e.matmul(
                            p[:], Bs_t(1)[:, ts(dc)], Wk_(dc)[:, ts(oc, 512)],
                            start=(dc == 0), stop=(dc == ND - 1)),
                            rd=[bBs[1], bW], wr=[pbb])
                    P.op("scalar", lambda e, oc=oc, p=pb_: e.activation(
                        Bs_t(4)[:, ts(oc, 512)], p[:], AF.Identity),
                        rd=[pbb], wr=[bBs[4]])
                    P.op("vector", lambda e, oc=oc, p=pl: e.scalar_tensor_tensor(
                        Bs_t(4)[:, ts(oc, 512)], p[:], 1.0 / 16.0,
                        Bs_t(4)[:, ts(oc, 512)], OP.mult, OP.add),
                        rd=[plb, bBs[4]], wr=[bBs[4]])
                P.op("vector", lambda e, i=i: e.tensor_tensor(
                    A_t(i), A_t(i), Bs_t(4), OP.add), rd=[bBs[4], bA[i]],
                    wr=[bA[i]])

        bccin = [Buf(), Buf()]
        bccout = [Buf(), Buf()]
        bxsp = Buf()
        bosp = Buf()
        bxnsp = Buf()
        btsp = Buf()
        bdnsp = Buf()
        bKv = Buf()

        def fresh(olds):
            b = Buf()
            rs = []
            for o in olds:
                if o.w is not None:
                    rs.append(o.w)
                rs.extend(o.rs)
            b.rs = [t for t in rs if t is not None]
            return b

        def A_sl(slot, c0, c1):
            return A[:, slot * 1024 + c0:slot * 1024 + c1]

        def emit_mha(cfg):
            ba_ensure_fm()
            cci, cco = ccx_in[cfg["cc"]], ccx_out[cfg["cc"]]
            bci, bco = bccin[cfg["cc"]], bccout[cfg["cc"]]
            rope = cfg["rope"]
            if cfg["fold2"]:
                emit_ln(eps=1e-5, fold2=1e-6)
                bcast_row(1, cfg["lns"])
                bcast_row(2, cfg["lnb"])
            else:
                emit_ln(eps=1e-5)
            for i in range(NT):
                P.op("scalar", lambda e, i=i: e.activation(
                    Sc_t(0), A_t(i), AF.Identity, bias=ln_b(i), scale=ln_s(i)),
                    rd=[bA[i], bcols], wr=[bSc[0]])
                if cfg["fold2"]:
                    P.op("vector", lambda e: e.tensor_tensor(
                        Sc_t(0), Sc_t(0), Cn_t(1), OP.mult),
                        rd=[bSc[0], bCn[1]], wr=[bSc[0]])
                    P.op("vector", lambda e: e.tensor_tensor(
                        Sc_t(0), Sc_t(0), Cn_t(2), OP.add),
                        rd=[bSc[0], bCn[2]], wr=[bSc[0]])
                    P.dma("scalar", xnspill[ts(i), :], Sc_t(0), rd=[bSc[0]],
                          wr=[bxnsp])
                for kc in range(ND):
                    transpose_f32(Sc_t(0)[:, ts(kc)], Ba_t(kc)[:, ts(i)],
                                  rd=[bSc[0]], wr=[bBa[kc]])
                P.dma("scalar", xspill[ts(i), :], A_t(i), rd=[bA[i]],
                      wr=[bxsp])
            for kc in range(ND):
                P.dma("scalar", cci[ts(kc), :], Ba_t(kc), rd=[bBa[kc]],
                      wr=[bci])
            P.op("gpsimd", lambda e: e.collective_compute(
                "AllGather", OP.bypass, replica_groups=groups,
                ins=[cci.opt()], outs=[cco.opt()]), rd=[bci], wr=[bco])

            allA = bA + bAf
            bA0 = fresh(allA)
            bA1 = fresh(allA)
            bA2 = fresh(allA)
            bA3 = fresh(allA)
            bqf = [fresh(allA) for _ in range(4)]
            bkf = fresh(allA)
            bs14 = fresh(allA)
            bs15 = fresh(allA)

            c2pi = (6.28125, 1.9352436065674e-3, 6.336622499e-7)
            inv2pi = 0.15915494309189535

            def build_tab(tcol_ap, dst_cos, dst_sin, dbuf):
                th = A_sl(3, 128, 160)
                fr = A_sl(3, 512, 544)
                v = A_sl(3, 544, 576)
                kk = A_sl(3, 576, 608)
                dd = A_sl(3, 608, 640)
                g1 = A_sl(3, 640, 672)
                g2 = A_sl(3, 672, 704)
                red = A_sl(3, 704, 736)
                sA = A_sl(3, 736, 768)
                sB = A_sl(3, 768, 800)
                cp = A_sl(3, 800, 832)
                P.op("vector", lambda e: e.tensor_scalar(
                    fr, th, tcol_ap, None, OP.mult), rd=[bA3], wr=[bA3])
                P.op("vector", lambda e: e.tensor_scalar(
                    v, fr, inv2pi, None, OP.mult), rd=[bA3], wr=[bA3])
                P.op("vector", lambda e: e.tensor_copy(rint[:, 0:32], v),
                     rd=[bA3], wr=[brint])
                P.op("vector", lambda e: e.tensor_copy(kk, rint[:, 0:32]),
                     rd=[brint, bA3], wr=[bA3])
                P.op("vector", lambda e: e.tensor_tensor(
                    dd, v, kk, OP.subtract), rd=[bA3], wr=[bA3])
                P.op("vector", lambda e: e.tensor_scalar(
                    g1, dd, 0.5, None, OP.is_ge), rd=[bA3], wr=[bA3])
                P.op("vector", lambda e: e.tensor_scalar(
                    g2, dd, -0.5, None, OP.is_le), rd=[bA3], wr=[bA3])
                P.op("vector", lambda e: e.tensor_tensor(
                    g1, g1, g2, OP.subtract), rd=[bA3], wr=[bA3])
                P.op("vector", lambda e: e.tensor_tensor(
                    kk, kk, g1, OP.add), rd=[bA3], wr=[bA3])
                P.op("vector", lambda e: e.scalar_tensor_tensor(
                    red, kk, -c2pi[0], fr, OP.mult, OP.add),
                    rd=[bA3], wr=[bA3])
                P.op("vector", lambda e: e.scalar_tensor_tensor(
                    red, kk, -c2pi[1], red, OP.mult, OP.add),
                    rd=[bA3], wr=[bA3])
                P.op("vector", lambda e: e.scalar_tensor_tensor(
                    red, kk, -c2pi[2], red, OP.mult, OP.add),
                    rd=[bA3], wr=[bA3])
                P.op("scalar", lambda e: e.activation(
                    sA, red, AF.Sin, scale=0.5), rd=[bA3], wr=[bA3])
                P.op("scalar", lambda e: e.activation(
                    sB, red, AF.Sin, scale=0.25), rd=[bA3], wr=[bA3])
                P.op("scalar", lambda e: e.activation(
                    cp, sB, AF.Square), rd=[bA3], wr=[bA3])
                P.op("vector", lambda e: e.tensor_scalar(
                    cp, cp, -2.0, 1.0, OP.mult, OP.add), rd=[bA3], wr=[bA3])
                P.op("vector", lambda e, dst_sin=dst_sin: e.tensor_tensor(
                    dst_sin, sA, cp, OP.mult), rd=[bA3], wr=[dbuf])
                P.op("vector", lambda e, dst_sin=dst_sin: e.tensor_scalar(
                    dst_sin, dst_sin, 2.0, None, OP.mult), rd=[dbuf],
                    wr=[dbuf])
                P.op("scalar", lambda e: e.activation(
                    sA, sA, AF.Square), rd=[bA3], wr=[bA3])
                P.op("vector", lambda e, dst_cos=dst_cos: e.tensor_scalar(
                    dst_cos, sA, -2.0, 1.0, OP.mult, OP.add),
                    rd=[bA3], wr=[dbuf])

            if rope:
                P.op("gpsimd", lambda e: e.iota(
                    A_sl(3, 128, 160), pattern=[[1, 32]], base=0,
                    channel_multiplier=0,
                    allow_small_or_imprecise_dtypes=True), wr=[bA3])
                _lnth = -0.2878231366242557
                P.op("scalar", lambda e: e.activation(
                    A_sl(3, 128, 160), A_sl(3, 128, 160), AF.Exp,
                    scale=_lnth), rd=[bA3], wr=[bA3])
                for g in range(32):
                    P.op("gpsimd", lambda e, g=g: e.iota(
                        A_sl(3, 160, 168), pattern=[[0, 8]], base=g * 128,
                        channel_multiplier=1,
                        allow_small_or_imprecise_dtypes=True),
                        rd=[bA3], wr=[bA3])
                    build_tab(A_sl(3, 160, 161),
                              A_sl(0, g * 32, g * 32 + 32),
                              A_sl(1, g * 32, g * 32 + 32), bA0)
                for i in range(NT):
                    P.dma("sync", A_sl(3, 160, 161), di["tpos"][ts(i), :],
                          rd=[bA3], wr=[bA3])
                    build_tab(A_sl(3, 160, 161),
                              A_sl(2, i * 32, i * 32 + 32),
                              A_sl(2, 512 + i * 32, 512 + i * 32 + 32), bA2)

            dbg("cosk0", A_sl(0, 0, 1024), [bA0])
            dbg("sink0", A_sl(1, 0, 1024), [bA0])
            dbg("cosq0", A_sl(2, 0, 1024), [bA2])

            def rope_apply(tm_ap, tm_buf, cos_ap, sin_ap, tab_buf,
                           t1_ap, t2_ap, sbuf_):
                if not isinstance(sbuf_, list):
                    sbuf_ = [sbuf_]
                v3 = tm_ap.rearrange("p (g j) -> p g j", j=32)
                cb = cos_ap.unsqueeze(1).broadcast_to([128, 16, 32])
                sb2 = sin_ap.unsqueeze(1).broadcast_to([128, 16, 32])
                P.op("vector", lambda e: e.tensor_tensor(
                    t1_ap.rearrange("p (g j) -> p g j", j=32), v3, cb,
                    OP.mult), rd=[tm_buf, tab_buf], wr=list(sbuf_))
                P.op("vector", lambda e: e.tensor_tensor(
                    t2_ap.rearrange("p (g j) -> p g j", j=32), v3, sb2,
                    OP.mult), rd=[tm_buf, tab_buf], wr=list(sbuf_))
                h4 = tm_ap.rearrange("p (h q j) -> p h q j", q=2, j=32)
                t1h = t1_ap.rearrange("p (h q j) -> p h q j", q=2, j=32)
                t2h = t2_ap.rearrange("p (h q j) -> p h q j", q=2, j=32)
                P.op("vector", lambda e: e.tensor_tensor(
                    h4[:, :, 0, :], t1h[:, :, 0, :], t2h[:, :, 1, :],
                    OP.subtract), rd=list(sbuf_), wr=[tm_buf])
                P.op("vector", lambda e: e.tensor_tensor(
                    h4[:, :, 1, :], t1h[:, :, 1, :], t2h[:, :, 0, :],
                    OP.add), rd=list(sbuf_), wr=[tm_buf])

            bxn_kc = fresh(bBs + bBa)
            bvau = fresh(bBs + bBa)
            bprB = fresh(bBs + bBa)
            bacc = fresh(bBs + bBa)
            if cfg["bias"]:
                bcast_row(0, cfg["bq"])
                bcast_row(1, cfg["bk"])
                bcast_row(2, cfg["bv"])

            def vau2(t):
                return Ba[:, 4096 + t * 768:4096 + t * 768 + 768]

            def prA(kt):
                return Sc[:, kt * 512:kt * 512 + 512]

            def prAbuf(kt):
                return bSc[0] if kt < 2 else bSc[1]

            def prB(kt):
                if kt < 2:
                    return Ba[:, 7168 + kt * 512:7168 + kt * 512 + 512]
                if kt == 2:
                    return A_sl(15, 512, 1024)
                return A_sl(15, 0, 512)

            def prBbuf(kt):
                return bprB if kt < 2 else bs15

            def pv_soff(pa, hp):
                return pa * 64 if hp < 2 else (1 - pa) * 64

            def acc_ap(pa, hp, c0, c1):
                r0 = pv_soff(pa, hp)
                b0 = 8192 + hp * 2048
                return Ba[r0:r0 + 64, b0 + c0:b0 + c1]

            def den_rg(rg):
                if rg < 2:
                    return A[:, 3072 + rg * 512:3072 + rg * 512 + 512], bA3
                if rg < 4:
                    return (A[:, 14336 + (rg - 2) * 512:
                              14336 + (rg - 2) * 512 + 512], bs14)
                if rg == 4:
                    return aq16.bitcast(F32), baq
                if rg == 5:
                    return aqf16.bitcast(F32), baqf
                if rg == 6:
                    return rint.bitcast(F32), brint
                return Cn[:, 3072:3584], bCn3

            def den_ap(pa, hp, qb):
                rg = (hp % 2) * 4 + qb
                row = (64 - pv_soff(pa, hp)) + 32 * (hp // 2)
                reg, tok = den_rg(rg)
                return reg[row:row + 1, :], tok

            for g in range(2):
                bxn_kc = fresh([bxn_kc, bvau, bprB])
                for dc in range(ND):
                    P.dma("sync", Ba[:, dc * 1024:(dc + 1) * 1024],
                          cci[ts(dc), 0:1024], rd=[bci], wr=[bxn_kc])
                    P.dma("sync",
                          Ba[:, 8192 + dc * 1024:8192 + (dc + 1) * 1024],
                          cci[ts(dc), 1024:2048], rd=[bci], wr=[bacc])
                load_weight(cfg["wq"], csl=ts(g, 512))

                def xn_own(dc, tt):
                    base = dc * 1024 if tt < 8 else 8192 + dc * 1024
                    off = (tt % 8) * 128
                    return Ba[:, base + off:base + off + 128]

                for tt in range(NT):
                    if cfg["qfm_direct"]:
                        for oc in range(4):
                            p, pb = pacq()
                            for dc in range(ND):
                                P.op("tensor", lambda e, dc=dc, oc=oc, tt=tt,
                                     p=p: e.matmul(
                                         p[:, 0:128],
                                         Wk_(dc, 512)[:, ts(oc)],
                                         xn_own(dc, tt),
                                         start=(dc == 0),
                                         stop=(dc == ND - 1)),
                                     rd=[bW, bxn_kc, bacc], wr=[pb])
                            P.op("vector", lambda e, oc=oc, tt=tt, p=p:
                                 e.tensor_copy(
                                     A[:, (4 + 2 * oc) * 1024 + tt * 128:
                                       (4 + 2 * oc) * 1024 + tt * 128 + 128],
                                     p[:, 0:128]), rd=[pb], wr=[bqf[oc]])
                    else:
                        p, pb = pacq()
                        for dc in range(ND):
                            P.op("tensor", lambda e, dc=dc, tt=tt, p=p:
                                 e.matmul(p[:], xn_own(dc, tt),
                                          Wk_(dc, 512),
                                          start=(dc == 0),
                                          stop=(dc == ND - 1)),
                                 rd=[bW, bxn_kc, bacc], wr=[pb])
                        P.op("vector", lambda e, p=p, g=g: e.tensor_tensor(
                            A_sl(14, 0, 512), p[:],
                            Cn_t(0)[:, g * 512:g * 512 + 512], OP.add),
                            rd=[pb, bCn[0]], wr=[bs14])
                        rope_apply(A_sl(14, 0, 512), bs14,
                                   A_sl(2, tt * 32, tt * 32 + 32),
                                   A_sl(2, 512 + tt * 32, 512 + tt * 32 + 32),
                                   bA2, A_sl(14, 512, 1024),
                                   A_sl(15, 512, 1024), bs15)
                        if tt == 0 and g == 0:
                            dbg("qrope0", A_sl(14, 0, 512), [bs14])
                        for c in range(4):
                            transpose_f32(
                                A_sl(14, c * 128, c * 128 + 128),
                                A[:, (4 + 2 * c) * 1024 + tt * 128:
                                  (4 + 2 * c) * 1024 + tt * 128 + 128],
                                rd=[bs14], wr=[bqf[c]])

                load_weight(cfg["wk"], csl=ts(g, 512))
                load_weight(cfg["wv"], csl=ts(g, 512), woff=4096)
                bvau = fresh([bvau, bxn_kc, bacc])
                bprB = fresh([bprB, bxn_kc, bacc])
                for t in range(4):
                    P.op("vector", lambda e, t=t: e.memset(
                        vau2(t).rearrange("p (hp s) -> p hp s", s=192)
                        [:, :, 64:128], 1.0), wr=[bvau])

                for kc in range(8):
                    blk = kc // 4
                    c0 = (kc * 512) % 2048
                    for dc in range(ND):
                        P.dma("sync", Ba[:, dc * 512:(dc + 1) * 512],
                              cco[blk * 1024 + dc * 128:
                                  blk * 1024 + (dc + 1) * 128, c0:c0 + 512],
                              rd=[bco], wr=[bxn_kc])
                    for kt in range(4):
                        gt = kc * 4 + kt
                        p, pb = pacq()
                        for dc in range(ND):
                            P.op("tensor", lambda e, dc=dc, kt=kt, p=p:
                                 e.matmul(
                                     p[:],
                                     Ba[:, dc * 512 + kt * 128:
                                        dc * 512 + kt * 128 + 128],
                                     W[:, dc * 512:(dc + 1) * 512],
                                     start=(dc == 0), stop=(dc == ND - 1)),
                                 rd=[bW, bxn_kc], wr=[pb])
                        if cfg["bias"]:
                            P.op("vector", lambda e, p=p, g=g: e.tensor_tensor(
                                Sc[:, 0:512], p[:],
                                Cn_t(1)[:, g * 512:g * 512 + 512], OP.add),
                                rd=[pb, bCn[1]], wr=[bSc[0]])
                        else:
                            P.op("vector", lambda e, p=p: e.tensor_copy(
                                Sc[:, 0:512], p[:]), rd=[pb], wr=[bSc[0]])
                        if rope:
                            rope_apply(Sc[:, 0:512], bSc[0],
                                       A_sl(0, gt * 32, gt * 32 + 32),
                                       A_sl(1, gt * 32, gt * 32 + 32),
                                       bA0, Sc[:, 512:1024],
                                       Sc[:, 1024:1536], [bSc[0], bSc[1]])
                        if kc == 0 and kt == 0 and g == 0:
                            dbg("krope0", Sc[:, 0:512], [bSc[0]])
                        for c in range(4):
                            transpose_f32(
                                Sc[:, c * 128:c * 128 + 128],
                                A[:, 12 * 1024 + c * 512 + kt * 128:
                                  12 * 1024 + c * 512 + kt * 128 + 128],
                                rd=[bSc[0]], wr=[bkf])
                        p, pb = pacq()
                        for dc in range(ND):
                            P.op("tensor", lambda e, dc=dc, kt=kt, p=p:
                                 e.matmul(
                                     p[:],
                                     Ba[:, dc * 512 + kt * 128:
                                        dc * 512 + kt * 128 + 128],
                                     W[:, 4096 + dc * 512:
                                       4096 + (dc + 1) * 512],
                                     start=(dc == 0), stop=(dc == ND - 1)),
                                 rd=[bW, bxn_kc], wr=[pb])
                        vv = vau2(kt).rearrange("p (hp s) -> p hp s", s=192)
                        pv4 = p[:].rearrange("p (hp pa d) -> p hp pa d",
                                             pa=2, d=64)
                        if cfg["bias"]:
                            bv4 = Cn_t(2)[:, g * 512:g * 512 + 512].rearrange(
                                "p (hp pa d) -> p hp pa d", pa=2, d=64)
                        for (hsl, pa, scol) in (
                                (slice(0, 2), 0, slice(0, 64)),
                                (slice(0, 2), 1, slice(128, 192)),
                                (slice(2, 4), 0, slice(128, 192)),
                                (slice(2, 4), 1, slice(0, 64))):
                            if cfg["bias"]:
                                P.op("vector", lambda e, pv4=pv4, bv4=bv4,
                                     vv=vv, hsl=hsl, pa=pa, scol=scol:
                                     e.tensor_tensor(
                                         vv[:, hsl, scol],
                                         pv4[:, hsl, pa, :],
                                         bv4[:, hsl, pa, :], OP.add),
                                     rd=[pb, bCn[2]], wr=[bvau])
                            else:
                                P.op("vector", lambda e, pv4=pv4, vv=vv,
                                     hsl=hsl, pa=pa, scol=scol:
                                     e.tensor_copy(vv[:, hsl, scol],
                                                   pv4[:, hsl, pa, :]),
                                     rd=[pb], wr=[bvau])
                    if kc == 0 and g == 0:
                        dbg("vau0", vau2(0)[:, 0:192], [bvau])
                    for hp in range(4):
                        for qb in range(4):
                            for kt in range(4):
                                pA_, pAb = pacq()
                                pB_, pBb = pacq()
                                kbase = 12 * 1024 + hp * 512 + kt * 128
                                qbase = (4 + 2 * hp) * 1024 + qb * 512
                                P.op("tensor", lambda e, kbase=kbase,
                                     qbase=qbase, p=pA_: e.matmul(
                                         p[:], A[0:64, kbase:kbase + 128],
                                         A[0:64, qbase:qbase + 512],
                                         start=True, stop=True,
                                         tile_position=(0, 0)),
                                     rd=[bkf, bqf[hp]], wr=[pAb])
                                P.op("tensor", lambda e, kbase=kbase,
                                     qbase=qbase, p=pB_: e.matmul(
                                         p[:], A[64:128, kbase:kbase + 128],
                                         A[64:128, qbase:qbase + 512],
                                         start=True, stop=True,
                                         tile_position=(64, 0)),
                                     rd=[bkf, bqf[hp]], wr=[pBb])
                                P.op("scalar", lambda e, kt=kt, p=pA_:
                                     e.activation(prA(kt), p[:], AF.Exp,
                                                  scale=cfg["sc"]),
                                     rd=[pAb],
                                     wr=[bSc[0] if kt < 2 else bSc[1]])
                                P.op("scalar", lambda e, kt=kt, p=pB_:
                                     e.activation(prB(kt), p[:], AF.Exp,
                                                  scale=cfg["sc"]),
                                     rd=[pBb], wr=[prBbuf(kt)])
                            if (kc == 0 and g == 0 and hp == 0
                                    and qb == 0):
                                dbg("prA0", prA(0), [bSc[0]])
                            for pa in range(2):
                                po, pob = pacq()
                                soff = pv_soff(pa, hp)
                                for kt in range(4):
                                    pr = (prA(kt) if pa == 0 else prB(kt))
                                    prb = (prAbuf(kt) if pa == 0
                                           else prBbuf(kt))
                                    s0 = 4096 + kt * 768 + hp * 192 + soff
                                    P.op("tensor", lambda e, s0=s0, po=po,
                                         pr=pr, kt=kt: e.matmul(
                                             po[:],
                                             Ba[:, s0:s0 + 128], pr,
                                             start=(kt == 0),
                                             stop=(kt == 3)),
                                         rd=[prb, bvau], wr=[pob])
                                a_ap = acc_ap(pa, hp, qb * 512, qb * 512 + 512)
                                d_ap, dtok = den_ap(pa, hp, qb)
                                dn = (64 - soff) + 32 * (hp // 2)
                                if kc == 0:
                                    P.op("vector", lambda e, po=po, a_ap=a_ap,
                                         soff=soff: e.tensor_copy(
                                             a_ap, po[soff:soff + 64, :]),
                                         rd=[pob], wr=[bacc])
                                    P.op("vector", lambda e, po=po, d_ap=d_ap,
                                         dn=dn: e.tensor_copy(
                                             d_ap, po[dn:dn + 1, :]),
                                         rd=[pob], wr=[dtok])
                                else:
                                    P.op("vector", lambda e, po=po, a_ap=a_ap,
                                         soff=soff: e.tensor_tensor(
                                             a_ap, a_ap, po[soff:soff + 64, :],
                                             OP.add), rd=[pob], wr=[bacc])
                                    P.op("vector", lambda e, po=po, d_ap=d_ap,
                                         dn=dn: e.tensor_tensor(
                                             d_ap, d_ap, po[dn:dn + 1, :],
                                             OP.add),
                                        rd=[pob], wr=[dtok])
                if g == 0:
                    dbg("acc0", Ba[:, 8192:8192 + 512], [bacc])
                    dbg("den0", A_sl(3, 0, 128), [bA3])
                for hp in range(4):
                    for pa in range(2):
                        for qb in range(4):
                            d_ap, dtok = den_ap(pa, hp, qb)
                            P.op("vector", lambda e, d_ap=d_ap:
                                 e.reciprocal(d_ap, d_ap),
                                 rd=[dtok], wr=[dtok])
                            h = hp * 2 + pa
                            P.dma("scalar",
                                  denspill[h:h + 1, qb * 512:qb * 512 + 512],
                                  d_ap, rd=[dtok], wr=[bdnsp])
                for h in range(8):
                    hp, pa = h // 2, h % 2
                    r0 = pv_soff(pa, hp)
                    P.dma("sync", A[r0:r0 + 64, 12288:14336],
                          denspill[h:h + 1, :].partition_broadcast(64),
                          rd=[bdnsp], wr=[bkf])
                    P.op("vector", lambda e, r0=r0, hp=hp: e.tensor_tensor(
                        Ba[r0:r0 + 64,
                           8192 + hp * 2048:8192 + (hp + 1) * 2048],
                        Ba[r0:r0 + 64,
                           8192 + hp * 2048:8192 + (hp + 1) * 2048],
                        A[r0:r0 + 64, 12288:14336], OP.mult),
                        rd=[bkf], wr=[bacc])

                if cfg["wo"] is not None:
                    for hp in range(4):
                        r0 = g * 512 + hp * 128
                        if hp < 2:
                            P.dma("sync", W[:, hp * 1024:(hp + 1) * 1024],
                                  di[cfg["wo"]][r0:r0 + 128, :], wr=[bW])
                        else:
                            P.dma("sync",
                                  W[0:64, hp * 1024:(hp + 1) * 1024],
                                  di[cfg["wo"]][r0 + 64:r0 + 128, :],
                                  wr=[bW])
                            P.dma("sync",
                                  W[64:128, hp * 1024:(hp + 1) * 1024],
                                  di[cfg["wo"]][r0:r0 + 64, :], wr=[bW])
                    if g == 1:
                        for i in range(NT):
                            bA[i] = fresh([bA0, bA1, bA2, bA3, *bqf, bkf,
                                           bs14, bs15, bA[i], bAf[i // 2]])
                        bcast_row(0, cfg["bo"])
                    for tt in range(NT):
                        if g == 1:
                            P.dma("sync", A_t(tt), xspill[ts(tt), :],
                                  rd=[bxsp], wr=[bA[tt]])
                            P.dma("sync", Sc[:, 1024:2048],
                                  tspill[ts(tt), :], rd=[btsp], wr=[bSc[1]])
                        for ec in range(2):
                            p, pb = pacq()
                            for hp in range(4):
                                P.op("tensor", lambda e, hp=hp, tt=tt, ec=ec,
                                     p=p: e.matmul(
                                         p[:],
                                         Ba[:, 8192 + hp * 2048 + tt * 128:
                                            8192 + hp * 2048 + tt * 128
                                            + 128],
                                         W[:, hp * 1024 + ec * 512:
                                           hp * 1024 + ec * 512 + 512],
                                         start=(hp == 0), stop=(hp == 3)),
                                     rd=[bacc, bW], wr=[pb])
                            if g == 0:
                                P.op("vector", lambda e, p=p, ec=ec:
                                     e.tensor_copy(
                                         Sc[:, ec * 512:ec * 512 + 512],
                                         p[:]), rd=[pb], wr=[bSc[0]])
                            else:
                                P.op("vector", lambda e, p=p, ec=ec:
                                     e.tensor_tensor(
                                         Sc[:, ec * 512:ec * 512 + 512],
                                         p[:],
                                         Sc[:, 1024 + ec * 512:
                                            1024 + ec * 512 + 512], OP.add),
                                     rd=[pb, bSc[1]], wr=[bSc[0]])
                        if g == 0:
                            P.dma("scalar", tspill[ts(tt), :], Sc[:, 0:1024],
                                  rd=[bSc[0]], wr=[btsp])
                        else:
                            P.op("vector", lambda e: e.tensor_tensor(
                                Sc[:, 0:1024], Sc[:, 0:1024], Cn_t(0),
                                OP.add), rd=[bCn[0]], wr=[bSc[0]])
                            P.op("vector", lambda e, tt=tt: e.tensor_tensor(
                                A_t(tt), A_t(tt), Sc[:, 0:1024], OP.add),
                                rd=[bSc[0]], wr=[bA[tt]])
                else:
                    def acc_tp(hp, tt, dst0):
                        src = Ba[:, 8192 + hp * 2048 + tt * 128:
                                 8192 + hp * 2048 + tt * 128 + 128]
                        if hp < 2:
                            transpose_f32(src, Sc[:, dst0:dst0 + 128],
                                          rd=[bacc], wr=[bSc[0]])
                        else:
                            p, pb2 = pacq()
                            P.op("tensor", lambda e, p=p, src=src:
                                 e.transpose(p[:, 0:128], src, ident[:]),
                                 rd=[bacc, bident], wr=[pb2])
                            P.op("vector", lambda e, p=p, dst0=dst0:
                                 e.tensor_copy(Sc[:, dst0 + 64:dst0 + 128],
                                               p[:, 0:64]),
                                 rd=[pb2], wr=[bSc[0]])
                            P.op("vector", lambda e, p=p, dst0=dst0:
                                 e.tensor_copy(Sc[:, dst0:dst0 + 64],
                                               p[:, 64:128]),
                                 rd=[pb2], wr=[bSc[0]])
                    if g == 1:
                        for i in range(NT):
                            bA[i] = fresh([bA0, bA1, bA2, bA3, *bqf, bkf,
                                           bs14, bs15, bA[i], bAf[i // 2]])
                    for tt in range(NT):
                        if g == 0:
                            for hp in range(4):
                                acc_tp(hp, tt, hp * 128)
                            P.dma("scalar", tspill[ts(tt), 0:512],
                                  Sc[:, 0:512], rd=[bSc[0]], wr=[btsp])
                        else:
                            P.dma("sync", A_t(tt), xspill[ts(tt), :],
                                  rd=[bxsp], wr=[bA[tt]])
                            for hp in range(4):
                                acc_tp(hp, tt, 512 + hp * 128)
                            P.dma("sync", Sc[:, 0:512],
                                  tspill[ts(tt), 0:512], rd=[btsp],
                                  wr=[bSc[0]])
                            P.dma("sync", Sc[:, 1024:2048],
                                  xnspill[ts(tt), :], rd=[bxnsp],
                                  wr=[bSc[1]])
                            P.op("vector", lambda e, tt=tt: e.tensor_tensor(
                                A_t(tt), A_t(tt), Sc_t(1), OP.add),
                                rd=[bSc[1]], wr=[bA[tt]])
                            P.op("vector", lambda e, tt=tt: e.tensor_tensor(
                                A_t(tt), A_t(tt), Sc_t(0), OP.add),
                                rd=[bSc[0]], wr=[bA[tt]])
            for j in range(ND):
                bBa[j] = fresh([bacc, bxn_kc, bvau, bprB, bs15, bBa[j]])
            for i in range(16):
                bBs[i] = fresh([bBa[i // 2]])
            ba_state[0] = "slots"

        def emit_l0b():
            emit_mha(dict(cc=0, rope=True, fold2=False, bias=True,
                          wq="l0b_wqT", wk="l0b_wkT", wv="l0b_wvT",
                          bq="l0b_bq", bk="l0b_bk", bv="l0b_bv",
                          wo="l0b_woT", bo="l0b_bo", sc=0.125,
                          qfm_direct=False, lns=None, lnb=None))

        def emit_l1a():
            emit_mha(dict(cc=1, rope=False, fold2=True, bias=False,
                          wq="l1a_wq", wk="l1a_wk", wv="l1a_wv",
                          bq=None, bk=None, bv=None,
                          wo=None, bo=None, sc=0.03125,
                          qfm_direct=True, lns="l1a_lns", lnb="l1a_lnb"))

        def emit_mb():
            ba_ensure_fm()
            bch_i = Buf()
            bch_o = Buf()
            bh0_i = Buf()
            bh0_o = Buf()
            bWs = bW
            buspb = Buf()
            bCns = Buf()
            bCnk = Buf()
            emit_ln(eps=1e-5)
            for i in range(NT):
                P.op("scalar", lambda e, i=i: e.activation(
                    Sc_t(0), A_t(i), AF.Identity, bias=ln_b(i), scale=ln_s(i)),
                    rd=[bA[i], bcols], wr=[bSc[0]])
                for kc in range(ND):
                    transpose_f32(Sc_t(0)[:, ts(kc)], Ba_t(kc)[:, ts(i)],
                                  rd=[bSc[0]], wr=[bBa[kc]])
                P.dma("scalar", xspill[ts(i), :], A_t(i), rd=[bA[i]],
                      wr=[bxsp])
            P.dma("sync", cols[0:1, 230:231], di["modd"], wr=[bcols])
            pmk, pmkb = pacq()
            P.op("tensor", lambda e, p=pmk: e.matmul(
                p[:, 0:1], ones_c[0:1, :], cols[0:1, 230:231],
                start=True, stop=True), rd=[bcols], wr=[pmkb])
            P.op("vector", lambda e, p=pmk: e.tensor_copy(
                cols[:, 231:232], p[:, 0:1]), rd=[pmkb], wr=[bcols])
            P.op("vector", lambda e: e.tensor_scalar(
                cols[:, 232:233], cols[:, 231:232], -1.0, 1.0, OP.mult,
                OP.add), rd=[bcols], wr=[bcols])
            modd_c = cols[:, 231:232]
            mev_c = cols[:, 232:233]

            allA = bA + bAf
            bxm = [fresh(allA) for _ in range(ND)]

            def load_ip(half):
                load_weight("mb_ipT", csl=slice(half * 1024,
                                                half * 1024 + 1024))

            for half in range(2):
                load_ip(half)
                for bt, rows in ((0, (0, 1)), (15, (127,))):
                    for oc in range(2):
                        p, pb = pacq()
                        for dc in range(ND):
                            P.op("tensor", lambda e, dc=dc, bt=bt, oc=oc, p=p:
                                 e.matmul(p[:], Ba_t(dc)[:, ts(bt)],
                                          Wk_(dc)[:, ts(oc, 512)],
                                          start=(dc == 0),
                                          stop=(dc == ND - 1)),
                                 rd=[bW, bBa[dc]], wr=[pb])
                        P.op("vector", lambda e, oc=oc, p=p: e.tensor_copy(
                            Sc_t(1)[:, ts(oc, 512)], p[:]), rd=[pb],
                            wr=[bSc[1]])
                    for r, ridx in (((0, 1), None) if False else
                                    [(r, {(0, 0): 0, (0, 1): 1,
                                          (15, 127): 2}[(bt, r)])
                                     for r in rows]):
                        P.dma("scalar",
                              cch_in[ridx:ridx + 1,
                                     half * 1024:half * 1024 + 1024],
                              Sc_t(1)[r:r + 1, :], rd=[bSc[1]], wr=[bch_i])
                        if ridx == 2:
                            P.dma("scalar",
                                  cch_in[3:4, half * 1024:half * 1024 + 1024],
                                  Sc_t(1)[r:r + 1, :], rd=[bSc[1]],
                                  wr=[bch_i])
            P.op("gpsimd", lambda e: e.collective_compute(
                "AllGather", OP.bypass, replica_groups=groups,
                ins=[cch_in.opt()], outs=[cch_out.opt()]),
                rd=[bch_i], wr=[bch_o])
            P.dma("sync", Sc[0:8, 0:2048], cch_out, rd=[bch_o],
                  wr=[bSc[0], bSc[1]])
            for zi in range(2):
                for ct in range(ND):
                    p, pb = pacq()
                    P.op("tensor", lambda e, zi=zi, ct=ct, p=p: e.transpose(
                        p[:, 0:8], Sc[0:8, zi * 1024 + ct * 128:
                                      zi * 1024 + ct * 128 + 128],
                        ident[0:8, 0:8]),
                        rd=[bSc[0], bSc[1], bident], wr=[pb])
                    base = zi * 64 + ct * 8
                    P.op("vector", lambda e, p=p, base=base: e.tensor_tensor(
                        rowb[:, base:base + 1], p[:, 2:3], modd_c, OP.mult),
                        rd=[pb, bcols], wr=[browb])
                    P.op("vector", lambda e, p=p, base=base: e.tensor_tensor(
                        rowb[:, base + 1:base + 2], p[:, 4:5], mev_c,
                        OP.mult), rd=[pb, bcols], wr=[browb])
                    P.op("vector", lambda e, p=p, base=base: e.tensor_tensor(
                        rowb[:, base + 2:base + 3], p[:, 5:6], mev_c,
                        OP.mult), rd=[pb, bcols], wr=[browb])

            load_ip(0)
            for ct in range(ND):
                for tc in range(4):
                    p, pb = pacq()
                    for dc in range(ND):
                        P.op("tensor", lambda e, dc=dc, ct=ct, tc=tc, p=p:
                             e.matmul(p[:], Wk_(dc)[:, ts(ct)],
                                      Ba_t(dc)[:, ts(tc, 512)],
                                      start=(dc == 0), stop=(dc == ND - 1)),
                             rd=[bW, bBa[dc]], wr=[pb])
                    P.op("vector", lambda e, ct=ct, tc=tc, p=p: e.tensor_copy(
                        A[:, ct * 2048 + tc * 512:ct * 2048 + tc * 512 + 512],
                        p[:]), rd=[pb], wr=[bxm[ct]])

            P.dma("sync", Cn[:, 2048:2080].rearrange("p (a k) -> p a k", a=8),
                  di["mb_cx"].rearrange("(a p) k -> p a k", p=128), wr=[bCnk])
            P.dma("sync", Cn[:, 2080:2112].rearrange("p (a k) -> p a k", a=8),
                  di["mb_cz"].rearrange("(a p) k -> p a k", p=128), wr=[bCnk])

            def conv_silu(src_ap, src_buf, wbase, wc_buf, halo_base,
                          dst_ap, dst_buf):
                scr = Cn[:, 0:2048]
                w_cols = Cn[:, wbase:wbase + 4]
                P.op("vector", lambda e: e.tensor_scalar(
                    scr, src_ap, w_cols[:, 1:2], None, OP.mult),
                    rd=[src_buf, wc_buf], wr=[bCns])
                P.op("vector", lambda e: e.scalar_tensor_tensor(
                    scr[:, 1:2048], src_ap[:, 0:2047], w_cols[:, 0:1],
                    scr[:, 1:2048], OP.mult, OP.add),
                    rd=[src_buf, wc_buf, bCns], wr=[bCns])
                P.op("vector", lambda e: e.scalar_tensor_tensor(
                    scr[:, 0:2047], src_ap[:, 1:2048], w_cols[:, 2:3],
                    scr[:, 0:2047], OP.mult, OP.add),
                    rd=[src_buf, wc_buf, bCns], wr=[bCns])
                P.op("vector", lambda e: e.scalar_tensor_tensor(
                    scr[:, 0:2046], src_ap[:, 2:2048], w_cols[:, 3:4],
                    scr[:, 0:2046], OP.mult, OP.add),
                    rd=[src_buf, wc_buf, bCns], wr=[bCns])
                hL = rowb[:, halo_base:halo_base + 1]
                hR0 = rowb[:, halo_base + 1:halo_base + 2]
                hR1 = rowb[:, halo_base + 2:halo_base + 3]
                P.op("vector", lambda e: e.scalar_tensor_tensor(
                    scr[:, 0:1], hL, w_cols[:, 0:1], scr[:, 0:1],
                    OP.mult, OP.add), rd=[browb, wc_buf, bCns], wr=[bCns])
                P.op("vector", lambda e: e.scalar_tensor_tensor(
                    scr[:, 2047:2048], hR0, w_cols[:, 2:3],
                    scr[:, 2047:2048], OP.mult, OP.add),
                    rd=[browb, wc_buf, bCns], wr=[bCns])
                P.op("vector", lambda e: e.scalar_tensor_tensor(
                    scr[:, 2046:2047], hR0, w_cols[:, 3:4],
                    scr[:, 2046:2047], OP.mult, OP.add),
                    rd=[browb, wc_buf, bCns], wr=[bCns])
                P.op("vector", lambda e: e.scalar_tensor_tensor(
                    scr[:, 2047:2048], hR1, w_cols[:, 3:4],
                    scr[:, 2047:2048], OP.mult, OP.add),
                    rd=[browb, wc_buf, bCns], wr=[bCns])
                P.op("scalar", lambda e: e.activation(
                    dst_ap, scr, AF.Silu), rd=[bCns], wr=[dst_buf])

            load_ip(1)
            for ct in range(ND):
                for tc in range(4):
                    p, pb = pacq()
                    for dc in range(ND):
                        P.op("tensor", lambda e, dc=dc, ct=ct, tc=tc, p=p:
                             e.matmul(p[:], Wk_(dc)[:, ts(ct)],
                                      Ba_t(dc)[:, ts(tc, 512)],
                                      start=(dc == 0), stop=(dc == ND - 1)),
                             rd=[bW, bBa[dc]], wr=[pb])
                    P.op("vector", lambda e, tc=tc, p=p: e.tensor_copy(
                        Sc[:, tc * 512:tc * 512 + 512], p[:]), rd=[pb],
                        wr=[bSc[0], bSc[1]])
                conv_silu(Sc[:, 0:2048], bSc[0], 2080 + ct * 4, bCnk,
                          64 + ct * 8, Sc[:, 0:2048], bSc[0])
                P.dma("scalar",
                      tspill[ts(ct, 256), :].rearrange(
                          "(p x) d -> p (x d)", p=128),
                      Sc[:, 0:2048], rd=[bSc[0]], wr=[btsp])
            for ct in range(ND):
                conv_silu(A[:, ct * 2048:(ct + 1) * 2048], bxm[ct],
                          2048 + ct * 4, bCnk, ct * 8,
                          A[:, ct * 2048:(ct + 1) * 2048], bxm[ct])

            dbg("mxc0", A[:, 0:2048], [bxm[0]])
            load_weight("mb_xpT")
            for tc in range(4):
                p, pb = pacq()
                for dc in range(ND):
                    P.op("tensor", lambda e, dc=dc, tc=tc, p=p: e.matmul(
                        p[0:96, :], W[:, dc * 96:(dc + 1) * 96],
                        A[:, dc * 2048 + tc * 512:dc * 2048 + tc * 512 + 512],
                        start=(dc == 0), stop=(dc == ND - 1)),
                        rd=[bW, bxm[dc]], wr=[pb])
                P.op("vector", lambda e, tc=tc, p=p: e.tensor_copy(
                    Sc[0:96, tc * 512:tc * 512 + 512], p[0:96, :]),
                    rd=[pb], wr=[bSc[0], bSc[1]])

            dbg("mxdbl", Sc[0:96, 0:2048], [bSc[0], bSc[1]])
            bbcs = Buf()
            P.dma("scalar", bcsp, Sc[64:96, 0:2048], rd=[bSc[0], bSc[1]],
                  wr=[bbcs])

            bdel = [fresh([bBs[2 * j], bBs[2 * j + 1], bBa[j]])
                    for j in range(ND)]
            P.dma("sync", W[0:64, 1024:2048], di["mb_dtT"], wr=[bW])
            P.dma("sync", rowb[0:8, :],
                  di["mb_dtb"].rearrange("(a f) o -> (o a) f", a=8),
                  wr=[browb])
            pdt, pdtb = pacq()
            P.op("tensor", lambda e, p=pdt: e.transpose(
                p[:, 0:8], rowb[0:8, :], ident[0:8, 0:8]),
                rd=[browb, bident], wr=[pdtb])
            P.op("vector", lambda e, p=pdt: e.tensor_copy(
                cols[:, 240:248], p[:, 0:8]), rd=[pdtb], wr=[bcols])
            P.dma("sync", rowb[0:8, :],
                  di["mb_dvec"].rearrange("(a f) o -> (o a) f", a=8),
                  wr=[browb])
            pdv, pdvb = pacq()
            P.op("tensor", lambda e, p=pdv: e.transpose(
                p[:, 0:8], rowb[0:8, :], ident[0:8, 0:8]),
                rd=[browb, bident], wr=[pdvb])
            P.op("vector", lambda e, p=pdv: e.tensor_copy(
                cols[:, 248:256], p[:, 0:8]), rd=[pdvb], wr=[bcols])
            _one = KC(1.0)
            for ct in range(ND):
                for tc in range(4):
                    p, pb = pacq()
                    P.op("tensor", lambda e, ct=ct, tc=tc, p=p: e.matmul(
                        p[:], W[0:64, 1024 + ct * 128:1024 + ct * 128 + 128],
                        Sc[0:64, tc * 512:tc * 512 + 512],
                        start=True, stop=True),
                        rd=[bW, bSc[0], bSc[1]], wr=[pb])
                    P.op("vector", lambda e, ct=ct, tc=tc, p=p: e.tensor_copy(
                        Ba[:, ct * 2048 + tc * 512:ct * 2048 + tc * 512 + 512],
                        p[:]), rd=[pb], wr=[bdel[ct]])
                dr = Ba[:, ct * 2048:(ct + 1) * 2048]
                P.op("scalar", lambda e, dr=dr, ct=ct: e.activation(
                    Cn[:, 0:2048], dr, AF.Abs, bias=cols[:, 240 + ct:241 + ct]),
                    rd=[bdel[ct], bcols], wr=[bCns])
                P.op("scalar", lambda e: e.activation(
                    Cn[:, 0:2048], Cn[:, 0:2048], AF.Exp, scale=-1.0),
                    rd=[bCns], wr=[bCns])
                P.op("scalar", lambda e, _one=_one: e.activation(
                    Cn[:, 0:2048], Cn[:, 0:2048], AF.Ln, bias=_one),
                    rd=[bCns, bkconst], wr=[bCns])
                P.op("scalar", lambda e, dr=dr, ct=ct: e.activation(
                    W[:, 2048:4096], dr, AF.Relu,
                    bias=cols[:, 240 + ct:241 + ct]),
                    rd=[bdel[ct], bcols], wr=[bWs])
                P.op("vector", lambda e, dr=dr: e.tensor_tensor(
                    dr, Cn[:, 0:2048], W[:, 2048:4096], OP.add),
                    rd=[bCns, bWs], wr=[bdel[ct]])
                P.op("vector", lambda e, ct=ct, dr=dr: e.tensor_tensor(
                    Cn[:, 0:2048], dr,
                    A[:, ct * 2048:(ct + 1) * 2048], OP.mult),
                    rd=[bdel[ct], bxm[ct]], wr=[bCns])
                P.dma("scalar",
                      ospill[ts(ct, 256), :].rearrange(
                          "(p x) d -> p (x d)", p=128),
                      Cn[:, 0:2048], rd=[bCns], wr=[bosp])
                P.op("vector", lambda e, ct=ct: e.tensor_scalar(
                    A[:, ct * 2048:(ct + 1) * 2048],
                    A[:, ct * 2048:(ct + 1) * 2048],
                    cols[:, 248 + ct:249 + ct], None, OP.mult),
                    rd=[bcols], wr=[bxm[ct]])
            dbg("mdel0", Ba[:, 0:2048], [bdel[0]])
            dbg("my0", A[:, 0:2048], [bxm[0]])
            P.op("vector", lambda e: e.memset(W[:, 0:2048], 0.0), wr=[bWs])
            for ct in range(ND):
                P.op("vector", lambda e, ct=ct: e.tensor_tensor_scan(
                    W[:, 2048:4096], Ba[:, ct * 2048:(ct + 1) * 2048],
                    W[:, 0:2048], 0.0, OP.add, OP.add),
                    rd=[bdel[ct], bWs], wr=[bWs])
                P.dma("scalar",
                      uspill[ts(ct, 256), :].rearrange(
                          "(p x) d -> p (x d)", p=128),
                      W[:, 2048:4096], rd=[bWs], wr=[buspb])
            for ct in range(ND):
                P.dma("sync", Cn[:, 2112 + ct * 16:2112 + ct * 16 + 16],
                      di["mb_alog"][ts(ct), :], wr=[bCnk])
            P.op("scalar", lambda e: e.activation(
                Cn[:, 2112:2240], Cn[:, 2112:2240], AF.Exp),
                rd=[bCnk], wr=[bCnk])
            P.op("vector", lambda e: e.tensor_scalar(
                Cn[:, 2112:2240], Cn[:, 2112:2240], -1.0, None, OP.mult),
                rd=[bCnk], wr=[bCnk])

            for n in range(16):
                P.dma("sync", W[:, 4096:6144],
                      bcsp[n:n + 1, :].partition_broadcast(128),
                      rd=[bbcs], wr=[bWs])
                P.dma("sync", W[:, 6144:8192],
                      bcsp[16 + n:17 + n, :].partition_broadcast(128),
                      rd=[bbcs], wr=[bWs])
                for ct in range(ND):
                    acol = Cn[:, 2112 + ct * 16 + n:2112 + ct * 16 + n + 1]
                    P.op("scalar", lambda e, ct=ct, acol=acol: e.activation(
                        W[:, 0:2048], Ba[:, ct * 2048:(ct + 1) * 2048],
                        AF.Exp, scale=acol),
                        rd=[bdel[ct], bCnk], wr=[bWs])
                    P.dma("sync", W[:, 2048:4096],
                          ospill[ts(ct, 256), :].rearrange(
                              "(p x) d -> p (x d)", p=128),
                          rd=[bosp], wr=[bWs])
                    P.op("vector", lambda e, ct=ct: e.tensor_tensor(
                        W[:, 2048:4096], W[:, 2048:4096],
                        W[:, 4096:6144], OP.mult), rd=[bWs], wr=[bWs])
                    P.op("vector", lambda e: e.tensor_tensor_scan(
                        Cn[:, 0:2048], W[:, 0:2048], W[:, 2048:4096], 0.0,
                        OP.mult, OP.add), rd=[bWs], wr=[bCns])
                    P.op("vector", lambda e, ct=ct, n=n: e.tensor_copy(
                        Cn[:, 2368 + ct * 16 + n:2368 + ct * 16 + n + 1],
                        Cn[:, 2047:2048]), rd=[bCns], wr=[bCnk])
                    P.op("vector", lambda e: e.tensor_tensor(
                        Cn[:, 0:2048], Cn[:, 0:2048], W[:, 6144:8192],
                        OP.mult), rd=[bCns, bWs], wr=[bCns])
                    P.op("vector", lambda e, ct=ct: e.tensor_tensor(
                        A[:, ct * 2048:(ct + 1) * 2048],
                        A[:, ct * 2048:(ct + 1) * 2048],
                        Cn[:, 0:2048], OP.add), rd=[bCns], wr=[bxm[ct]])
            dbg("mys", A[:, 0:2048], [bxm[0]])
            dbg("mhend", Cn[:, 2368:2496], [bCnk])
            for ct in range(ND):
                P.dma("scalar", cc0_in[ts(ct), :],
                      Cn[:, 2368 + ct * 16:2368 + ct * 16 + 16],
                      rd=[bCnk], wr=[bh0_i])
            P.op("gpsimd", lambda e: e.collective_compute(
                "AllGather", OP.bypass, replica_groups=groups,
                ins=[cc0_in.opt()], outs=[cc0_out.opt()]),
                rd=[bh0_i], wr=[bh0_o])
            for ct in range(ND):
                P.dma("sync", Cn[:, 2240 + ct * 16:2240 + ct * 16 + 16],
                      cc0_out[ts(ct), :], rd=[bh0_o], wr=[bCnk])
            P.op("vector", lambda e: e.tensor_scalar(
                Cn[:, 2240:2368], Cn[:, 2240:2368], modd_c, None, OP.mult),
                rd=[bcols, bCnk], wr=[bCnk])
            for n in range(16):
                P.dma("sync", W[:, 6144:8192],
                      bcsp[16 + n:17 + n, :].partition_broadcast(128),
                      rd=[bbcs], wr=[bWs])
                for ct in range(ND):
                    P.dma("sync", W[:, 0:2048],
                          uspill[ts(ct, 256), :].rearrange(
                              "(p x) d -> p (x d)", p=128),
                          rd=[buspb], wr=[bWs])
                    acol = Cn[:, 2112 + ct * 16 + n:2112 + ct * 16 + n + 1]
                    P.op("scalar", lambda e, acol=acol: e.activation(
                        W[:, 2048:4096], W[:, 0:2048], AF.Exp, scale=acol),
                        rd=[bWs, bCnk], wr=[bWs])
                    h0c = Cn[:, 2240 + ct * 16 + n:2240 + ct * 16 + n + 1]
                    P.op("vector", lambda e, h0c=h0c: e.scalar_tensor_tensor(
                        W[:, 0:2048], W[:, 2048:4096], h0c, W[:, 6144:8192],
                        OP.mult, OP.mult), rd=[bWs, bCnk], wr=[bWs])
                    P.op("vector", lambda e, ct=ct: e.tensor_tensor(
                        A[:, ct * 2048:(ct + 1) * 2048],
                        A[:, ct * 2048:(ct + 1) * 2048],
                        W[:, 0:2048], OP.add), rd=[bWs], wr=[bxm[ct]])

            load_weight("mb_opT", csl=slice(0, 1024), rsl=slice(0, 1024))
            for tt in range(NT):
                for oc in range(2):
                    p, pb = pacq()
                    for dc in range(ND):
                        P.op("tensor", lambda e, dc=dc, tt=tt, oc=oc, p=p:
                             e.matmul(
                                 p[:],
                                 A[:, dc * 2048 + tt * 128:
                                   dc * 2048 + tt * 128 + 128],
                                 W[:, dc * 1024 + oc * 512:
                                   dc * 1024 + oc * 512 + 512],
                                 start=(dc == 0), stop=(dc == ND - 1)),
                             rd=[bW, bxm[dc]], wr=[pb])
                    P.op("vector", lambda e, oc=oc, p=p: e.tensor_copy(
                        Sc[:, oc * 512:oc * 512 + 512], p[:]), rd=[pb],
                        wr=[bSc[0]])
                P.dma("scalar", ospill[ts(tt), :], Sc[:, 0:1024],
                      rd=[bSc[0]], wr=[bosp])
            bzc = []
            for ct in range(ND):
                bzc.append(fresh([bdel[ct]]))
                P.dma("sync", Ba[:, ct * 2048:(ct + 1) * 2048],
                      tspill[ts(ct, 256), :].rearrange(
                          "(p x) d -> p (x d)", p=128),
                      rd=[btsp], wr=[bzc[ct]])
            load_weight("mb_opT", csl=slice(0, 1024), rsl=slice(1024, 2048))
            for tt in range(NT):
                bA[tt] = fresh([bxm[tt // 2], bA[tt]])
                P.dma("sync", A_t(tt), xspill[ts(tt), :], rd=[bxsp],
                      wr=[bA[tt]])
                P.dma("sync", Sc[:, 1024:2048], ospill[ts(tt), :],
                      rd=[bosp], wr=[bSc[1]])
                for oc in range(2):
                    p, pb = pacq()
                    for dc in range(ND):
                        P.op("tensor", lambda e, dc=dc, tt=tt, oc=oc, p=p:
                             e.matmul(
                                 p[:],
                                 Ba[:, dc * 2048 + tt * 128:
                                   dc * 2048 + tt * 128 + 128],
                                 W[:, dc * 1024 + oc * 512:
                                   dc * 1024 + oc * 512 + 512],
                                 start=(dc == 0), stop=(dc == ND - 1)),
                             rd=[bW, bzc[dc]], wr=[pb])
                    P.op("vector", lambda e, oc=oc, p=p: e.tensor_copy(
                        Sc[:, oc * 512:oc * 512 + 512], p[:]), rd=[pb],
                        wr=[bSc[0]])
                P.op("vector", lambda e, tt=tt: e.tensor_tensor(
                    A_t(tt), A_t(tt), Sc[:, 1024:2048], OP.add),
                    rd=[bSc[1]], wr=[bA[tt]])
                P.op("vector", lambda e, tt=tt: e.tensor_tensor(
                    A_t(tt), A_t(tt), Sc[:, 0:1024], OP.add),
                    rd=[bSc[0]], wr=[bA[tt]])
            for j in range(ND):
                bBa[j] = fresh([bzc[j], bBa[j]])
            for i in range(16):
                bBs[i] = fresh([bBa[i // 2]])
            ba_state[0] = "slots"

        emitters = {"l0a": emit_l0a, "l0f": emit_l0ffn, "l1f": emit_l1f,
                    "l0b": emit_l0b, "l1a": emit_l1a, "mb": emit_mb}
        for sl in sublayers:
            if sl in emitters:
                emitters[sl]()

        st = []
        for i in range(NT):
            st.append(P.dma("scalar", xout[ts(i), :], A_t(i), rd=[bA[i]]))
        P.waits("sync", st)

        with nc.Block() as block:
            P.replay(block)

    return nc



def _prep_inputs(I):
    c = np.ascontiguousarray
    base = {
        "l0a_wkT": c(I["L0_attn_a_Wk"].T), "l0a_wvT": c(I["L0_attn_a_Wv"].T),
        "l0a_wqT": c(I["L0_attn_a_Wq"].T), "l0a_wkpT": c(I["L0_attn_a_Wkp"].T),
        "l0a_wqpT": c(I["L0_attn_a_Wqp"].T), "l0a_woT": c(I["L0_attn_a_Wo"].T),
        "l0a_bk": I["L0_attn_a_bk"].reshape(1, D), "l0a_bv": I["L0_attn_a_bv"].reshape(1, D),
        "l0a_bq": I["L0_attn_a_bq"].reshape(1, D), "l0a_bkp": I["L0_attn_a_bkp"].reshape(1, D),
        "l0a_bqp": I["L0_attn_a_bqp"].reshape(1, D), "l0a_bo": I["L0_attn_a_bo"].reshape(1, D),
        "l0a_ps": I["L0_attn_a_phase_scale"].reshape(1, D),
        "l0a_lng": I["L0_attn_a_ln_g"].reshape(1, D),
        "l0a_lnb": I["L0_attn_a_ln_b"].reshape(1, D),
        "l0f_wT": c(I["L0_ffn_weight"].T),
        "l0b_wqT": c(I["L0_attn_b_Wq"].T), "l0b_wkT": c(I["L0_attn_b_Wk"].T),
        "l0b_wvT": c(I["L0_attn_b_Wv"].T), "l0b_woT": c(I["L0_attn_b_Wo"].T),
        "l0b_bq": I["L0_attn_b_bq"].reshape(1, D), "l0b_bk": I["L0_attn_b_bk"].reshape(1, D),
        "l0b_bv": I["L0_attn_b_bv"].reshape(1, D), "l0b_bo": I["L0_attn_b_bo"].reshape(1, D),
        "l1a_wq": c(I["L1_attn_a_wq"]), "l1a_wk": c(I["L1_attn_a_wk"]),
        "l1a_wv": c(I["L1_attn_a_wv"]),
        "l1a_lns": I["L1_attn_a_ln_scale"].reshape(1, D),
        "l1a_lnb": I["L1_attn_a_ln_bias"].reshape(1, D),
        "l1f_wT": c(I["L1_ffn_weight"].T),
        "l1f_gwT": c(I["L1_ffn_gate_w"].T),
        "l1f_uT": c(I["L1_ffn_lora_U"].transpose(2, 0, 1).reshape(D, 128)),
        "l1f_v": c(I["L1_ffn_lora_V"].transpose(0, 2, 1).reshape(128, D)),
        "mb_ipT": c(I["L1_attn_b_in_proj_w"].T),
        "mb_xpT": c(I["L1_attn_b_x_proj_w"].T),
        "mb_dtT": c(I["L1_attn_b_dt_proj_w"].T),
        "mb_opT": c(I["L1_attn_b_out_proj_w"].T),
        "mb_cx": c(I["L1_attn_b_conv_x_w"].reshape(D, 4)),
        "mb_cz": c(I["L1_attn_b_conv_z_w"].reshape(D, 4)),
        "mb_alog": c(I["L1_attn_b_A_log"]),
        "mb_dvec": I["L1_attn_b_D"].reshape(D, 1),
        "mb_dtb": I["L1_attn_b_dt_proj_b"].reshape(D, 1),
    }
    base = {k: np.ascontiguousarray(v, dtype=np.float32) for k, v in base.items()}
    x = np.ascontiguousarray(I["L0_attn_a_x"], dtype=np.float32)
    in_maps = []
    for core in range(NCORES):
        b, h = core // 2, core % 2
        m = dict(base)
        m["xin"] = np.ascontiguousarray(x[b, h * T:(h + 1) * T, :])
        m["tpos"] = np.arange(h * T, (h + 1) * T, dtype=np.float32).reshape(T, 1)
        m["modd"] = np.full((1, 1), float(h), dtype=np.float32)
        in_maps.append(m)
    return in_maps


_NC_CACHE = {}


def kernel(**inputs):
    in_maps = _prep_inputs(inputs)
    if "nc" not in _NC_CACHE:
        _NC_CACHE["nc"] = build_program(NCORES)
    nc = _NC_CACHE["nc"]
    res = run_bass_kernel_spmd(nc, in_maps, core_ids=list(range(NCORES)))
    out = np.empty((B, S, D), dtype=np.float32)
    for core in range(NCORES):
        b, h = core // 2, core % 2
        out[b, h * T:(h + 1) * T, :] = res.results[core]["xout"]
    return out


# version 05 (final)
# speedup vs baseline: 1.3535x; 1.0326x over previous
import contextlib

import numpy as np

import concourse.bass as bass
import concourse.mybir as mybir
from concourse.bass_utils import run_bass_kernel_spmd

F32 = mybir.dt.float32
BF16 = mybir.dt.bfloat16
I32 = mybir.dt.int32
AF = mybir.ActivationFunctionType
OP = mybir.AluOpType
AX = mybir.AxisListType.X

NCORES = 8
B, S, D = 4, 4096, 1024
T = 2048
NT = T // 128
ND = D // 128
H, HD = 16, 64

ENGS = ("sync", "tensor", "vector", "scalar", "gpsimd")

SUBLAYERS = ("l0a", "l0f", "l0b", "l0f", "l1a", "l1f", "mb", "l1f")
DEBUG_TAPS = ()


def ts(i, sz=128):
    return slice(i * sz, (i + 1) * sz)


class Buf:
    __slots__ = ("w", "rs")

    def __init__(self):
        self.w = None
        self.rs = []


class Prog:
    def __init__(self, nc):
        self.nc = nc
        self.cmds = {e: [] for e in ENGS}
        self.sems = {}
        self.cnt = {}
        self.qwait = {}
        self.qgate = {}
        self.waited = {e: {} for e in ENGS}
        self.stk = None

    def sem(self, name):
        h = self.stk.enter_context(self.nc.semaphore(name))
        self.sems[name] = h
        self.cnt[name] = 0

    def _edges(self, eng, rd, wr, extra):
        toks = list(extra)
        for b in rd:
            toks.append(b.w)
        for b in wr:
            toks.append(b.w)
            toks.extend(b.rs)
        self.waits(eng, toks)

    def _done(self, rd, wr, tok):
        for b in rd:
            b.rs.append(tok)
        for b in wr:
            b.w = tok
            b.rs = []

    def dma(self, q, out, in_, rd=(), wr=(), extra=(), **kw):
        self._edges(q, rd, wr, extra)
        sem = "q_" + q
        w = self.qwait.get(sem, 0)
        if w > self.qgate.get(sem, 0):
            if self.waited[q].get(sem, 0) < w:
                s0 = self.sems[sem]

                def g(e, s0=s0, w=w):
                    e.wait_ge(s0, w)

                self.cmds[q].append(g)
                self.waited[q][sem] = w
            self.qgate[sem] = w
        self.cnt[sem] += 16
        v = self.cnt[sem]
        s = self.sems[sem]

        def f(e, out=out, in_=in_, s=s, kw=kw):
            e.dma_start(out=out, in_=in_, **kw).then_inc(s, 16)

        self.cmds[q].append(f)
        tok = (sem, v)
        self._done(rd, wr, tok)
        return tok

    def op(self, eng, fn, rd=(), wr=(), extra=()):
        self._edges(eng, rd, wr, extra)
        sem = "t_" + eng
        self.cnt[sem] += 1
        v = self.cnt[sem]
        s = self.sems[sem]

        def f(e, fn=fn, s=s):
            fn(e).then_inc(s, 1)

        self.cmds[eng].append(f)
        tok = (sem, v)
        self._done(rd, wr, tok)
        return tok

    def waits(self, eng, toks):
        best = {}
        wd = self.waited[eng]
        for t in toks:
            if t is None:
                continue
            sem, v = t
            if sem.startswith("q_"):
                v = self.cnt[sem]
                if self.qwait.get(sem, 0) < v:
                    self.qwait[sem] = v
            if best.get(sem, -1) < v:
                best[sem] = v
        for sem in sorted(best):
            v = best[sem]
            if wd.get(sem, 0) >= v:
                continue
            wd[sem] = v
            s = self.sems[sem]

            def f(e, s=s, v=v):
                e.wait_ge(s, v)

            self.cmds[eng].append(f)

    def replay(self, block):
        for eng in ENGS:
            cmds = self.cmds[eng]

            def runner(e, cmds=cmds):
                for fn in cmds:
                    fn(e)

            getattr(block, eng)(runner)



def build_program(n_ranks=NCORES, sublayers=SUBLAYERS):
    nc = bass.Bass("TRN2", target_bir_lowering=False)
    di = {}

    def inp(name, shape):
        di[name] = nc.dram_tensor(name, shape, F32, kind="ExternalInput").ap()

    inp("xin", [T, D])
    inp("tpos", [T, 1])
    inp("modd", [1, 1])
    for w in ("l0a_wkT", "l0a_wvT", "l0a_wqT", "l0a_wkpT", "l0a_wqpT", "l0a_woT"):
        inp(w, [D, D])
    for v in ("l0a_bk", "l0a_bv", "l0a_bq", "l0a_bkp", "l0a_bqp", "l0a_bo",
              "l0a_ps", "l0a_lng", "l0a_lnb"):
        inp(v, [1, D])
    inp("l0f_wT", [D, D])
    for w in ("l0b_wqT", "l0b_wkT", "l0b_wvT", "l0b_woT"):
        inp(w, [D, D])
    for v in ("l0b_bq", "l0b_bk", "l0b_bv", "l0b_bo"):
        inp(v, [1, D])
    for w in ("l1a_wq", "l1a_wk", "l1a_wv"):
        inp(w, [D, D])
    inp("l1a_lns", [1, D])
    inp("l1a_lnb", [1, D])
    inp("l1f_wT", [D, D])
    inp("l1f_gwT", [D, 8])
    inp("l1f_uT", [D, 128])
    inp("l1f_v", [128, D])
    inp("mb_ipT", [D, 2048])
    inp("mb_xpT", [D, 96])
    inp("mb_dtT", [64, D])
    inp("mb_opT", [2048, D])
    inp("mb_cx", [D, 4])
    inp("mb_cz", [D, 4])
    inp("mb_alog", [D, 16])
    inp("mb_dvec", [D, 1])
    inp("mb_dtb", [D, 1])

    xout = nc.dram_tensor("xout", [T, D], F32, kind="ExternalOutput").ap()

    xspill = nc.dram_tensor("xspill", [T, D], F32).ap()
    tspill = nc.dram_tensor("tspill", [T, D], F32).ap()
    uspill = nc.dram_tensor("uspill", [T, D], F32).ap()
    ospill = nc.dram_tensor("ospill", [T, D], F32).ap()
    xnspill = nc.dram_tensor("xnspill", [T, D], F32).ap()
    denspill = nc.dram_tensor("denspill", [8, 2048], F32).ap()
    ccx_in = [nc.dram_tensor(f"ccx{j}_in", [D, T], F32).ap() for j in range(2)]
    cch_in = nc.dram_tensor("cch_in", [4, 2048], F32).ap()
    cch_out = nc.dram_tensor("cch_out", [8, 2048], F32).ap()
    cc0_in = nc.dram_tensor("cc0_in", [D, 16], F32).ap()
    bcsp = nc.dram_tensor("bcsp", [32, 2048], F32).ap()
    cc0_out = nc.dram_tensor("cc0_out", [2 * D, 16], F32).ap()
    ccx_out = [nc.dram_tensor(f"ccx{j}_out", [2 * D, T], F32).ap()
               for j in range(2)]
    groups = [[2 * i, 2 * i + 1] for i in range(max(1, n_ranks // 2))]

    P = Prog(nc)
    dbg_n = [0]

    def dbg(name, ap_fn, rd):
        if name not in DEBUG_TAPS:
            return
        shp = [ap_fn.shape[0], ap_fn.shape[1]]
        o = nc.dram_tensor("dbg_" + name + f"_{dbg_n[0]}", shp, F32,
                           kind="ExternalOutput").ap()
        dbg_n[0] += 1
        P.dma("scalar", o, ap_fn, rd=rd)

    with contextlib.ExitStack() as stk:
        P.stk = stk
        for s in ("q_sync", "q_scalar", "q_gpsimd", "q_vector",
                  "t_sync", "t_tensor", "t_vector", "t_scalar", "t_gpsimd"):
            P.sem(s)

        def sbuf(name, shape, dt=F32):
            return stk.enter_context(nc.sbuf_tensor(name, shape, dt))

        A = sbuf("A", [128, 16384])
        Ba = sbuf("Ba", [128, 16384])
        W = sbuf("W", [128, 8192])
        Sc = sbuf("Sc", [128, 2048])
        Cn = sbuf("Cn", [128, 3584])
        cols = sbuf("cols", [128, 512])
        ident = sbuf("ident", [128, 128])
        ident16 = sbuf("ident16", [128, 128], BF16)
        tri = sbuf("tri", [128, 128])
        rowb = sbuf("rowb", [128, 128])
        wq16 = sbuf("wq16", [128, 8192], BF16)
        aq16 = sbuf("aq16", [128, 1024], BF16)
        aqf16 = sbuf("aqf16", [128, 1024], BF16)
        rint = sbuf("rint", [128, 512], I32)

        psum = [stk.enter_context(nc.psum_tensor(f"p{i}", [128, 512], F32))
                for i in range(8)]
        pB = [Buf() for _ in range(8)]
        prr = [0]

        def pacq():
            i = prr[0]
            prr[0] = (prr[0] + 1) % 8
            return psum[i], pB[i]

        bA = [Buf() for _ in range(NT)]
        bAf = [Buf() for _ in range(ND)]
        bBa = [Buf() for _ in range(ND)]
        bW = Buf()
        bSc = [Buf() for _ in range(2)]
        bCn = [Buf() for _ in range(3)]
        bCn3 = Buf()
        bcols = Buf()
        bident = Buf()
        bident16 = Buf()
        btri = Buf()
        bwq = Buf()
        baq = Buf()
        baqf = Buf()
        brint = Buf()

        def A_t(i):
            return A[:, ts(i, 1024)]

        def A_f(j):
            return A[:, ts(j, 2048)]

        def Ba_t(j):
            return Ba[:, ts(j, 2048)]

        bBs = [Buf() for _ in range(16)]

        def Bs_t(i):
            return Ba[:, ts(i, 1024)]

        ba_state = ["slots"]

        def ba_ensure_slots():
            if ba_state[0] != "slots":
                ba_alias_fm2slots()

        def ba_ensure_fm():
            if ba_state[0] != "fm":
                ba_alias_slots2fm()

        def ba_alias_fm2slots():
            ba_state[0] = "slots"
            for i in range(16):
                j = i // 2
                rs = []
                if bBa[j].w is not None:
                    rs.append(bBa[j].w)
                rs.extend(bBa[j].rs)
                bBs[i].w = None
                bBs[i].rs = rs

        def ba_alias_slots2fm():
            ba_state[0] = "fm"
            for j in range(ND):
                rs = []
                for i in (2 * j, 2 * j + 1):
                    if bBs[i].w is not None:
                        rs.append(bBs[i].w)
                    rs.extend(bBs[i].rs)
                bBa[j].w = None
                bBa[j].rs = rs

        def Sc_t(i):
            return Sc[:, ts(i, 1024)]

        def Cn_t(i):
            return Cn[:, ts(i, 1024)]

        def a_alias_tm2fm():
            for j in range(ND):
                rs = []
                w = None
                for i in (2 * j, 2 * j + 1):
                    if bA[i].w is not None:
                        rs.append(bA[i].w)
                    rs.extend(bA[i].rs)
                bAf[j].w = None
                bAf[j].rs = rs

        def a_alias_fm2tm():
            for i in range(NT):
                j = i // 2
                rs = []
                if bAf[j].w is not None:
                    rs.append(bAf[j].w)
                rs.extend(bAf[j].rs)
                bA[i].w = None
                bA[i].rs = rs

        ones_c = nc.const_aps.tensor(1.0, [128, 128], F32)
        kconst = sbuf("kconst", [128, 16])
        bkconst = Buf()
        _kc_map = {}
        _kc_next = [0]

        def KC(v):
            v = float(v)
            if v not in _kc_map:
                j = _kc_next[0]
                _kc_next[0] += 1
                P.op("vector", lambda e, j=j, v=v: e.memset(
                    kconst[:, j:j + 1], v), wr=[bkconst])
                _kc_map[v] = kconst[:, j:j + 1]
            return _kc_map[v]
        P.op("gpsimd", lambda e: e.affine_select(
            ident[:], ones_c, pattern=[[1, 128]], compare_op=OP.is_equal,
            fill=0.0, base=0, channel_multiplier=-1), wr=[bident])
        P.op("vector", lambda e: e.tensor_copy(ident16[:], ident[:]),
             rd=[bident], wr=[bident16])
        P.op("vector", lambda e: e.memset(tri[:], 0.0), wr=[btri])
        for _blk in range(2):
            P.op("gpsimd", lambda e, _blk=_blk: e.affine_select(
                tri[ts(_blk, 64), ts(_blk, 64)], ones_c[0:64, 0:64],
                pattern=[[1, 64]], compare_op=OP.is_ge, fill=0.0, base=0,
                channel_multiplier=-1), rd=[btri], wr=[btri])

        def transpose_f32(src_ap, dst_ap, rd, wr):
            p, pb = pacq()
            P.op("tensor", lambda e, p=p: e.transpose(
                p[:, 0:128], src_ap, ident[:]), rd=[*rd, bident], wr=[pb])
            return P.op("vector", lambda e, p=p: e.tensor_copy(
                dst_ap, p[:, 0:128]), rd=[pb], wr=list(wr))

        def transpose_bf16(src_ap, dst_ap, rd, wr):
            p, pb = pacq()
            P.op("tensor", lambda e, p=p: e.transpose(
                p[:].bitcast(BF16)[:, 0:128], src_ap, ident16[:]),
                rd=[*rd, bident16], wr=[pb])
            return P.op("vector", lambda e, p=p: e.tensor_copy(
                dst_ap, p[:].bitcast(BF16)[:, 0:128]), rd=[pb], wr=list(wr))

        for i in range(NT):
            P.dma("sync", A_t(i), di["xin"][ts(i), :], wr=[bA[i]])

        browb = Buf()

        def load_cols(dst_cols_ap, name):
            P.dma("sync", rowb[0:8, :],
                  di[name].rearrange("o (a f) -> (o a) f", a=8), wr=[browb])
            p, pb = pacq()
            P.op("tensor", lambda e, p=p: e.transpose(
                p[:, 0:8], rowb[0:8, :], ident[0:8, 0:8]),
                rd=[browb, bident], wr=[pb])
            return P.op("vector", lambda e, p=p: e.tensor_copy(
                dst_cols_ap, p[:, 0:8]), rd=[pb], wr=[bcols])

        def bcast_row(slot, name):
            return P.dma("sync", Cn_t(slot),
                         di[name].partition_broadcast(128), wr=[bCn[slot]])

        def load_weight(name, csl=None, woff=0, rsl=None):
            src = di[name]
            if rsl is None:
                rsl = slice(0, src.shape[0])
            nk = (rsl.stop - rsl.start) // 128
            if csl is None:
                csl = slice(0, src.shape[1])
            ncols = csl.stop - csl.start
            for kc in range(nk):
                r0 = rsl.start + kc * 128
                P.dma("sync", W[:, woff + kc * ncols:woff + (kc + 1) * ncols],
                      src[r0:r0 + 128, csl], wr=[bW])

        def Wk_(kc, ncols=D):
            return W[:, kc * ncols:(kc + 1) * ncols]

        def emit_ln(eps=1e-5, fold2=None):
            for i in range(NT):
                P.op("vector", lambda e, i=i: e.reduce_sum(
                    cols[:, i:i + 1], A_t(i), axis=AX), rd=[bA[i]], wr=[bcols])
                P.op("scalar", lambda e, i=i: e.activation(
                    Sc_t(0), A_t(i), AF.Square, accum_out=cols[:, 16 + i:17 + i]),
                    rd=[bA[i]], wr=[bSc[0], bcols])
            P.op("vector", lambda e: e.tensor_scalar(
                cols[:, 32:48], cols[:, 0:16], 1.0 / D, None, OP.mult),
                rd=[bcols], wr=[bcols])
            P.op("vector", lambda e: e.tensor_scalar(
                cols[:, 112:128], cols[:, 16:32], 1.0 / D, None, OP.mult),
                rd=[bcols], wr=[bcols])
            P.op("vector", lambda e: e.tensor_tensor(
                cols[:, 96:112], cols[:, 32:48], cols[:, 32:48], OP.mult),
                rd=[bcols], wr=[bcols])
            P.op("vector", lambda e: e.tensor_tensor(
                cols[:, 48:64], cols[:, 112:128], cols[:, 96:112], OP.subtract),
                rd=[bcols], wr=[bcols])
            _b = KC(eps)
            P.op("scalar", lambda e, _b=_b: e.activation(
                cols[:, 96:112], cols[:, 48:64], AF.Ln, bias=_b),
                rd=[bcols, bkconst], wr=[bcols])
            if fold2 is None:
                P.op("scalar", lambda e: e.activation(
                    cols[:, 64:80], cols[:, 96:112], AF.Exp, scale=-0.5),
                    rd=[bcols], wr=[bcols])
            else:
                P.op("scalar", lambda e: e.activation(
                    cols[:, 128:144], cols[:, 96:112], AF.Exp, scale=-0.5),
                    rd=[bcols], wr=[bcols])
                P.op("vector", lambda e: e.tensor_tensor(
                    cols[:, 144:160], cols[:, 128:144], cols[:, 128:144],
                    OP.mult), rd=[bcols], wr=[bcols])
                P.op("vector", lambda e: e.tensor_tensor(
                    cols[:, 160:176], cols[:, 48:64], cols[:, 144:160],
                    OP.mult), rd=[bcols], wr=[bcols])
                _b2 = KC(fold2)
                P.op("scalar", lambda e, _b2=_b2: e.activation(
                    cols[:, 176:192], cols[:, 160:176], AF.Ln,
                    bias=_b2), rd=[bcols, bkconst], wr=[bcols])
                P.op("scalar", lambda e: e.activation(
                    cols[:, 144:160], cols[:, 176:192], AF.Exp, scale=-0.5),
                    rd=[bcols], wr=[bcols])
                P.op("vector", lambda e: e.tensor_tensor(
                    cols[:, 64:80], cols[:, 128:144], cols[:, 144:160],
                    OP.mult), rd=[bcols], wr=[bcols])
            P.op("vector", lambda e: e.tensor_tensor(
                cols[:, 96:112], cols[:, 32:48], cols[:, 64:80], OP.mult),
                rd=[bcols], wr=[bcols])
            P.op("vector", lambda e: e.tensor_scalar(
                cols[:, 80:96], cols[:, 96:112], -1.0, None, OP.mult),
                rd=[bcols], wr=[bcols])

        def ln_s(i):
            return cols[:, 64 + i:65 + i]

        def ln_b(i):
            return cols[:, 80 + i:81 + i]

        def emit_l0a():
            ba_ensure_fm()
            emit_ln()
            for i in range(NT):
                P.op("scalar", lambda e, i=i: e.activation(
                    Sc_t(0), A_t(i), AF.Identity, bias=ln_b(i), scale=ln_s(i)),
                    rd=[bA[i], bcols], wr=[bSc[0]])
                for kc in range(ND):
                    transpose_f32(Sc_t(0)[:, ts(kc)], Ba_t(kc)[:, ts(i)],
                                  rd=[bSc[0]], wr=[bBa[kc]])
                P.dma("scalar", xspill[ts(i), :], A_t(i), rd=[bA[i]])
            a_alias_tm2fm()
            dbg("xnfm0", Ba_t(0)[:, 0:1024], [bBa[0]])

            bias_cols = cols[:, 448:456]

            def proj_to_fm(dst_f, dst_b, bias_name):
                load_cols(bias_cols, bias_name)
                for ec in range(ND):
                    for tc in range(4):
                        p, pb = pacq()
                        for dc in range(ND):
                            P.op("tensor", lambda e, dc=dc, ec=ec, tc=tc, p=p:
                                 e.matmul(p[:], Wk_(dc)[:, ts(ec)],
                                          Ba_t(dc)[:, ts(tc, 512)],
                                          start=(dc == 0), stop=(dc == ND - 1)),
                                 rd=[bW, bBa[dc]], wr=[pb])
                        P.op("scalar", lambda e, ec=ec, tc=tc, p=p:
                             e.activation(dst_f(ec)[:, ts(tc, 512)], p[:],
                                          AF.Identity,
                                          bias=bias_cols[:, ec:ec + 1]),
                             rd=[pb, bcols], wr=[dst_b[ec]])

            def phase_to_hbm(dst_dram, bias_name):
                bcast_row(0, bias_name)
                for i in range(NT):
                    for oc in range(2):
                        p, pb = pacq()
                        for ec in range(ND):
                            P.op("tensor", lambda e, ec=ec, i=i, oc=oc, p=p:
                                 e.matmul(p[:], A_f(ec)[:, ts(i)],
                                          Wk_(ec)[:, ts(oc, 512)],
                                          start=(ec == 0), stop=(ec == ND - 1)),
                                 rd=[bW, bAf[ec]], wr=[pb])
                        P.op("vector", lambda e, oc=oc, p=p: e.tensor_tensor(
                            Sc_t(0)[:, ts(oc, 512)], p[:],
                            Cn_t(0)[:, ts(oc, 512)], OP.add),
                            rd=[pb, bCn[0]], wr=[bSc[0]])
                    P.op("scalar", lambda e: e.activation(
                        Sc_t(1), Sc_t(0), AF.Tanh), rd=[bSc[0]], wr=[bSc[1]])
                    P.op("vector", lambda e: e.tensor_tensor(
                        Sc_t(0), Sc_t(1), Cn_t(1), OP.mult),
                        rd=[bSc[1], bCn[1]], wr=[bSc[0]])
                    P.dma("scalar", dst_dram[ts(i), :], Sc_t(0), rd=[bSc[0]])

            bcast_row(1, "l0a_ps")
            load_weight("l0a_wkT")
            proj_to_fm(A_f, bAf, "l0a_bk")
            dbg("keyfm0", A_f(0)[:, 0:1024], [bAf[0]])
            load_weight("l0a_wkpT")
            phase_to_hbm(tspill, "l0a_bkp")
            load_weight("l0a_wqT")
            proj_to_fm(A_f, bAf, "l0a_bq")
            load_weight("l0a_wqpT")
            phase_to_hbm(uspill, "l0a_bqp")
            a_alias_fm2tm()
            load_weight("l0a_wvT")
            bcast_row(0, "l0a_bv")
            for i in range(NT):
                for oc in range(2):
                    p, pb = pacq()
                    for dc in range(ND):
                        P.op("tensor", lambda e, dc=dc, i=i, oc=oc, p=p:
                             e.matmul(p[:], Ba_t(dc)[:, ts(i)],
                                      Wk_(dc)[:, ts(oc, 512)],
                                      start=(dc == 0), stop=(dc == ND - 1)),
                             rd=[bW, bBa[dc]], wr=[pb])
                    P.op("vector", lambda e, i=i, oc=oc, p=p: e.tensor_tensor(
                        A_t(i)[:, ts(oc, 512)], p[:], Cn_t(0)[:, ts(oc, 512)],
                        OP.add), rd=[pb, bCn[0]], wr=[bA[i]])

            ba_alias_fm2slots()
            bcast_row(1, "l0a_lng")
            bcast_row(2, "l0a_lnb")
            load_weight("l0a_woT")
            bcast_row(0, "l0a_bo")
            for i in range(NT):
                P.dma("sync", Bs_t(1), tspill[ts(i), :], wr=[bBs[1]])
                P.dma("sync", Bs_t(2), uspill[ts(i), :], wr=[bBs[2]])
                P.dma("sync", Bs_t(9), xspill[ts(i), :], wr=[bBs[9]])
                if i == 0:
                    dbg("kp0", Bs_t(1), [bBs[1]])
                    dbg("qp0", Bs_t(2), [bBs[2]])
                    dbg("val0", A_t(0), [bA[0]])
                P.op("scalar", lambda e: e.activation(
                    Bs_t(3), Bs_t(1), AF.Sin, scale=0.5), rd=[bBs[1]],
                    wr=[bBs[3]])
                P.op("scalar", lambda e: e.activation(
                    Bs_t(3), Bs_t(3), AF.Square), rd=[bBs[3]], wr=[bBs[3]])
                P.op("vector", lambda e: e.tensor_scalar(
                    Bs_t(3), Bs_t(3), -2.0, 1.0, OP.mult, OP.add),
                    rd=[bBs[3]], wr=[bBs[3]])
                P.op("scalar", lambda e: e.activation(
                    Bs_t(4), Bs_t(1), AF.Sin), rd=[bBs[1]], wr=[bBs[4]])
                P.op("vector", lambda e, i=i: e.tensor_tensor(
                    Bs_t(3), A_t(i), Bs_t(3), OP.mult), rd=[bA[i], bBs[3]],
                    wr=[bBs[3]])
                P.op("vector", lambda e, i=i: e.tensor_tensor(
                    Bs_t(4), A_t(i), Bs_t(4), OP.mult), rd=[bA[i], bBs[4]],
                    wr=[bBs[4]])
                for (sslot, dslot) in ((3, 5), (4, 6)):
                    for oc in range(2):
                        p, pb = pacq()
                        P.op("tensor", lambda e, sslot=sslot, oc=oc, p=p:
                             e.matmul(p[:], tri[:],
                                      Bs_t(sslot)[:, ts(oc, 512)],
                                      start=True, stop=True),
                             rd=[btri, bBs[sslot]], wr=[pb])
                        P.op("vector", lambda e, dslot=dslot, oc=oc, p=p:
                             e.tensor_copy(Bs_t(dslot)[:, ts(oc, 512)], p[:]),
                             rd=[pb], wr=[bBs[dslot]])
                if i == 0:
                    dbg("memr0", Bs_t(5), [bBs[5]])
                P.op("scalar", lambda e: e.activation(
                    Bs_t(3), Bs_t(2), AF.Sin, scale=0.5), rd=[bBs[2], bBs[3]],
                    wr=[bBs[3]])
                P.op("scalar", lambda e: e.activation(
                    Bs_t(3), Bs_t(3), AF.Square), rd=[bBs[3]], wr=[bBs[3]])
                P.op("vector", lambda e: e.tensor_scalar(
                    Bs_t(3), Bs_t(3), -2.0, 1.0, OP.mult, OP.add),
                    rd=[bBs[3]], wr=[bBs[3]])
                P.op("scalar", lambda e: e.activation(
                    Bs_t(4), Bs_t(2), AF.Sin), rd=[bBs[2], bBs[4]], wr=[bBs[4]])
                P.op("vector", lambda e: e.tensor_tensor(
                    Bs_t(3), Bs_t(5), Bs_t(3), OP.mult), rd=[bBs[5], bBs[3]],
                    wr=[bBs[3]])
                P.op("vector", lambda e: e.tensor_tensor(
                    Bs_t(4), Bs_t(6), Bs_t(4), OP.mult), rd=[bBs[6], bBs[4]],
                    wr=[bBs[4]])
                P.op("vector", lambda e: e.tensor_tensor(
                    Bs_t(7), Bs_t(3), Bs_t(4), OP.add), rd=[bBs[3], bBs[4]],
                    wr=[bBs[7]])
                if i == 0:
                    dbg("retr0", Bs_t(7), [bBs[7]])
                P.op("vector", lambda e: e.reduce_sum(
                    cols[:, 200:201], Bs_t(7), axis=AX), rd=[bBs[7]],
                    wr=[bcols])
                P.op("scalar", lambda e: e.activation(
                    Sc_t(0), Bs_t(7), AF.Square, accum_out=cols[:, 201:202]),
                    rd=[bBs[7]], wr=[bSc[0], bcols])
                P.op("vector", lambda e: e.tensor_scalar(
                    cols[:, 202:203], cols[:, 200:201], 1.0 / D, None,
                    OP.mult), rd=[bcols], wr=[bcols])
                P.op("vector", lambda e: e.tensor_scalar(
                    cols[:, 203:204], cols[:, 201:202], 1.0 / D, None,
                    OP.mult), rd=[bcols], wr=[bcols])
                P.op("vector", lambda e: e.tensor_tensor(
                    cols[:, 204:205], cols[:, 202:203], cols[:, 202:203],
                    OP.mult), rd=[bcols], wr=[bcols])
                P.op("vector", lambda e: e.tensor_tensor(
                    cols[:, 204:205], cols[:, 203:204], cols[:, 204:205],
                    OP.subtract), rd=[bcols], wr=[bcols])
                _b3 = KC(1e-5 * D)
                P.op("scalar", lambda e, _b3=_b3: e.activation(
                    cols[:, 205:206], cols[:, 204:205], AF.Ln, bias=_b3),
                    rd=[bcols, bkconst], wr=[bcols])
                P.op("scalar", lambda e: e.activation(
                    cols[:, 205:206], cols[:, 205:206], AF.Exp, scale=-0.5),
                    rd=[bcols], wr=[bcols])
                P.op("vector", lambda e: e.tensor_tensor(
                    cols[:, 206:207], cols[:, 202:203], cols[:, 205:206],
                    OP.mult), rd=[bcols], wr=[bcols])
                P.op("vector", lambda e: e.tensor_scalar(
                    cols[:, 206:207], cols[:, 206:207], -1.0, None, OP.mult),
                    rd=[bcols], wr=[bcols])
                P.op("scalar", lambda e: e.activation(
                    Bs_t(7), Bs_t(7), AF.Identity, bias=cols[:, 206:207],
                    scale=cols[:, 205:206]), rd=[bBs[7], bcols], wr=[bBs[7]])
                P.op("vector", lambda e: e.tensor_tensor(
                    Bs_t(7), Bs_t(7), Cn_t(1), OP.mult), rd=[bBs[7], bCn[1]],
                    wr=[bBs[7]])
                P.op("vector", lambda e: e.tensor_tensor(
                    Bs_t(7), Bs_t(7), Cn_t(2), OP.add), rd=[bBs[7], bCn[2]],
                    wr=[bBs[7]])
                if i == 0:
                    dbg("lnr0", Bs_t(7), [bBs[7]])
                for kc in range(ND):
                    transpose_f32(Bs_t(7)[:, ts(kc)], Bs_t(0)[:, ts(kc)],
                                  rd=[bBs[7]], wr=[bBs[0]])
                for oc in range(2):
                    p, pb = pacq()
                    for dc in range(ND):
                        P.op("tensor", lambda e, dc=dc, oc=oc, p=p: e.matmul(
                            p[:], Bs_t(0)[:, ts(dc)], Wk_(dc)[:, ts(oc, 512)],
                            start=(dc == 0), stop=(dc == ND - 1)),
                            rd=[bW, bBs[0]], wr=[pb])
                    P.op("scalar", lambda e, oc=oc, p=p: e.activation(
                        Bs_t(8)[:, ts(oc, 512)], p[:], AF.Identity),
                        rd=[pb], wr=[bBs[8]])
                P.op("vector", lambda e: e.tensor_tensor(
                    Bs_t(8), Bs_t(8), Cn_t(0), OP.add), rd=[bBs[8], bCn[0]],
                    wr=[bBs[8]])
                P.op("scalar", lambda e, i=i: e.activation(
                    Bs_t(1), Bs_t(9), AF.Identity, bias=ln_b(i),
                    scale=ln_s(i)), rd=[bBs[9], bcols], wr=[bBs[1]])
                P.op("vector", lambda e: e.tensor_tensor(
                    Bs_t(8), Bs_t(8), Bs_t(1), OP.add), rd=[bBs[8], bBs[1]],
                    wr=[bBs[8]])
                P.op("vector", lambda e, i=i: e.tensor_tensor(
                    A_t(i), Bs_t(9), Bs_t(8), OP.add), rd=[bBs[9], bBs[8]],
                    wr=[bA[i]])

        ffn_ready = [False]
        ffn_s127 = cols[:, 496:497]

        def emit_l0ffn_setup():
            load_weight("l0f_wT")
            for kc in range(ND):
                P.op("scalar", lambda e, kc=kc: e.activation(
                    Sc_t(0), Wk_(kc), AF.Abs,
                    accum_out=cols[:, 480 + kc:481 + kc]),
                    rd=[bW], wr=[bSc[0], bcols])
            P.op("vector", lambda e: e.reduce_sum(
                cols[:, 488:489], cols[:, 480:488], axis=AX), rd=[bcols],
                wr=[bcols])
            p, pb = pacq()
            P.op("tensor", lambda e, p=p: e.matmul(
                p[0:1, 0:1], cols[:, 488:489], ones_c[:, 0:1],
                start=True, stop=True), rd=[bcols], wr=[pb])
            P.op("vector", lambda e, p=p: e.tensor_copy(
                cols[0:1, 489:490], p[0:1, 0:1]), rd=[pb], wr=[bcols])
            _b4 = KC(1e-8)
            P.op("scalar", lambda e, _b4=_b4: e.activation(
                cols[0:1, 490:491], cols[0:1, 489:490], AF.Identity,
                scale=1.0 / float(D * D), bias=_b4[0:1, :]),
                rd=[bcols, bkconst], wr=[bcols])
            P.op("vector", lambda e: e.reciprocal(
                cols[0:1, 491:492], cols[0:1, 490:491]), rd=[bcols], wr=[bcols])
            P.op("scalar", lambda e: e.activation(
                cols[0:1, 492:493], cols[0:1, 490:491], AF.Identity,
                scale=1.0 / 127.0), rd=[bcols], wr=[bcols])
            p2, pb2 = pacq()
            P.op("tensor", lambda e, p=p2: e.matmul(
                p[:, 0:2], ones_c[0:1, :], cols[0:1, 491:493],
                start=True, stop=True), rd=[bcols], wr=[pb2])
            P.op("vector", lambda e, p=p2: e.tensor_copy(
                cols[:, 493:494], p[:, 0:1]), rd=[pb2], wr=[bcols])
            P.op("vector", lambda e, p=p2: e.tensor_copy(
                ffn_s127, p[:, 1:2]), rd=[pb2], wr=[bcols])
            for kc in range(ND):
                P.op("vector", lambda e, kc=kc: e.tensor_scalar(
                    Bs_t(10), Wk_(kc), cols[:, 493:494], None, OP.mult),
                    rd=[bW, bcols], wr=[bBs[10]])
                P.op("vector", lambda e: e.tensor_scalar(
                    Bs_t(11), Bs_t(10), 0.5, None, OP.is_ge), rd=[bBs[10]],
                    wr=[bBs[11]])
                P.op("vector", lambda e: e.tensor_scalar(
                    Bs_t(12), Bs_t(10), -0.5, None, OP.is_le), rd=[bBs[10]],
                    wr=[bBs[12]])
                P.op("vector", lambda e, kc=kc: e.tensor_tensor(
                    wq16[:, ts(kc, 1024)], Bs_t(11), Bs_t(12), OP.subtract),
                    rd=[bBs[11], bBs[12]], wr=[bwq])
            ffn_ready[0] = True

        def emit_l0ffn():
            ba_ensure_slots()
            if not ffn_ready[0]:
                emit_l0ffn_setup()
            emit_ln()
            for i in range(NT):
                c0 = 208 + 8 * (i % 2)
                P.op("scalar", lambda e, i=i: e.activation(
                    Bs_t(0), A_t(i), AF.Identity, bias=ln_b(i), scale=ln_s(i)),
                    rd=[bA[i], bcols], wr=[bBs[0]])
                P.op("vector", lambda e, c0=c0: e.reduce_max(
                    cols[:, c0:c0 + 1], Bs_t(0), axis=AX,
                    apply_absolute_value=True), rd=[bBs[0]], wr=[bcols])
                P.op("vector", lambda e, c0=c0: e.tensor_scalar(
                    cols[:, c0 + 1:c0 + 2], cols[:, c0:c0 + 1], 1e-8, None,
                    OP.add), rd=[bcols], wr=[bcols])
                P.op("vector", lambda e, c0=c0: e.reciprocal(
                    cols[:, c0 + 2:c0 + 3], cols[:, c0 + 1:c0 + 2]),
                    rd=[bcols], wr=[bcols])
                P.op("vector", lambda e, c0=c0: e.tensor_scalar(
                    cols[:, c0 + 3:c0 + 4], cols[:, c0 + 2:c0 + 3], 127.0,
                    None, OP.mult), rd=[bcols], wr=[bcols])
                P.op("scalar", lambda e, c0=c0: e.activation(
                    Bs_t(1), Bs_t(0), AF.Identity,
                    scale=cols[:, c0 + 3:c0 + 4]), rd=[bBs[0], bcols],
                    wr=[bBs[1]])
                for _h in range(2):
                    P.op("vector", lambda e, _h=_h: e.tensor_copy(
                        rint[:], Bs_t(1)[:, ts(_h, 512)]),
                        rd=[bBs[1]], wr=[brint])
                    P.op("vector", lambda e, _h=_h: e.tensor_copy(
                        Bs_t(2)[:, ts(_h, 512)], rint[:]),
                        rd=[brint], wr=[bBs[2]])
                P.op("vector", lambda e: e.tensor_tensor(
                    Bs_t(3), Bs_t(1), Bs_t(2), OP.subtract),
                    rd=[bBs[1], bBs[2]], wr=[bBs[3]])
                P.op("vector", lambda e: e.tensor_scalar(
                    Bs_t(4), Bs_t(3), 0.5, None, OP.is_ge), rd=[bBs[3]],
                    wr=[bBs[4]])
                P.op("vector", lambda e: e.tensor_scalar(
                    Bs_t(3), Bs_t(3), -0.5, None, OP.is_le), rd=[bBs[3]],
                    wr=[bBs[3]])
                P.op("vector", lambda e: e.tensor_tensor(
                    Bs_t(4), Bs_t(4), Bs_t(3), OP.subtract),
                    rd=[bBs[4], bBs[3]], wr=[bBs[4]])
                P.op("vector", lambda e: e.tensor_tensor(
                    aq16[:], Bs_t(2), Bs_t(4), OP.add), rd=[bBs[2], bBs[4]],
                    wr=[baq])
                for kc in range(ND):
                    transpose_bf16(aq16[:, ts(kc)], aqf16[:, ts(kc)],
                                   rd=[baq], wr=[baqf])
                P.op("vector", lambda e, c0=c0: e.tensor_tensor(
                    cols[:, c0 + 4:c0 + 5], cols[:, c0 + 1:c0 + 2], ffn_s127,
                    OP.mult), rd=[bcols], wr=[bcols])
                for oc in range(2):
                    p, pb = pacq()
                    for kc in range(ND):
                        P.op("tensor", lambda e, kc=kc, oc=oc, p=p: e.matmul(
                            p[:], aqf16[:, ts(kc)],
                            wq16[:, ts(kc, 1024)][:, ts(oc, 512)],
                            start=(kc == 0), stop=(kc == ND - 1)),
                            rd=[baqf, bwq], wr=[pb])
                    P.op("scalar", lambda e, oc=oc, p=p, c0=c0: e.activation(
                        Bs_t(1)[:, ts(oc, 512)], p[:], AF.Identity,
                        scale=cols[:, c0 + 4:c0 + 5]), rd=[pb, bcols],
                        wr=[bBs[1]])
                P.op("vector", lambda e, i=i: e.tensor_tensor(
                    A_t(i), A_t(i), Bs_t(1), OP.add), rd=[bBs[1], bA[i]],
                    wr=[bA[i]])

        def emit_l1f():
            ba_ensure_slots()
            load_weight("l1f_wT")
            P.dma("sync", rowb[:, 0:64].rearrange("p (a g) -> p a g", a=8),
                  di["l1f_gwT"].rearrange("(a p) g -> p a g", p=128),
                  wr=[browb])
            P.dma("sync", Cn_t(0), di["l1f_v"], wr=[bCn[0]])
            P.dma("sync", Cn_t(1).rearrange("p (a r) -> p a r", a=8),
                  di["l1f_uT"].rearrange("(a p) r -> p a r", p=128),
                  wr=[bCn[1]])
            emit_ln()
            for i in range(NT):
                P.op("scalar", lambda e, i=i: e.activation(
                    Bs_t(0), A_t(i), AF.Identity, bias=ln_b(i), scale=ln_s(i)),
                    rd=[bA[i], bcols], wr=[bBs[0]])
                for kc in range(ND):
                    transpose_f32(Bs_t(0)[:, ts(kc)], Bs_t(1)[:, ts(kc)],
                                  rd=[bBs[0]], wr=[bBs[1]])
                pg, pgb = pacq()
                for dc in range(ND):
                    P.op("tensor", lambda e, dc=dc, p=pg: e.matmul(
                        p[:, 0:8], Bs_t(1)[:, ts(dc)],
                        rowb[:, dc * 8:(dc + 1) * 8],
                        start=(dc == 0), stop=(dc == ND - 1)),
                        rd=[bBs[1], browb], wr=[pgb])
                pt, ptb = pacq()
                for dc in range(ND):
                    P.op("tensor", lambda e, dc=dc, p=pt: e.matmul(
                        p[:, 0:128], Bs_t(1)[:, ts(dc)],
                        Cn_t(1)[:, ts(dc)],
                        start=(dc == 0), stop=(dc == ND - 1)),
                        rd=[bBs[1], bCn[1]], wr=[ptb])
                P.op("scalar", lambda e, p=pg: e.activation(
                    Bs_t(2)[:, 0:8], p[:, 0:8], AF.Exp), rd=[pgb], wr=[bBs[2]])
                P.op("vector", lambda e: e.reduce_sum(
                    cols[:, 220:221], Bs_t(2)[:, 0:8], axis=AX),
                    rd=[bBs[2]], wr=[bcols])
                P.op("vector", lambda e: e.reciprocal(
                    cols[:, 221:222], cols[:, 220:221]), rd=[bcols], wr=[bcols])
                P.op("scalar", lambda e: e.activation(
                    Bs_t(2)[:, 16:24], Bs_t(2)[:, 0:8], AF.Identity,
                    scale=cols[:, 221:222]), rd=[bBs[2], bcols], wr=[bBs[2]])
                P.op("vector", lambda e, p=pt: e.tensor_tensor(
                    Bs_t(3)[:, 0:128].rearrange("p (e r) -> p e r", e=8),
                    p[:, 0:128].rearrange("p (e r) -> p e r", e=8),
                    Bs_t(2)[:, 16:24].unsqueeze(2).broadcast_to([128, 8, 16]),
                    OP.mult), rd=[ptb, bBs[2]], wr=[bBs[3]])
                transpose_f32(Bs_t(3)[:, 0:128], Bs_t(3)[:, 128:256],
                              rd=[bBs[3]], wr=[bBs[3]])
                for oc in range(2):
                    pl, plb = pacq()
                    P.op("tensor", lambda e, oc=oc, p=pl: e.matmul(
                        p[:], Bs_t(3)[:, 128:256], Cn_t(0)[:, ts(oc, 512)],
                        start=True, stop=True), rd=[bBs[3], bCn[0]], wr=[plb])
                    pb_, pbb = pacq()
                    for dc in range(ND):
                        P.op("tensor", lambda e, dc=dc, oc=oc, p=pb_: e.matmul(
                            p[:], Bs_t(1)[:, ts(dc)], Wk_(dc)[:, ts(oc, 512)],
                            start=(dc == 0), stop=(dc == ND - 1)),
                            rd=[bBs[1], bW], wr=[pbb])
                    P.op("scalar", lambda e, oc=oc, p=pb_: e.activation(
                        Bs_t(4)[:, ts(oc, 512)], p[:], AF.Identity),
                        rd=[pbb], wr=[bBs[4]])
                    P.op("vector", lambda e, oc=oc, p=pl: e.scalar_tensor_tensor(
                        Bs_t(4)[:, ts(oc, 512)], p[:], 1.0 / 16.0,
                        Bs_t(4)[:, ts(oc, 512)], OP.mult, OP.add),
                        rd=[plb, bBs[4]], wr=[bBs[4]])
                P.op("vector", lambda e, i=i: e.tensor_tensor(
                    A_t(i), A_t(i), Bs_t(4), OP.add), rd=[bBs[4], bA[i]],
                    wr=[bA[i]])

        bccin = [Buf(), Buf()]
        bccout = [Buf(), Buf()]
        bxsp = Buf()
        bosp = Buf()
        bxnsp = Buf()
        btsp = Buf()
        bdnsp = Buf()
        bKv = Buf()

        def fresh(olds):
            b = Buf()
            rs = []
            for o in olds:
                if o.w is not None:
                    rs.append(o.w)
                rs.extend(o.rs)
            b.rs = [t for t in rs if t is not None]
            return b

        def A_sl(slot, c0, c1):
            return A[:, slot * 1024 + c0:slot * 1024 + c1]

        def emit_mha(cfg):
            ba_ensure_fm()
            cci, cco = ccx_in[cfg["cc"]], ccx_out[cfg["cc"]]
            bci, bco = bccin[cfg["cc"]], bccout[cfg["cc"]]
            rope = cfg["rope"]
            if cfg["fold2"]:
                emit_ln(eps=1e-5, fold2=1e-6)
                bcast_row(1, cfg["lns"])
                bcast_row(2, cfg["lnb"])
            else:
                emit_ln(eps=1e-5)
            for i in range(NT):
                P.op("scalar", lambda e, i=i: e.activation(
                    Sc_t(0), A_t(i), AF.Identity, bias=ln_b(i), scale=ln_s(i)),
                    rd=[bA[i], bcols], wr=[bSc[0]])
                if cfg["fold2"]:
                    P.op("vector", lambda e: e.tensor_tensor(
                        Sc_t(0), Sc_t(0), Cn_t(1), OP.mult),
                        rd=[bSc[0], bCn[1]], wr=[bSc[0]])
                    P.op("vector", lambda e: e.tensor_tensor(
                        Sc_t(0), Sc_t(0), Cn_t(2), OP.add),
                        rd=[bSc[0], bCn[2]], wr=[bSc[0]])
                    P.dma("scalar", xnspill[ts(i), :], Sc_t(0), rd=[bSc[0]],
                          wr=[bxnsp])
                for kc in range(ND):
                    transpose_f32(Sc_t(0)[:, ts(kc)], Ba_t(kc)[:, ts(i)],
                                  rd=[bSc[0]], wr=[bBa[kc]])
                P.dma("scalar", xspill[ts(i), :], A_t(i), rd=[bA[i]],
                      wr=[bxsp])
            for kc in range(ND):
                P.dma("scalar", cci[ts(kc), :], Ba_t(kc), rd=[bBa[kc]],
                      wr=[bci])
            P.op("gpsimd", lambda e: e.collective_compute(
                "AllGather", OP.bypass, replica_groups=groups,
                ins=[cci.opt()], outs=[cco.opt()]), rd=[bci], wr=[bco])

            allA = bA + bAf
            bA0 = fresh(allA)
            bA1 = fresh(allA)
            bA2 = fresh(allA)
            bA3 = fresh(allA)
            bqf = [fresh(allA) for _ in range(4)]
            bkf = fresh(allA)
            bs14 = fresh(allA)
            bs15 = fresh(allA)

            c2pi = (6.28125, 1.9352436065674e-3, 6.336622499e-7)
            inv2pi = 0.15915494309189535

            def build_tab(tcol_ap, dst_cos, dst_sin, dbuf):
                th = A_sl(3, 128, 160)
                fr = A_sl(3, 512, 544)
                v = A_sl(3, 544, 576)
                kk = A_sl(3, 576, 608)
                dd = A_sl(3, 608, 640)
                g1 = A_sl(3, 640, 672)
                g2 = A_sl(3, 672, 704)
                red = A_sl(3, 704, 736)
                sA = A_sl(3, 736, 768)
                sB = A_sl(3, 768, 800)
                cp = A_sl(3, 800, 832)
                P.op("vector", lambda e: e.tensor_scalar(
                    fr, th, tcol_ap, None, OP.mult), rd=[bA3], wr=[bA3])
                P.op("vector", lambda e: e.tensor_scalar(
                    v, fr, inv2pi, None, OP.mult), rd=[bA3], wr=[bA3])
                P.op("vector", lambda e: e.tensor_copy(rint[:, 0:32], v),
                     rd=[bA3], wr=[brint])
                P.op("vector", lambda e: e.tensor_copy(kk, rint[:, 0:32]),
                     rd=[brint, bA3], wr=[bA3])
                P.op("vector", lambda e: e.tensor_tensor(
                    dd, v, kk, OP.subtract), rd=[bA3], wr=[bA3])
                P.op("vector", lambda e: e.tensor_scalar(
                    g1, dd, 0.5, None, OP.is_ge), rd=[bA3], wr=[bA3])
                P.op("vector", lambda e: e.tensor_scalar(
                    g2, dd, -0.5, None, OP.is_le), rd=[bA3], wr=[bA3])
                P.op("vector", lambda e: e.tensor_tensor(
                    g1, g1, g2, OP.subtract), rd=[bA3], wr=[bA3])
                P.op("vector", lambda e: e.tensor_tensor(
                    kk, kk, g1, OP.add), rd=[bA3], wr=[bA3])
                P.op("vector", lambda e: e.scalar_tensor_tensor(
                    red, kk, -c2pi[0], fr, OP.mult, OP.add),
                    rd=[bA3], wr=[bA3])
                P.op("vector", lambda e: e.scalar_tensor_tensor(
                    red, kk, -c2pi[1], red, OP.mult, OP.add),
                    rd=[bA3], wr=[bA3])
                P.op("vector", lambda e: e.scalar_tensor_tensor(
                    red, kk, -c2pi[2], red, OP.mult, OP.add),
                    rd=[bA3], wr=[bA3])
                P.op("scalar", lambda e: e.activation(
                    sA, red, AF.Sin, scale=0.5), rd=[bA3], wr=[bA3])
                P.op("scalar", lambda e: e.activation(
                    sB, red, AF.Sin, scale=0.25), rd=[bA3], wr=[bA3])
                P.op("scalar", lambda e: e.activation(
                    cp, sB, AF.Square), rd=[bA3], wr=[bA3])
                P.op("vector", lambda e: e.tensor_scalar(
                    cp, cp, -2.0, 1.0, OP.mult, OP.add), rd=[bA3], wr=[bA3])
                P.op("vector", lambda e, dst_sin=dst_sin: e.tensor_tensor(
                    dst_sin, sA, cp, OP.mult), rd=[bA3], wr=[dbuf])
                P.op("vector", lambda e, dst_sin=dst_sin: e.tensor_scalar(
                    dst_sin, dst_sin, 2.0, None, OP.mult), rd=[dbuf],
                    wr=[dbuf])
                P.op("scalar", lambda e: e.activation(
                    sA, sA, AF.Square), rd=[bA3], wr=[bA3])
                P.op("vector", lambda e, dst_cos=dst_cos: e.tensor_scalar(
                    dst_cos, sA, -2.0, 1.0, OP.mult, OP.add),
                    rd=[bA3], wr=[dbuf])

            if rope:
                P.op("gpsimd", lambda e: e.iota(
                    A_sl(3, 128, 160), pattern=[[1, 32]], base=0,
                    channel_multiplier=0,
                    allow_small_or_imprecise_dtypes=True), wr=[bA3])
                _lnth = -0.2878231366242557
                P.op("scalar", lambda e: e.activation(
                    A_sl(3, 128, 160), A_sl(3, 128, 160), AF.Exp,
                    scale=_lnth), rd=[bA3], wr=[bA3])
                for g in range(32):
                    P.op("gpsimd", lambda e, g=g: e.iota(
                        A_sl(3, 160, 168), pattern=[[0, 8]], base=g * 128,
                        channel_multiplier=1,
                        allow_small_or_imprecise_dtypes=True),
                        rd=[bA3], wr=[bA3])
                    build_tab(A_sl(3, 160, 161),
                              A_sl(0, g * 32, g * 32 + 32),
                              A_sl(1, g * 32, g * 32 + 32), bA0)
                for i in range(NT):
                    P.dma("sync", A_sl(3, 160, 161), di["tpos"][ts(i), :],
                          rd=[bA3], wr=[bA3])
                    build_tab(A_sl(3, 160, 161),
                              A_sl(2, i * 32, i * 32 + 32),
                              A_sl(2, 512 + i * 32, 512 + i * 32 + 32), bA2)

            dbg("cosk0", A_sl(0, 0, 1024), [bA0])
            dbg("sink0", A_sl(1, 0, 1024), [bA0])
            dbg("cosq0", A_sl(2, 0, 1024), [bA2])

            def rope_apply(tm_ap, tm_buf, cos_ap, sin_ap, tab_buf,
                           t1_ap, t2_ap, sbuf_):
                if not isinstance(sbuf_, list):
                    sbuf_ = [sbuf_]
                v3 = tm_ap.rearrange("p (g j) -> p g j", j=32)
                cb = cos_ap.unsqueeze(1).broadcast_to([128, 16, 32])
                sb2 = sin_ap.unsqueeze(1).broadcast_to([128, 16, 32])
                P.op("vector", lambda e: e.tensor_tensor(
                    t1_ap.rearrange("p (g j) -> p g j", j=32), v3, cb,
                    OP.mult), rd=[tm_buf, tab_buf], wr=list(sbuf_))
                P.op("vector", lambda e: e.tensor_tensor(
                    t2_ap.rearrange("p (g j) -> p g j", j=32), v3, sb2,
                    OP.mult), rd=[tm_buf, tab_buf], wr=list(sbuf_))
                h4 = tm_ap.rearrange("p (h q j) -> p h q j", q=2, j=32)
                t1h = t1_ap.rearrange("p (h q j) -> p h q j", q=2, j=32)
                t2h = t2_ap.rearrange("p (h q j) -> p h q j", q=2, j=32)
                P.op("vector", lambda e: e.tensor_tensor(
                    h4[:, :, 0, :], t1h[:, :, 0, :], t2h[:, :, 1, :],
                    OP.subtract), rd=list(sbuf_), wr=[tm_buf])
                P.op("vector", lambda e: e.tensor_tensor(
                    h4[:, :, 1, :], t1h[:, :, 1, :], t2h[:, :, 0, :],
                    OP.add), rd=list(sbuf_), wr=[tm_buf])

            bxn_kc = fresh(bBs + bBa)
            bvau = fresh(bBs + bBa)
            bprB = fresh(bBs + bBa)
            bacc = fresh(bBs + bBa)
            if cfg["bias"]:
                bcast_row(0, cfg["bq"])
                bcast_row(1, cfg["bk"])
                bcast_row(2, cfg["bv"])

            def vau2(t):
                return Ba[:, 4096 + t * 768:4096 + t * 768 + 768]

            def prA(kt):
                return Sc[:, kt * 512:kt * 512 + 512]

            def prAbuf(kt):
                return bSc[0] if kt < 2 else bSc[1]

            def prB(kt):
                if kt < 2:
                    return Ba[:, 7168 + kt * 512:7168 + kt * 512 + 512]
                if kt == 2:
                    return A_sl(15, 512, 1024)
                return A_sl(15, 0, 512)

            def prBbuf(kt):
                return bprB if kt < 2 else bs15

            def pv_soff(pa, hp):
                return pa * 64 if hp < 2 else (1 - pa) * 64

            def acc_ap(pa, hp, c0, c1):
                r0 = pv_soff(pa, hp)
                b0 = 8192 + hp * 2048
                return Ba[r0:r0 + 64, b0 + c0:b0 + c1]

            def den_rg(rg):
                if rg < 2:
                    return A[:, 3072 + rg * 512:3072 + rg * 512 + 512], bA3
                if rg < 4:
                    return (A[:, 14336 + (rg - 2) * 512:
                              14336 + (rg - 2) * 512 + 512], bs14)
                if rg == 4:
                    return aq16[:].bitcast(F32), baq
                if rg == 5:
                    return aqf16[:].bitcast(F32), baqf
                if rg == 6:
                    return rint[:].bitcast(F32), brint
                return Cn[:, 3072:3584], bCn3

            def den_ap(pa, hp, qb):
                rg = (hp % 2) * 4 + qb
                row = (64 - pv_soff(pa, hp)) + 32 * (hp // 2)
                reg, tok = den_rg(rg)
                return reg[row:row + 1, :], tok

            for g in range(2):
                bxn_kc = fresh([bxn_kc, bvau, bprB])
                for dc in range(ND):
                    P.dma("sync", Ba[:, dc * 1024:(dc + 1) * 1024],
                          cci[ts(dc), 0:1024], rd=[bci], wr=[bxn_kc])
                    P.dma("sync",
                          Ba[:, 8192 + dc * 1024:8192 + (dc + 1) * 1024],
                          cci[ts(dc), 1024:2048], rd=[bci], wr=[bacc])
                load_weight(cfg["wq"], csl=ts(g, 512))

                def xn_own(dc, tt):
                    base = dc * 1024 if tt < 8 else 8192 + dc * 1024
                    off = (tt % 8) * 128
                    return Ba[:, base + off:base + off + 128]

                for tt in range(NT):
                    if cfg["qfm_direct"]:
                        for oc in range(4):
                            p, pb = pacq()
                            for dc in range(ND):
                                P.op("tensor", lambda e, dc=dc, oc=oc, tt=tt,
                                     p=p: e.matmul(
                                         p[:, 0:128],
                                         Wk_(dc, 512)[:, ts(oc)],
                                         xn_own(dc, tt),
                                         start=(dc == 0),
                                         stop=(dc == ND - 1)),
                                     rd=[bW, bxn_kc, bacc], wr=[pb])
                            P.op("vector", lambda e, oc=oc, tt=tt, p=p:
                                 e.tensor_copy(
                                     A[:, (4 + 2 * oc) * 1024 + tt * 128:
                                       (4 + 2 * oc) * 1024 + tt * 128 + 128],
                                     p[:, 0:128]), rd=[pb], wr=[bqf[oc]])
                    else:
                        p, pb = pacq()
                        for dc in range(ND):
                            P.op("tensor", lambda e, dc=dc, tt=tt, p=p:
                                 e.matmul(p[:], xn_own(dc, tt),
                                          Wk_(dc, 512),
                                          start=(dc == 0),
                                          stop=(dc == ND - 1)),
                                 rd=[bW, bxn_kc, bacc], wr=[pb])
                        P.op("vector", lambda e, p=p, g=g: e.tensor_tensor(
                            A_sl(14, 0, 512), p[:],
                            Cn_t(0)[:, g * 512:g * 512 + 512], OP.add),
                            rd=[pb, bCn[0]], wr=[bs14])
                        rope_apply(A_sl(14, 0, 512), bs14,
                                   A_sl(2, tt * 32, tt * 32 + 32),
                                   A_sl(2, 512 + tt * 32, 512 + tt * 32 + 32),
                                   bA2, A_sl(14, 512, 1024),
                                   A_sl(15, 512, 1024), bs15)
                        if tt == 0 and g == 0:
                            dbg("qrope0", A_sl(14, 0, 512), [bs14])
                        for c in range(4):
                            transpose_f32(
                                A_sl(14, c * 128, c * 128 + 128),
                                A[:, (4 + 2 * c) * 1024 + tt * 128:
                                  (4 + 2 * c) * 1024 + tt * 128 + 128],
                                rd=[bs14], wr=[bqf[c]])

                load_weight(cfg["wk"], csl=ts(g, 512))
                load_weight(cfg["wv"], csl=ts(g, 512), woff=4096)
                bvau = fresh([bvau, bxn_kc, bacc])
                bprB = fresh([bprB, bxn_kc, bacc])
                for t in range(4):
                    P.op("vector", lambda e, t=t: e.memset(
                        vau2(t).rearrange("p (hp s) -> p hp s", s=192)
                        [:, :, 64:128], 1.0), wr=[bvau])

                for kc in range(8):
                    blk = kc // 4
                    c0 = (kc * 512) % 2048
                    for dc in range(ND):
                        P.dma("sync", Ba[:, dc * 512:(dc + 1) * 512],
                              cco[blk * 1024 + dc * 128:
                                  blk * 1024 + (dc + 1) * 128, c0:c0 + 512],
                              rd=[bco], wr=[bxn_kc])
                    for kt in range(4):
                        gt = kc * 4 + kt
                        p, pb = pacq()
                        for dc in range(ND):
                            P.op("tensor", lambda e, dc=dc, kt=kt, p=p:
                                 e.matmul(
                                     p[:],
                                     Ba[:, dc * 512 + kt * 128:
                                        dc * 512 + kt * 128 + 128],
                                     W[:, dc * 512:(dc + 1) * 512],
                                     start=(dc == 0), stop=(dc == ND - 1)),
                                 rd=[bW, bxn_kc], wr=[pb])
                        if cfg["bias"]:
                            P.op("vector", lambda e, p=p, g=g: e.tensor_tensor(
                                Sc[:, 0:512], p[:],
                                Cn_t(1)[:, g * 512:g * 512 + 512], OP.add),
                                rd=[pb, bCn[1]], wr=[bSc[0]])
                        else:
                            P.op("vector", lambda e, p=p: e.tensor_copy(
                                Sc[:, 0:512], p[:]), rd=[pb], wr=[bSc[0]])
                        if rope:
                            rope_apply(Sc[:, 0:512], bSc[0],
                                       A_sl(0, gt * 32, gt * 32 + 32),
                                       A_sl(1, gt * 32, gt * 32 + 32),
                                       bA0, Sc[:, 512:1024],
                                       Sc[:, 1024:1536], [bSc[0], bSc[1]])
                        if kc == 0 and kt == 0 and g == 0:
                            dbg("krope0", Sc[:, 0:512], [bSc[0]])
                        for c in range(4):
                            transpose_f32(
                                Sc[:, c * 128:c * 128 + 128],
                                A[:, 12 * 1024 + c * 512 + kt * 128:
                                  12 * 1024 + c * 512 + kt * 128 + 128],
                                rd=[bSc[0]], wr=[bkf])
                        p, pb = pacq()
                        for dc in range(ND):
                            P.op("tensor", lambda e, dc=dc, kt=kt, p=p:
                                 e.matmul(
                                     p[:],
                                     Ba[:, dc * 512 + kt * 128:
                                        dc * 512 + kt * 128 + 128],
                                     W[:, 4096 + dc * 512:
                                       4096 + (dc + 1) * 512],
                                     start=(dc == 0), stop=(dc == ND - 1)),
                                 rd=[bW, bxn_kc], wr=[pb])
                        vv = vau2(kt).rearrange("p (hp s) -> p hp s", s=192)
                        pv4 = p[:].rearrange("p (hp pa d) -> p hp pa d",
                                             pa=2, d=64)
                        if cfg["bias"]:
                            bv4 = Cn_t(2)[:, g * 512:g * 512 + 512].rearrange(
                                "p (hp pa d) -> p hp pa d", pa=2, d=64)
                        for (hsl, pa, scol) in (
                                (slice(0, 2), 0, slice(0, 64)),
                                (slice(0, 2), 1, slice(128, 192)),
                                (slice(2, 4), 0, slice(128, 192)),
                                (slice(2, 4), 1, slice(0, 64))):
                            if cfg["bias"]:
                                P.op("vector", lambda e, pv4=pv4, bv4=bv4,
                                     vv=vv, hsl=hsl, pa=pa, scol=scol:
                                     e.tensor_tensor(
                                         vv[:, hsl, scol],
                                         pv4[:, hsl, pa, :],
                                         bv4[:, hsl, pa, :], OP.add),
                                     rd=[pb, bCn[2]], wr=[bvau])
                            else:
                                P.op("vector", lambda e, pv4=pv4, vv=vv,
                                     hsl=hsl, pa=pa, scol=scol:
                                     e.tensor_copy(vv[:, hsl, scol],
                                                   pv4[:, hsl, pa, :]),
                                     rd=[pb], wr=[bvau])
                    if kc == 0 and g == 0:
                        dbg("vau0", vau2(0)[:, 0:192], [bvau])
                    for hp in range(4):
                        for qb in range(4):
                            for kt in range(4):
                                pA_, pAb = pacq()
                                pB_, pBb = pacq()
                                kbase = 12 * 1024 + hp * 512 + kt * 128
                                qbase = (4 + 2 * hp) * 1024 + qb * 512
                                P.op("tensor", lambda e, kbase=kbase,
                                     qbase=qbase, p=pA_: e.matmul(
                                         p[:], A[0:64, kbase:kbase + 128],
                                         A[0:64, qbase:qbase + 512],
                                         start=True, stop=True,
                                         tile_position=(0, 0)),
                                     rd=[bkf, bqf[hp]], wr=[pAb])
                                P.op("tensor", lambda e, kbase=kbase,
                                     qbase=qbase, p=pB_: e.matmul(
                                         p[:], A[64:128, kbase:kbase + 128],
                                         A[64:128, qbase:qbase + 512],
                                         start=True, stop=True,
                                         tile_position=(64, 0)),
                                     rd=[bkf, bqf[hp]], wr=[pBb])
                                P.op("scalar", lambda e, kt=kt, p=pA_:
                                     e.activation(prA(kt), p[:], AF.Exp,
                                                  scale=cfg["sc"]),
                                     rd=[pAb],
                                     wr=[bSc[0] if kt < 2 else bSc[1]])
                                P.op("scalar", lambda e, kt=kt, p=pB_:
                                     e.activation(prB(kt), p[:], AF.Exp,
                                                  scale=cfg["sc"]),
                                     rd=[pBb], wr=[prBbuf(kt)])
                            if (kc == 0 and g == 0 and hp == 0
                                    and qb == 0):
                                dbg("prA0", prA(0), [bSc[0]])
                            for pa in range(2):
                                po, pob = pacq()
                                soff = pv_soff(pa, hp)
                                for kt in range(4):
                                    pr = (prA(kt) if pa == 0 else prB(kt))
                                    prb = (prAbuf(kt) if pa == 0
                                           else prBbuf(kt))
                                    s0 = 4096 + kt * 768 + hp * 192 + soff
                                    P.op("tensor", lambda e, s0=s0, po=po,
                                         pr=pr, kt=kt: e.matmul(
                                             po[:],
                                             Ba[:, s0:s0 + 128], pr,
                                             start=(kt == 0),
                                             stop=(kt == 3)),
                                         rd=[prb, bvau], wr=[pob])
                                a_ap = acc_ap(pa, hp, qb * 512, qb * 512 + 512)
                                d_ap, dtok = den_ap(pa, hp, qb)
                                dn = (64 - soff) + 32 * (hp // 2)
                                if kc == 0:
                                    P.op("vector", lambda e, po=po, a_ap=a_ap,
                                         soff=soff: e.tensor_copy(
                                             a_ap, po[soff:soff + 64, :]),
                                         rd=[pob], wr=[bacc])
                                    P.op("vector", lambda e, po=po, d_ap=d_ap,
                                         dn=dn: e.tensor_copy(
                                             d_ap, po[dn:dn + 1, :]),
                                         rd=[pob], wr=[dtok])
                                else:
                                    P.op("vector", lambda e, po=po, a_ap=a_ap,
                                         soff=soff: e.tensor_tensor(
                                             a_ap, a_ap, po[soff:soff + 64, :],
                                             OP.add), rd=[pob], wr=[bacc])
                                    P.op("vector", lambda e, po=po, d_ap=d_ap,
                                         dn=dn: e.tensor_tensor(
                                             d_ap, d_ap, po[dn:dn + 1, :],
                                             OP.add),
                                        rd=[pob], wr=[dtok])
                if g == 0:
                    dbg("acc0", Ba[:, 8192:8192 + 512], [bacc])
                    dbg("den0", A_sl(3, 0, 128), [bA3])
                for rg in range(8):
                    reg, tok = den_rg(rg)
                    P.op("vector", lambda e, reg=reg: e.reciprocal(reg, reg),
                         rd=[tok], wr=[tok])
                for hp in range(4):
                    for pa in range(2):
                        for qb in range(4):
                            d_ap, dtok = den_ap(pa, hp, qb)
                            h = hp * 2 + pa
                            P.dma("scalar",
                                  denspill[h:h + 1, qb * 512:qb * 512 + 512],
                                  d_ap, rd=[dtok], wr=[bdnsp])
                for h in range(8):
                    hp, pa = h // 2, h % 2
                    r0 = pv_soff(pa, hp)
                    P.dma("sync", A[r0:r0 + 64, 12288:14336],
                          denspill[h:h + 1, :].partition_broadcast(64),
                          rd=[bdnsp], wr=[bkf])
                    P.op("vector", lambda e, r0=r0, hp=hp: e.tensor_tensor(
                        Ba[r0:r0 + 64,
                           8192 + hp * 2048:8192 + (hp + 1) * 2048],
                        Ba[r0:r0 + 64,
                           8192 + hp * 2048:8192 + (hp + 1) * 2048],
                        A[r0:r0 + 64, 12288:14336], OP.mult),
                        rd=[bkf], wr=[bacc])

                if cfg["wo"] is not None:
                    for hp in range(4):
                        r0 = g * 512 + hp * 128
                        if hp < 2:
                            P.dma("sync", W[:, hp * 1024:(hp + 1) * 1024],
                                  di[cfg["wo"]][r0:r0 + 128, :], wr=[bW])
                        else:
                            P.dma("sync",
                                  W[0:64, hp * 1024:(hp + 1) * 1024],
                                  di[cfg["wo"]][r0 + 64:r0 + 128, :],
                                  wr=[bW])
                            P.dma("sync",
                                  W[64:128, hp * 1024:(hp + 1) * 1024],
                                  di[cfg["wo"]][r0:r0 + 64, :], wr=[bW])
                    if g == 1:
                        for i in range(NT):
                            bA[i] = fresh([bA0, bA1, bA2, bA3, *bqf, bkf,
                                           bs14, bs15, bA[i], bAf[i // 2]])
                        bcast_row(0, cfg["bo"])
                    for tt in range(NT):
                        if g == 1:
                            P.dma("sync", A_t(tt), xspill[ts(tt), :],
                                  rd=[bxsp], wr=[bA[tt]])
                            P.dma("sync", Sc[:, 1024:2048],
                                  tspill[ts(tt), :], rd=[btsp], wr=[bSc[1]])
                        for ec in range(2):
                            p, pb = pacq()
                            for hp in range(4):
                                P.op("tensor", lambda e, hp=hp, tt=tt, ec=ec,
                                     p=p: e.matmul(
                                         p[:],
                                         Ba[:, 8192 + hp * 2048 + tt * 128:
                                            8192 + hp * 2048 + tt * 128
                                            + 128],
                                         W[:, hp * 1024 + ec * 512:
                                           hp * 1024 + ec * 512 + 512],
                                         start=(hp == 0), stop=(hp == 3)),
                                     rd=[bacc, bW], wr=[pb])
                            if g == 0:
                                P.op("vector", lambda e, p=p, ec=ec:
                                     e.tensor_copy(
                                         Sc[:, ec * 512:ec * 512 + 512],
                                         p[:]), rd=[pb], wr=[bSc[0]])
                            else:
                                P.op("vector", lambda e, p=p, ec=ec:
                                     e.tensor_tensor(
                                         Sc[:, ec * 512:ec * 512 + 512],
                                         p[:],
                                         Sc[:, 1024 + ec * 512:
                                            1024 + ec * 512 + 512], OP.add),
                                     rd=[pb, bSc[1]], wr=[bSc[0]])
                        if g == 0:
                            P.dma("scalar", tspill[ts(tt), :], Sc[:, 0:1024],
                                  rd=[bSc[0]], wr=[btsp])
                        else:
                            P.op("vector", lambda e: e.tensor_tensor(
                                Sc[:, 0:1024], Sc[:, 0:1024], Cn_t(0),
                                OP.add), rd=[bCn[0]], wr=[bSc[0]])
                            P.op("vector", lambda e, tt=tt: e.tensor_tensor(
                                A_t(tt), A_t(tt), Sc[:, 0:1024], OP.add),
                                rd=[bSc[0]], wr=[bA[tt]])
                else:
                    def acc_tp(hp, tt, dst0):
                        src = Ba[:, 8192 + hp * 2048 + tt * 128:
                                 8192 + hp * 2048 + tt * 128 + 128]
                        if hp < 2:
                            transpose_f32(src, Sc[:, dst0:dst0 + 128],
                                          rd=[bacc], wr=[bSc[0]])
                        else:
                            p, pb2 = pacq()
                            P.op("tensor", lambda e, p=p, src=src:
                                 e.transpose(p[:, 0:128], src, ident[:]),
                                 rd=[bacc, bident], wr=[pb2])
                            P.op("vector", lambda e, p=p, dst0=dst0:
                                 e.tensor_copy(Sc[:, dst0 + 64:dst0 + 128],
                                               p[:, 0:64]),
                                 rd=[pb2], wr=[bSc[0]])
                            P.op("vector", lambda e, p=p, dst0=dst0:
                                 e.tensor_copy(Sc[:, dst0:dst0 + 64],
                                               p[:, 64:128]),
                                 rd=[pb2], wr=[bSc[0]])
                    if g == 1:
                        for i in range(NT):
                            bA[i] = fresh([bA0, bA1, bA2, bA3, *bqf, bkf,
                                           bs14, bs15, bA[i], bAf[i // 2]])
                    for tt in range(NT):
                        if g == 0:
                            for hp in range(4):
                                acc_tp(hp, tt, hp * 128)
                            P.dma("scalar", tspill[ts(tt), 0:512],
                                  Sc[:, 0:512], rd=[bSc[0]], wr=[btsp])
                        else:
                            P.dma("sync", A_t(tt), xspill[ts(tt), :],
                                  rd=[bxsp], wr=[bA[tt]])
                            for hp in range(4):
                                acc_tp(hp, tt, 512 + hp * 128)
                            P.dma("sync", Sc[:, 0:512],
                                  tspill[ts(tt), 0:512], rd=[btsp],
                                  wr=[bSc[0]])
                            P.dma("sync", Sc[:, 1024:2048],
                                  xnspill[ts(tt), :], rd=[bxnsp],
                                  wr=[bSc[1]])
                            P.op("vector", lambda e, tt=tt: e.tensor_tensor(
                                A_t(tt), A_t(tt), Sc_t(1), OP.add),
                                rd=[bSc[1]], wr=[bA[tt]])
                            P.op("vector", lambda e, tt=tt: e.tensor_tensor(
                                A_t(tt), A_t(tt), Sc_t(0), OP.add),
                                rd=[bSc[0]], wr=[bA[tt]])
            for j in range(ND):
                bBa[j] = fresh([bacc, bxn_kc, bvau, bprB, bs15, bBa[j]])
            for i in range(16):
                bBs[i] = fresh([bBa[i // 2]])
            ba_state[0] = "slots"

        def emit_l0b():
            emit_mha(dict(cc=0, rope=True, fold2=False, bias=True,
                          wq="l0b_wqT", wk="l0b_wkT", wv="l0b_wvT",
                          bq="l0b_bq", bk="l0b_bk", bv="l0b_bv",
                          wo="l0b_woT", bo="l0b_bo", sc=0.125,
                          qfm_direct=False, lns=None, lnb=None))

        def emit_l1a():
            emit_mha(dict(cc=1, rope=False, fold2=True, bias=False,
                          wq="l1a_wq", wk="l1a_wk", wv="l1a_wv",
                          bq=None, bk=None, bv=None,
                          wo=None, bo=None, sc=0.03125,
                          qfm_direct=True, lns="l1a_lns", lnb="l1a_lnb"))

        def emit_mb():
            ba_ensure_fm()
            bch_i = Buf()
            bch_o = Buf()
            bh0_i = Buf()
            bh0_o = Buf()
            bWs = bW
            buspb = Buf()
            bCns = Buf()
            bCnk = Buf()
            emit_ln(eps=1e-5)
            for i in range(NT):
                P.op("scalar", lambda e, i=i: e.activation(
                    Sc_t(0), A_t(i), AF.Identity, bias=ln_b(i), scale=ln_s(i)),
                    rd=[bA[i], bcols], wr=[bSc[0]])
                for kc in range(ND):
                    transpose_f32(Sc_t(0)[:, ts(kc)], Ba_t(kc)[:, ts(i)],
                                  rd=[bSc[0]], wr=[bBa[kc]])
                P.dma("scalar", xspill[ts(i), :], A_t(i), rd=[bA[i]],
                      wr=[bxsp])
            P.dma("sync", cols[0:1, 230:231], di["modd"], wr=[bcols])
            pmk, pmkb = pacq()
            P.op("tensor", lambda e, p=pmk: e.matmul(
                p[:, 0:1], ones_c[0:1, :], cols[0:1, 230:231],
                start=True, stop=True), rd=[bcols], wr=[pmkb])
            P.op("vector", lambda e, p=pmk: e.tensor_copy(
                cols[:, 231:232], p[:, 0:1]), rd=[pmkb], wr=[bcols])
            P.op("vector", lambda e: e.tensor_scalar(
                cols[:, 232:233], cols[:, 231:232], -1.0, 1.0, OP.mult,
                OP.add), rd=[bcols], wr=[bcols])
            modd_c = cols[:, 231:232]
            mev_c = cols[:, 232:233]

            allA = bA + bAf
            bxm = [fresh(allA) for _ in range(ND)]

            def load_ip(half):
                load_weight("mb_ipT", csl=slice(half * 1024,
                                                half * 1024 + 1024))

            for half in range(2):
                load_ip(half)
                for bt, rows in ((0, (0, 1)), (15, (127,))):
                    for oc in range(2):
                        p, pb = pacq()
                        for dc in range(ND):
                            P.op("tensor", lambda e, dc=dc, bt=bt, oc=oc, p=p:
                                 e.matmul(p[:], Ba_t(dc)[:, ts(bt)],
                                          Wk_(dc)[:, ts(oc, 512)],
                                          start=(dc == 0),
                                          stop=(dc == ND - 1)),
                                 rd=[bW, bBa[dc]], wr=[pb])
                        P.op("vector", lambda e, oc=oc, p=p: e.tensor_copy(
                            Sc_t(1)[:, ts(oc, 512)], p[:]), rd=[pb],
                            wr=[bSc[1]])
                    for r, ridx in (((0, 1), None) if False else
                                    [(r, {(0, 0): 0, (0, 1): 1,
                                          (15, 127): 2}[(bt, r)])
                                     for r in rows]):
                        P.dma("scalar",
                              cch_in[ridx:ridx + 1,
                                     half * 1024:half * 1024 + 1024],
                              Sc_t(1)[r:r + 1, :], rd=[bSc[1]], wr=[bch_i])
                        if ridx == 2:
                            P.dma("scalar",
                                  cch_in[3:4, half * 1024:half * 1024 + 1024],
                                  Sc_t(1)[r:r + 1, :], rd=[bSc[1]],
                                  wr=[bch_i])
            P.op("gpsimd", lambda e: e.collective_compute(
                "AllGather", OP.bypass, replica_groups=groups,
                ins=[cch_in.opt()], outs=[cch_out.opt()]),
                rd=[bch_i], wr=[bch_o])
            P.dma("sync", Sc[0:8, 0:2048], cch_out, rd=[bch_o],
                  wr=[bSc[0], bSc[1]])
            for zi in range(2):
                for ct in range(ND):
                    p, pb = pacq()
                    P.op("tensor", lambda e, zi=zi, ct=ct, p=p: e.transpose(
                        p[:, 0:8], Sc[0:8, zi * 1024 + ct * 128:
                                      zi * 1024 + ct * 128 + 128],
                        ident[0:8, 0:8]),
                        rd=[bSc[0], bSc[1], bident], wr=[pb])
                    base = zi * 64 + ct * 8
                    P.op("vector", lambda e, p=p, base=base: e.tensor_tensor(
                        rowb[:, base:base + 1], p[:, 2:3], modd_c, OP.mult),
                        rd=[pb, bcols], wr=[browb])
                    P.op("vector", lambda e, p=p, base=base: e.tensor_tensor(
                        rowb[:, base + 1:base + 2], p[:, 4:5], mev_c,
                        OP.mult), rd=[pb, bcols], wr=[browb])
                    P.op("vector", lambda e, p=p, base=base: e.tensor_tensor(
                        rowb[:, base + 2:base + 3], p[:, 5:6], mev_c,
                        OP.mult), rd=[pb, bcols], wr=[browb])

            load_ip(0)
            for ct in range(ND):
                for tc in range(4):
                    p, pb = pacq()
                    for dc in range(ND):
                        P.op("tensor", lambda e, dc=dc, ct=ct, tc=tc, p=p:
                             e.matmul(p[:], Wk_(dc)[:, ts(ct)],
                                      Ba_t(dc)[:, ts(tc, 512)],
                                      start=(dc == 0), stop=(dc == ND - 1)),
                             rd=[bW, bBa[dc]], wr=[pb])
                    P.op("vector", lambda e, ct=ct, tc=tc, p=p: e.tensor_copy(
                        A[:, ct * 2048 + tc * 512:ct * 2048 + tc * 512 + 512],
                        p[:]), rd=[pb], wr=[bxm[ct]])

            P.dma("sync", Cn[:, 2048:2080].rearrange("p (a k) -> p a k", a=8),
                  di["mb_cx"].rearrange("(a p) k -> p a k", p=128), wr=[bCnk])
            P.dma("sync", Cn[:, 2080:2112].rearrange("p (a k) -> p a k", a=8),
                  di["mb_cz"].rearrange("(a p) k -> p a k", p=128), wr=[bCnk])

            def conv_silu(src_ap, src_buf, wbase, wc_buf, halo_base,
                          dst_ap, dst_buf):
                scr = Cn[:, 0:2048]
                w_cols = Cn[:, wbase:wbase + 4]
                P.op("vector", lambda e: e.tensor_scalar(
                    scr, src_ap, w_cols[:, 1:2], None, OP.mult),
                    rd=[src_buf, wc_buf], wr=[bCns])
                P.op("vector", lambda e: e.scalar_tensor_tensor(
                    scr[:, 1:2048], src_ap[:, 0:2047], w_cols[:, 0:1],
                    scr[:, 1:2048], OP.mult, OP.add),
                    rd=[src_buf, wc_buf, bCns], wr=[bCns])
                P.op("vector", lambda e: e.scalar_tensor_tensor(
                    scr[:, 0:2047], src_ap[:, 1:2048], w_cols[:, 2:3],
                    scr[:, 0:2047], OP.mult, OP.add),
                    rd=[src_buf, wc_buf, bCns], wr=[bCns])
                P.op("vector", lambda e: e.scalar_tensor_tensor(
                    scr[:, 0:2046], src_ap[:, 2:2048], w_cols[:, 3:4],
                    scr[:, 0:2046], OP.mult, OP.add),
                    rd=[src_buf, wc_buf, bCns], wr=[bCns])
                hL = rowb[:, halo_base:halo_base + 1]
                hR0 = rowb[:, halo_base + 1:halo_base + 2]
                hR1 = rowb[:, halo_base + 2:halo_base + 3]
                P.op("vector", lambda e: e.scalar_tensor_tensor(
                    scr[:, 0:1], hL, w_cols[:, 0:1], scr[:, 0:1],
                    OP.mult, OP.add), rd=[browb, wc_buf, bCns], wr=[bCns])
                P.op("vector", lambda e: e.scalar_tensor_tensor(
                    scr[:, 2047:2048], hR0, w_cols[:, 2:3],
                    scr[:, 2047:2048], OP.mult, OP.add),
                    rd=[browb, wc_buf, bCns], wr=[bCns])
                P.op("vector", lambda e: e.scalar_tensor_tensor(
                    scr[:, 2046:2047], hR0, w_cols[:, 3:4],
                    scr[:, 2046:2047], OP.mult, OP.add),
                    rd=[browb, wc_buf, bCns], wr=[bCns])
                P.op("vector", lambda e: e.scalar_tensor_tensor(
                    scr[:, 2047:2048], hR1, w_cols[:, 3:4],
                    scr[:, 2047:2048], OP.mult, OP.add),
                    rd=[browb, wc_buf, bCns], wr=[bCns])
                P.op("scalar", lambda e: e.activation(
                    dst_ap, scr, AF.Silu), rd=[bCns], wr=[dst_buf])

            load_ip(1)
            for ct in range(ND):
                for tc in range(4):
                    p, pb = pacq()
                    for dc in range(ND):
                        P.op("tensor", lambda e, dc=dc, ct=ct, tc=tc, p=p:
                             e.matmul(p[:], Wk_(dc)[:, ts(ct)],
                                      Ba_t(dc)[:, ts(tc, 512)],
                                      start=(dc == 0), stop=(dc == ND - 1)),
                             rd=[bW, bBa[dc]], wr=[pb])
                    P.op("vector", lambda e, tc=tc, p=p: e.tensor_copy(
                        Sc[:, tc * 512:tc * 512 + 512], p[:]), rd=[pb],
                        wr=[bSc[0], bSc[1]])
                conv_silu(Sc[:, 0:2048], bSc[0], 2080 + ct * 4, bCnk,
                          64 + ct * 8, Sc[:, 0:2048], bSc[0])
                P.dma("scalar",
                      tspill[ts(ct, 256), :].rearrange(
                          "(p x) d -> p (x d)", p=128),
                      Sc[:, 0:2048], rd=[bSc[0]], wr=[btsp])
            for ct in range(ND):
                conv_silu(A[:, ct * 2048:(ct + 1) * 2048], bxm[ct],
                          2048 + ct * 4, bCnk, ct * 8,
                          A[:, ct * 2048:(ct + 1) * 2048], bxm[ct])

            dbg("mxc0", A[:, 0:2048], [bxm[0]])
            load_weight("mb_xpT")
            for tc in range(4):
                p, pb = pacq()
                for dc in range(ND):
                    P.op("tensor", lambda e, dc=dc, tc=tc, p=p: e.matmul(
                        p[0:96, :], W[:, dc * 96:(dc + 1) * 96],
                        A[:, dc * 2048 + tc * 512:dc * 2048 + tc * 512 + 512],
                        start=(dc == 0), stop=(dc == ND - 1)),
                        rd=[bW, bxm[dc]], wr=[pb])
                P.op("vector", lambda e, tc=tc, p=p: e.tensor_copy(
                    Sc[0:96, tc * 512:tc * 512 + 512], p[0:96, :]),
                    rd=[pb], wr=[bSc[0], bSc[1]])

            dbg("mxdbl", Sc[0:96, 0:2048], [bSc[0], bSc[1]])
            bbcs = Buf()
            P.dma("scalar", bcsp, Sc[64:96, 0:2048], rd=[bSc[0], bSc[1]],
                  wr=[bbcs])

            bdel = [fresh([bBs[2 * j], bBs[2 * j + 1], bBa[j]])
                    for j in range(ND)]
            P.dma("sync", W[0:64, 1024:2048], di["mb_dtT"], wr=[bW])
            P.dma("sync", rowb[0:8, :],
                  di["mb_dtb"].rearrange("(a f) o -> (o a) f", a=8),
                  wr=[browb])
            pdt, pdtb = pacq()
            P.op("tensor", lambda e, p=pdt: e.transpose(
                p[:, 0:8], rowb[0:8, :], ident[0:8, 0:8]),
                rd=[browb, bident], wr=[pdtb])
            P.op("vector", lambda e, p=pdt: e.tensor_copy(
                cols[:, 240:248], p[:, 0:8]), rd=[pdtb], wr=[bcols])
            P.dma("sync", rowb[0:8, :],
                  di["mb_dvec"].rearrange("(a f) o -> (o a) f", a=8),
                  wr=[browb])
            pdv, pdvb = pacq()
            P.op("tensor", lambda e, p=pdv: e.transpose(
                p[:, 0:8], rowb[0:8, :], ident[0:8, 0:8]),
                rd=[browb, bident], wr=[pdvb])
            P.op("vector", lambda e, p=pdv: e.tensor_copy(
                cols[:, 248:256], p[:, 0:8]), rd=[pdvb], wr=[bcols])
            _one = KC(1.0)
            for ct in range(ND):
                for tc in range(4):
                    p, pb = pacq()
                    P.op("tensor", lambda e, ct=ct, tc=tc, p=p: e.matmul(
                        p[:], W[0:64, 1024 + ct * 128:1024 + ct * 128 + 128],
                        Sc[0:64, tc * 512:tc * 512 + 512],
                        start=True, stop=True),
                        rd=[bW, bSc[0], bSc[1]], wr=[pb])
                    P.op("vector", lambda e, ct=ct, tc=tc, p=p: e.tensor_copy(
                        Ba[:, ct * 2048 + tc * 512:ct * 2048 + tc * 512 + 512],
                        p[:]), rd=[pb], wr=[bdel[ct]])
                dr = Ba[:, ct * 2048:(ct + 1) * 2048]
                P.op("scalar", lambda e, dr=dr, ct=ct: e.activation(
                    Cn[:, 0:2048], dr, AF.Abs, bias=cols[:, 240 + ct:241 + ct]),
                    rd=[bdel[ct], bcols], wr=[bCns])
                P.op("scalar", lambda e: e.activation(
                    Cn[:, 0:2048], Cn[:, 0:2048], AF.Exp, scale=-1.0),
                    rd=[bCns], wr=[bCns])
                P.op("scalar", lambda e, _one=_one: e.activation(
                    Cn[:, 0:2048], Cn[:, 0:2048], AF.Ln, bias=_one),
                    rd=[bCns, bkconst], wr=[bCns])
                P.op("scalar", lambda e, dr=dr, ct=ct: e.activation(
                    W[:, 2048:4096], dr, AF.Relu,
                    bias=cols[:, 240 + ct:241 + ct]),
                    rd=[bdel[ct], bcols], wr=[bWs])
                P.op("vector", lambda e, dr=dr: e.tensor_tensor(
                    dr, Cn[:, 0:2048], W[:, 2048:4096], OP.add),
                    rd=[bCns, bWs], wr=[bdel[ct]])
                P.op("vector", lambda e, ct=ct, dr=dr: e.tensor_tensor(
                    Cn[:, 0:2048], dr,
                    A[:, ct * 2048:(ct + 1) * 2048], OP.mult),
                    rd=[bdel[ct], bxm[ct]], wr=[bCns])
                P.dma("scalar",
                      ospill[ts(ct, 256), :].rearrange(
                          "(p x) d -> p (x d)", p=128),
                      Cn[:, 0:2048], rd=[bCns], wr=[bosp])
                P.op("vector", lambda e, ct=ct: e.tensor_scalar(
                    A[:, ct * 2048:(ct + 1) * 2048],
                    A[:, ct * 2048:(ct + 1) * 2048],
                    cols[:, 248 + ct:249 + ct], None, OP.mult),
                    rd=[bcols], wr=[bxm[ct]])
            dbg("mdel0", Ba[:, 0:2048], [bdel[0]])
            dbg("my0", A[:, 0:2048], [bxm[0]])
            P.op("vector", lambda e: e.memset(W[:, 0:2048], 0.0), wr=[bWs])
            for ct in range(ND):
                P.op("vector", lambda e, ct=ct: e.tensor_tensor_scan(
                    W[:, 2048:4096], Ba[:, ct * 2048:(ct + 1) * 2048],
                    W[:, 0:2048], 0.0, OP.add, OP.add),
                    rd=[bdel[ct], bWs], wr=[bWs])
                P.dma("scalar",
                      uspill[ts(ct, 256), :].rearrange(
                          "(p x) d -> p (x d)", p=128),
                      W[:, 2048:4096], rd=[bWs], wr=[buspb])
            for ct in range(ND):
                P.dma("sync", Cn[:, 2112 + ct * 16:2112 + ct * 16 + 16],
                      di["mb_alog"][ts(ct), :], wr=[bCnk])
            P.op("scalar", lambda e: e.activation(
                Cn[:, 2112:2240], Cn[:, 2112:2240], AF.Exp),
                rd=[bCnk], wr=[bCnk])
            P.op("vector", lambda e: e.tensor_scalar(
                Cn[:, 2112:2240], Cn[:, 2112:2240], -1.0, None, OP.mult),
                rd=[bCnk], wr=[bCnk])

            for n in range(16):
                P.dma("sync", W[:, 4096:6144],
                      bcsp[n:n + 1, :].partition_broadcast(128),
                      rd=[bbcs], wr=[bWs])
                P.dma("sync", W[:, 6144:8192],
                      bcsp[16 + n:17 + n, :].partition_broadcast(128),
                      rd=[bbcs], wr=[bWs])
                for ct in range(ND):
                    acol = Cn[:, 2112 + ct * 16 + n:2112 + ct * 16 + n + 1]
                    P.op("scalar", lambda e, ct=ct, acol=acol: e.activation(
                        W[:, 0:2048], Ba[:, ct * 2048:(ct + 1) * 2048],
                        AF.Exp, scale=acol),
                        rd=[bdel[ct], bCnk], wr=[bWs])
                    P.dma("sync", W[:, 2048:4096],
                          ospill[ts(ct, 256), :].rearrange(
                              "(p x) d -> p (x d)", p=128),
                          rd=[bosp], wr=[bWs])
                    P.op("vector", lambda e, ct=ct: e.tensor_tensor(
                        W[:, 2048:4096], W[:, 2048:4096],
                        W[:, 4096:6144], OP.mult), rd=[bWs], wr=[bWs])
                    P.op("vector", lambda e: e.tensor_tensor_scan(
                        Cn[:, 0:2048], W[:, 0:2048], W[:, 2048:4096], 0.0,
                        OP.mult, OP.add), rd=[bWs], wr=[bCns])
                    P.op("vector", lambda e, ct=ct, n=n: e.tensor_copy(
                        Cn[:, 2368 + ct * 16 + n:2368 + ct * 16 + n + 1],
                        Cn[:, 2047:2048]), rd=[bCns], wr=[bCnk])
                    P.op("vector", lambda e: e.tensor_tensor(
                        Cn[:, 0:2048], Cn[:, 0:2048], W[:, 6144:8192],
                        OP.mult), rd=[bCns, bWs], wr=[bCns])
                    P.op("vector", lambda e, ct=ct: e.tensor_tensor(
                        A[:, ct * 2048:(ct + 1) * 2048],
                        A[:, ct * 2048:(ct + 1) * 2048],
                        Cn[:, 0:2048], OP.add), rd=[bCns], wr=[bxm[ct]])
            dbg("mys", A[:, 0:2048], [bxm[0]])
            dbg("mhend", Cn[:, 2368:2496], [bCnk])
            for ct in range(ND):
                P.dma("scalar", cc0_in[ts(ct), :],
                      Cn[:, 2368 + ct * 16:2368 + ct * 16 + 16],
                      rd=[bCnk], wr=[bh0_i])
            P.op("gpsimd", lambda e: e.collective_compute(
                "AllGather", OP.bypass, replica_groups=groups,
                ins=[cc0_in.opt()], outs=[cc0_out.opt()]),
                rd=[bh0_i], wr=[bh0_o])
            for ct in range(ND):
                P.dma("sync", Cn[:, 2240 + ct * 16:2240 + ct * 16 + 16],
                      cc0_out[ts(ct), :], rd=[bh0_o], wr=[bCnk])
            P.op("vector", lambda e: e.tensor_scalar(
                Cn[:, 2240:2368], Cn[:, 2240:2368], modd_c, None, OP.mult),
                rd=[bcols, bCnk], wr=[bCnk])
            for n in range(16):
                P.dma("sync", W[:, 6144:8192],
                      bcsp[16 + n:17 + n, :].partition_broadcast(128),
                      rd=[bbcs], wr=[bWs])
                for ct in range(ND):
                    P.dma("sync", W[:, 0:2048],
                          uspill[ts(ct, 256), :].rearrange(
                              "(p x) d -> p (x d)", p=128),
                          rd=[buspb], wr=[bWs])
                    acol = Cn[:, 2112 + ct * 16 + n:2112 + ct * 16 + n + 1]
                    P.op("scalar", lambda e, acol=acol: e.activation(
                        W[:, 2048:4096], W[:, 0:2048], AF.Exp, scale=acol),
                        rd=[bWs, bCnk], wr=[bWs])
                    h0c = Cn[:, 2240 + ct * 16 + n:2240 + ct * 16 + n + 1]
                    P.op("vector", lambda e, h0c=h0c: e.scalar_tensor_tensor(
                        W[:, 0:2048], W[:, 2048:4096], h0c, W[:, 6144:8192],
                        OP.mult, OP.mult), rd=[bWs, bCnk], wr=[bWs])
                    P.op("vector", lambda e, ct=ct: e.tensor_tensor(
                        A[:, ct * 2048:(ct + 1) * 2048],
                        A[:, ct * 2048:(ct + 1) * 2048],
                        W[:, 0:2048], OP.add), rd=[bWs], wr=[bxm[ct]])

            load_weight("mb_opT", csl=slice(0, 1024), rsl=slice(0, 1024))
            for tt in range(NT):
                for oc in range(2):
                    p, pb = pacq()
                    for dc in range(ND):
                        P.op("tensor", lambda e, dc=dc, tt=tt, oc=oc, p=p:
                             e.matmul(
                                 p[:],
                                 A[:, dc * 2048 + tt * 128:
                                   dc * 2048 + tt * 128 + 128],
                                 W[:, dc * 1024 + oc * 512:
                                   dc * 1024 + oc * 512 + 512],
                                 start=(dc == 0), stop=(dc == ND - 1)),
                             rd=[bW, bxm[dc]], wr=[pb])
                    P.op("vector", lambda e, oc=oc, p=p: e.tensor_copy(
                        Sc[:, oc * 512:oc * 512 + 512], p[:]), rd=[pb],
                        wr=[bSc[0]])
                P.dma("scalar", ospill[ts(tt), :], Sc[:, 0:1024],
                      rd=[bSc[0]], wr=[bosp])
            bzc = []
            for ct in range(ND):
                bzc.append(fresh([bdel[ct]]))
                P.dma("sync", Ba[:, ct * 2048:(ct + 1) * 2048],
                      tspill[ts(ct, 256), :].rearrange(
                          "(p x) d -> p (x d)", p=128),
                      rd=[btsp], wr=[bzc[ct]])
            load_weight("mb_opT", csl=slice(0, 1024), rsl=slice(1024, 2048))
            for tt in range(NT):
                bA[tt] = fresh([bxm[tt // 2], bA[tt]])
                P.dma("sync", A_t(tt), xspill[ts(tt), :], rd=[bxsp],
                      wr=[bA[tt]])
                P.dma("sync", Sc[:, 1024:2048], ospill[ts(tt), :],
                      rd=[bosp], wr=[bSc[1]])
                for oc in range(2):
                    p, pb = pacq()
                    for dc in range(ND):
                        P.op("tensor", lambda e, dc=dc, tt=tt, oc=oc, p=p:
                             e.matmul(
                                 p[:],
                                 Ba[:, dc * 2048 + tt * 128:
                                   dc * 2048 + tt * 128 + 128],
                                 W[:, dc * 1024 + oc * 512:
                                   dc * 1024 + oc * 512 + 512],
                                 start=(dc == 0), stop=(dc == ND - 1)),
                             rd=[bW, bzc[dc]], wr=[pb])
                    P.op("vector", lambda e, oc=oc, p=p: e.tensor_copy(
                        Sc[:, oc * 512:oc * 512 + 512], p[:]), rd=[pb],
                        wr=[bSc[0]])
                P.op("vector", lambda e, tt=tt: e.tensor_tensor(
                    A_t(tt), A_t(tt), Sc[:, 1024:2048], OP.add),
                    rd=[bSc[1]], wr=[bA[tt]])
                P.op("vector", lambda e, tt=tt: e.tensor_tensor(
                    A_t(tt), A_t(tt), Sc[:, 0:1024], OP.add),
                    rd=[bSc[0]], wr=[bA[tt]])
            for j in range(ND):
                bBa[j] = fresh([bzc[j], bBa[j]])
            for i in range(16):
                bBs[i] = fresh([bBa[i // 2]])
            ba_state[0] = "slots"

        emitters = {"l0a": emit_l0a, "l0f": emit_l0ffn, "l1f": emit_l1f,
                    "l0b": emit_l0b, "l1a": emit_l1a, "mb": emit_mb}
        for sl in sublayers:
            if sl in emitters:
                emitters[sl]()

        st = []
        for i in range(NT):
            st.append(P.dma("scalar", xout[ts(i), :], A_t(i), rd=[bA[i]]))
        P.waits("sync", st)

        with nc.Block() as block:
            P.replay(block)

    return nc



def _prep_inputs(I):
    c = np.ascontiguousarray
    base = {
        "l0a_wkT": c(I["L0_attn_a_Wk"].T), "l0a_wvT": c(I["L0_attn_a_Wv"].T),
        "l0a_wqT": c(I["L0_attn_a_Wq"].T), "l0a_wkpT": c(I["L0_attn_a_Wkp"].T),
        "l0a_wqpT": c(I["L0_attn_a_Wqp"].T), "l0a_woT": c(I["L0_attn_a_Wo"].T),
        "l0a_bk": I["L0_attn_a_bk"].reshape(1, D), "l0a_bv": I["L0_attn_a_bv"].reshape(1, D),
        "l0a_bq": I["L0_attn_a_bq"].reshape(1, D), "l0a_bkp": I["L0_attn_a_bkp"].reshape(1, D),
        "l0a_bqp": I["L0_attn_a_bqp"].reshape(1, D), "l0a_bo": I["L0_attn_a_bo"].reshape(1, D),
        "l0a_ps": I["L0_attn_a_phase_scale"].reshape(1, D),
        "l0a_lng": I["L0_attn_a_ln_g"].reshape(1, D),
        "l0a_lnb": I["L0_attn_a_ln_b"].reshape(1, D),
        "l0f_wT": c(I["L0_ffn_weight"].T),
        "l0b_wqT": c(I["L0_attn_b_Wq"].T), "l0b_wkT": c(I["L0_attn_b_Wk"].T),
        "l0b_wvT": c(I["L0_attn_b_Wv"].T), "l0b_woT": c(I["L0_attn_b_Wo"].T),
        "l0b_bq": I["L0_attn_b_bq"].reshape(1, D), "l0b_bk": I["L0_attn_b_bk"].reshape(1, D),
        "l0b_bv": I["L0_attn_b_bv"].reshape(1, D), "l0b_bo": I["L0_attn_b_bo"].reshape(1, D),
        "l1a_wq": c(I["L1_attn_a_wq"]), "l1a_wk": c(I["L1_attn_a_wk"]),
        "l1a_wv": c(I["L1_attn_a_wv"]),
        "l1a_lns": I["L1_attn_a_ln_scale"].reshape(1, D),
        "l1a_lnb": I["L1_attn_a_ln_bias"].reshape(1, D),
        "l1f_wT": c(I["L1_ffn_weight"].T),
        "l1f_gwT": c(I["L1_ffn_gate_w"].T),
        "l1f_uT": c(I["L1_ffn_lora_U"].transpose(2, 0, 1).reshape(D, 128)),
        "l1f_v": c(I["L1_ffn_lora_V"].transpose(0, 2, 1).reshape(128, D)),
        "mb_ipT": c(I["L1_attn_b_in_proj_w"].T),
        "mb_xpT": c(I["L1_attn_b_x_proj_w"].T),
        "mb_dtT": c(I["L1_attn_b_dt_proj_w"].T),
        "mb_opT": c(I["L1_attn_b_out_proj_w"].T),
        "mb_cx": c(I["L1_attn_b_conv_x_w"].reshape(D, 4)),
        "mb_cz": c(I["L1_attn_b_conv_z_w"].reshape(D, 4)),
        "mb_alog": c(I["L1_attn_b_A_log"]),
        "mb_dvec": I["L1_attn_b_D"].reshape(D, 1),
        "mb_dtb": I["L1_attn_b_dt_proj_b"].reshape(D, 1),
    }
    base = {k: np.ascontiguousarray(v, dtype=np.float32) for k, v in base.items()}
    x = np.ascontiguousarray(I["L0_attn_a_x"], dtype=np.float32)
    in_maps = []
    for core in range(NCORES):
        b, h = core // 2, core % 2
        m = dict(base)
        m["xin"] = np.ascontiguousarray(x[b, h * T:(h + 1) * T, :])
        m["tpos"] = np.arange(h * T, (h + 1) * T, dtype=np.float32).reshape(T, 1)
        m["modd"] = np.full((1, 1), float(h), dtype=np.float32)
        in_maps.append(m)
    return in_maps


_NC_CACHE = {}


def kernel(**inputs):
    in_maps = _prep_inputs(inputs)
    if "nc" not in _NC_CACHE:
        _NC_CACHE["nc"] = build_program(NCORES)
    nc = _NC_CACHE["nc"]
    res = run_bass_kernel_spmd(nc, in_maps, core_ids=list(range(NCORES)))
    out = np.empty((B, S, D), dtype=np.float32)
    for core in range(NCORES):
        b, h = core // 2, core % 2
        out[b, h * T:(h + 1) * T, :] = res.results[core]["xout"]
    return out
